# version 22
# baseline (speedup 1.0000x reference)
import numpy as np
import concourse.bass as bass
import concourse.mybir as mybir
from concourse.bass_utils import run_bass_kernel_spmd

F32 = mybir.dt.float32
F32R = mybir.dt.float32r
F16 = mybir.dt.float16
BF16 = mybir.dt.bfloat16
I32 = mybir.dt.int32
AF = mybir.ActivationFunctionType
ALU = mybir.AluOpType
AX = mybir.AxisListType

L = 4096
D = 1024
S = 16
T = 512
NCH = L // T
NJ = T // 128
NK = D // 128
NCORE = 8

ENGINES = ("sync", "vector", "scalar", "tensor", "gpsimd")
STREAMS = ("qxin", "qxres", "qout", "qcrow", "qbcw", "qw",
           "qilb_s", "qilb_g", "qilc_sc", "qilc_s2")
UNORDERED = {"qilb_s", "qilb_g", "qilc_sc", "qilc_s2"}


class _Tok:
    __slots__ = ("eng", "idx", "marked", "count_after")

    def __init__(self, eng, idx):
        self.eng = eng
        self.idx = idx
        self.marked = False
        self.count_after = None


class _Builder:

    def __init__(self):
        self.ops = {e: [] for e in ENGINES}
        self.stream_n = {q: 0 for q in STREAMS}

    def emit(self, eng, fn, waits=(), dma=None):
        tok = _Tok(eng, len(self.ops[eng]))
        wl = {}
        for w in waits:
            if w is None:
                continue
            k = w.eng
            if k not in wl or w.idx > wl[k].idx:
                wl[k] = w
        for w in wl.values():
            w.marked = True
        prewait = None
        if dma is not None:
            tok.eng = dma
            tok.idx = self.stream_n[dma]
            tok.marked = True
            tok.count_after = 16 * (self.stream_n[dma] + 1)
            if self.stream_n[dma] > 0 and dma not in UNORDERED:
                prewait = 16 * self.stream_n[dma]
            self.stream_n[dma] += 1
        self.ops[eng].append({"fn": fn, "waits": list(wl.values()), "dma": dma,
                              "tok": tok, "prewait": prewait})
        return tok

    def resolve(self):
        for e in ENGINES:
            c = 0
            for op in self.ops[e]:
                if op["dma"] is not None:
                    continue
                if op["tok"].marked:
                    c += 1
                op["tok"].count_after = c

    def run(self, eng_obj, name, sems):
        wm = {}

        def w(key, val):
            if wm.get(key, -1) >= val:
                return
            eng_obj.wait_ge(sems[key], val)
            wm[key] = val

        for op in self.ops[name]:
            if op["prewait"] is not None:
                w(op["dma"], op["prewait"])
            for wt in op["waits"]:
                w(wt.eng, wt.count_after)
            inst = op["fn"](eng_obj)
            if op["dma"] is not None:
                inst.then_inc(sems[op["dma"]], 16)
            elif op["tok"].marked:
                inst.then_inc(sems[op["tok"].eng], 1)


def build_program():
    nc = bass.Bass("TRN2", target_bir_lowering=False)
    X = nc.dram_tensor("x", [L, D], F32, kind="ExternalInput").ap()
    WDBC = nc.dram_tensor("w_dbc", [96, D], F32, kind="ExternalInput").ap()
    WDT = nc.dram_tensor("w_dt", [D, 64], F32, kind="ExternalInput").ap()
    BDT = nc.dram_tensor("b_dt", [D], F32, kind="ExternalInput").ap()
    ALOG = nc.dram_tensor("a_log", [D, S], F32, kind="ExternalInput").ap()
    DPAR = nc.dram_tensor("d_par", [D], F32, kind="ExternalInput").ap()
    WINT = nc.dram_tensor("w_int", [D, D], I32, kind="ExternalInput").ap()
    WSC = nc.dram_tensor("w_sc", [1], F32, kind="ExternalInput").ap()
    FBIAS = nc.dram_tensor("f_bias", [D], F32, kind="ExternalInput").ap()
    Y = nc.dram_tensor("y", [L, D], F32, kind="ExternalOutput").ap()
    BCSCR = nc.dram_tensor("bcscr", [32, T], BF16, kind="Internal").ap()

    from contextlib import ExitStack
    with ExitStack() as ctx:
        def sb(name, shape, dt):
            return ctx.enter_context(nc.sbuf_tensor(name, shape, dt))

        xin = sb("xin", [128, 4096], F32)
        xntm = sb("xntm", [128, 2048], F32)
        scr = sb("scr", [128, 2048], F32)
        spbuf = sb("spbuf", [128, 1024], F32)
        preg = sb("preg", [128, 1024], F32)
        sgbuf = sb("sgbuf", [128, 1024], F32)
        gbuf = sb("gbuf", [128, 1024], F32)
        sfm = sb("sfm", [128, 1536], F32)
        xnfm = sb("xnfm", [128, 4096], F32R)
        xqfm = sb("xqfm", [128, 4096], F16)
        delta = sb("delta", [128, 4096], BF16)
        ubuf = sb("ubuf", [128, 4096], BF16)
        drones = sb("drones", [65, T], F32R)
        btctbf = sb("btctbf", [32, T], BF16)
        bcil = sb("bcil", [128, 32 * T], BF16)
        apool = sb("apool", [128, 3 * T], F32)
        bpool = sb("bpool", [128, 4 * T], BF16)
        dxnbuf = sb("dxnbuf", [128, 2 * T], BF16)
        hall = sb("hall", [128, 16 * T], BF16)
        ppool = sb("ppool", [128, 2 * 1024], BF16)
        wffn = sb("wffn", [128, 8192], F16)
        wdbc = sb("wdbc", [128, 768], F32R)
        wdt = sb("wdt", [65, 1024], F32R)
        acol = sb("acol", [128, 128], F32)
        dcol = sb("dcol", [128, 8], F32)
        biascol = sb("biascol", [128, 8], F32)
        cscol = sb("cscol", [128, 8], F32)
        ccols = sb("ccols", [128, 4], F32)
        wsrep = sb("wsrep", [128, 1], F32)
        bnbuf = sb("bnbuf", [128, 48], F32)
        mv = sb("mv", [128, 8], F32)
        cells = sb("cells", [128, 32], F32)
        eps5 = sb("eps5", [128, 1], F32)
        eps6 = sb("eps6", [128, 1], F32)
        onecol = sb("onecol", [128, 1], F32)
        ident32 = sb("ident32", [128, 128], F32)
        ident16 = sb("ident16", [128, 128], F16)
        identbf = sb("identbf", [128, 128], BF16)
        onesrow = sb("onesrow", [1, 128], F32R)
        onescol16 = sb("onescol16", [128, 1], F16)
        crow = sb("crow", [1, T], F32R)
        crepsb = sb("crepsb", [128, T], F32)
        statebuf = sb("statebuf", [128, 256], F32)
        xresm = sb("xresm", [128, 1024], F32)
        outm = sb("outm", [128, 1024], F32)

        ps = [ctx.enter_context(nc.psum_tensor(f"ps{i}", [128, 512], F32))
              for i in range(8)]

        sems = {}
        for e in ENGINES + STREAMS:
            sems[e] = ctx.enter_context(nc.semaphore(f"sem_{e}"))
        block = ctx.enter_context(nc.Block())

        B = _Builder()
        E = B.emit

        def cell(j, i):
            return cells[:, j * 8 + i:j * 8 + i + 1]

        reg_i32 = xin[:, 0:1024].bitcast(I32)
        reg_f32 = xin[:, 1024:2048]
        reg_dbc = xin[0:96, 2048:3072]
        reg_wdt = xin[:, 3072:3584]
        reg_alog = xin[:, 3584:3712]

        t_gset = E("gpsimd", lambda g: g.memset(ident32[:], 1.0))
        t_gset = E("gpsimd", lambda g: g.affine_select(
            out=ident32[:], in_=ident32[:], pattern=[[1, 128]],
            compare_op=ALU.is_equal, fill=0.0, base=0, channel_multiplier=-1),
            waits=[t_gset])
        t_g16 = E("gpsimd", lambda g: g.memset(ident16[:], 1.0))
        t_g16 = E("gpsimd", lambda g: g.affine_select(
            out=ident16[:], in_=ident16[:], pattern=[[1, 128]],
            compare_op=ALU.is_equal, fill=0.0, base=0, channel_multiplier=-1),
            waits=[t_g16])
        t_gbf = E("gpsimd", lambda g: g.memset(identbf[:], 1.0))
        t_gbf = E("gpsimd", lambda g: g.affine_select(
            out=identbf[:], in_=identbf[:], pattern=[[1, 128]],
            compare_op=ALU.is_equal, fill=0.0, base=0, channel_multiplier=-1),
            waits=[t_gbf])
        t_eps5 = E("gpsimd", lambda g: g.memset(eps5[:], 1e-5))
        t_eps6 = E("gpsimd", lambda g: g.memset(eps6[:], 1e-6))
        t_one = E("gpsimd", lambda g: g.memset(onecol[:], 1.0))
        t_orow0 = E("gpsimd", lambda g: g.memset(scr[0:1, 0:640], 1.0))
        t_orow = E("vector", lambda v: v.tensor_copy(
            out=onesrow[:], in_=scr[0:1, 0:128]), waits=[t_orow0])
        t_oc16 = E("gpsimd", lambda g: g.memset(onescol16[:], 1.0))
        t_drow = E("vector", lambda v: v.tensor_copy(
            out=drones[64:65, :], in_=scr[0:1, 128:640]), waits=[t_orow0])
        t_state = E("gpsimd", lambda g: g.memset(statebuf[:], 0.0))

        t_dcol = E("sync", lambda s: s.dma_start(
            out=dcol[:], in_=DPAR.rearrange("(k p) -> p k", p=128),
            allow_slow_non_contiguous=True), dma="qw")
        t_bias = E("sync", lambda s: s.dma_start(
            out=biascol[:], in_=FBIAS.rearrange("(k p) -> p k", p=128),
            allow_slow_non_contiguous=True), dma="qw")
        t_bdt = E("sync", lambda s: s.dma_start(
            out=wdt[64:65, :], in_=BDT.rearrange("(o d) -> o d", o=1).bitcast(F32R)), dma="qw")
        t_aload = E("sync", lambda s: s.dma_start(
            out=reg_alog.rearrange("p (k s) -> p k s", k=NK),
            in_=ALOG.rearrange("(k p) s -> p k s", p=128)), dma="qw")
        t_dbcl = E("sync", lambda s: s.dma_start(out=reg_dbc, in_=WDBC), dma="qw")
        t_wdtl = E("sync", lambda s: s.dma_start(
            out=reg_wdt.rearrange("p (k r) -> p k r", k=NK),
            in_=WDT.rearrange("(k p) r -> p k r", p=128)), dma="qw")

        t_wsrep_dma = E("sync", lambda s: s.dma_start(
            out=wsrep[:], in_=WSC.rearrange("(o d) -> o d", o=1)
            .broadcast_to([128, 1])), dma="qw")
        t_wsrep = t_wsrep_dma

        t_ae = E("scalar", lambda s: s.activation(
            out=scr[:, 0:128], in_=reg_alog, func=AF.Exp),
            waits=[t_aload, t_orow])
        t_acol = E("vector", lambda v: v.tensor_scalar(
            out=acol[:], in0=scr[:, 0:128], scalar1=-1.0, scalar2=None,
            op0=ALU.mult), waits=[t_ae])

        t_dbc_ev = [None] * NK
        for k in range(NK):
            bank = ps[k % 2]
            war = t_dbc_ev[k - 2] if k >= 2 else None
            t_tr = E("tensor", lambda t, k=k, bank=bank: t.transpose(
                out=bank[:, 0:96], in_=reg_dbc[:, k * 128:(k + 1) * 128],
                identity=ident32[0:96, 0:96]),
                waits=[t_dbcl, t_gset, war])
            t_dbc_ev[k] = E("vector", lambda v, k=k, bank=bank: v.tensor_copy(
                out=wdbc[:, k * 96:(k + 1) * 96], in_=bank[:, 0:96]), waits=[t_tr])

        t_wdt_ev = [None] * NK
        for k in range(NK):
            bank = ps[2 + k % 2]
            war = t_wdt_ev[k - 2] if k >= 2 else None
            t_tr = E("tensor", lambda t, k=k, bank=bank: t.transpose(
                out=bank[0:64, 0:128], in_=reg_wdt[:, k * 64:(k + 1) * 64],
                identity=ident32[:]),
                waits=[t_wdtl, t_gset, war])
            t_wdt_ev[k] = E("vector", lambda v, k=k, bank=bank: v.tensor_copy(
                out=wdt[0:64, k * 128:(k + 1) * 128], in_=bank[0:64, 0:128]),
                waits=[t_tr])

        t_wload = None
        t_wcopy = None
        t_wev_last = None
        t_wtr_last = [None, None]
        for m in range(NK):
            t_wload = E("sync", lambda s, m=m: s.dma_start(
                out=reg_i32, in_=WINT[m * 128:(m + 1) * 128, :]),
                dma="qw", waits=[t_wcopy])
            t_wcopy = E("vector", lambda v: v.tensor_copy(out=reg_f32, in_=reg_i32),
                        waits=[t_wload, t_wtr_last[1] or t_wtr_last[0]])
            for half in range(2):
                bank = ps[4 + half]
                trs = []
                for kk in range(4):
                    k = half * 4 + kk
                    trs.append(E("tensor", lambda t, k=k, kk=kk, bank=bank: t.transpose(
                        out=bank[:, kk * 128:(kk + 1) * 128],
                        in_=reg_f32[:, k * 128:(k + 1) * 128],
                        identity=ident32[:]),
                        waits=[t_wcopy, t_wev_last, t_gset]))
                t_wtr_last[half] = trs[-1]
                dst = wffn[:].rearrange("p (k o) -> p k o", k=NK)[
                    :, half * 4:(half + 1) * 4, m * 128:(m + 1) * 128]
                t_wev_last = E("vector", lambda v, bank=bank, dst=dst: v.tensor_copy(
                    out=dst, in_=bank[:].rearrange("p (a b) -> p a b", a=4)),
                    waits=[trs[-1]])

        t_cs = [None] * NK
        for m in range(NK):
            mm_last = None
            for k in range(NK):
                mm_last = E("tensor", lambda t, m=m, k=k: t.matmul(
                    out=ps[6][:, 0:1],
                    lhsT=wffn[:, k * 1024 + m * 128:k * 1024 + (m + 1) * 128],
                    rhs=onescol16[:], start=(k == 0), stop=(k == NK - 1)),
                    waits=[t_wev_last, t_oc16, t_cs[m - 1] if m else None])
            t_cs[m] = E("vector", lambda v, m=m: v.tensor_scalar(
                out=cscol[:, m:m + 1], in0=ps[6][:, 0:1], scalar1=-1536.0,
                scalar2=None, op0=ALU.mult), waits=[mm_last])

        t_setup_w = t_wev_last
        t_setup_x = t_cs[NK - 1]
        t_setup_pe = t_wtr_last[1]
        t_scr_set = t_acol
        t_evac_bank_init = [t_dbc_ev[NK - 2], t_dbc_ev[NK - 1]]

        t_lnts = [[None] * NJ for _ in range(NCH + 1)]
        t_xin_dma = [None] * NJ
        t_xn_ev = None
        t_xq_ev = None
        t_state_gather = None
        t_sfm_roll = [t_wsrep, t_wsrep, t_wsrep]
        t_sg_slot = [None, None, None]
        t_preg_roll = [t_wev_last, t_cs[NK - 1]]
        t_silu_roll = [None, None]
        t_sgw_roll = [None, None]
        t_gbuf_roll = [None, None]
        ps34_reader = [t_wev_last, t_wev_last]
        t_sp_roll = [None, None]
        t_xqslot = [None] * NJ
        t_scrslot = [None, None]
        t_evac_bank_roll = list(t_evac_bank_init)
        t_crow_prev = [None] * NJ
        t_quant_j = [None] * NJ
        t_ln_tok = [None] * NJ
        t_cc_next = None
        t_dxn = [None] * NK
        t_yred_of = {}
        R = {"dr": None, "bt": None, "bcw": None, "crepsb": None,
             "dmm_last": None, "ffn_last": None, "tcc": None,
             "il_last": [], "ilB": [], "ilC": [],
             "tb_last": None, "tym_last_vec": None, "tym_last_g": None,
             "bcmm": None, "pend_ev": None, "pre0": None}
        t_outblk_m = [None] * (NK + 2)
        t_outdma_m = [None] * (NK + 2)
        t_xres_m = [None] * (NK + 2)
        t_scan_hist = {}
        t_scan_all_prev = None
        t_ymult_last_prev_chunk = None
        t_dbc_done_prev = None
        t_crepsb = None
        t_yp_reader_prev = t_wsrep

        for c in range(NCH):
            par = c % 2
            cur0 = par * 128
            nxt0 = (1 - par) * 128

            def ph1a(cc):
                for j in range(NJ):
                    w = ([t_lnts[cc - 1][j], t_quant_j[j]] if cc > 0 else
                         [t_setup_pe, t_setup_x, t_scr_set, t_ae])
                    t_xin_dma[j] = E("sync", lambda s, cc=cc, j=j: s.dma_start(
                        out=xin[:, j * 1024:(j + 1) * 1024],
                        in_=X[cc * T + j * 128:cc * T + (j + 1) * 128, :]),
                        dma="qxin", waits=w)
                tag = None
                for j in range(NJ):
                    E("vector", lambda v, j=j: v.bn_stats(
                        out=bnbuf[:, j * 12:j * 12 + 6],
                        in_=xin[:, j * 1024:j * 1024 + 512]), waits=[t_xin_dma[j]])
                    tb2 = E("vector", lambda v, j=j: v.bn_stats(
                        out=bnbuf[:, j * 12 + 6:j * 12 + 12],
                        in_=xin[:, j * 1024 + 512:j * 1024 + 1024]),
                        waits=[t_xin_dma[j]])
                    tag = E("vector", lambda v, j=j: v.bn_aggr(
                        out=mv[:, j * 2:j * 2 + 2],
                        in_=bnbuf[:, j * 12:j * 12 + 12]), waits=[tb2])
                mv3 = mv[:].rearrange("p (j two) -> p j two", two=2)
                tcc_l = [None, None]
                for pair in range(2):
                    j0 = pair * 2
                    vars2 = mv3[:, j0:j0 + 2, 1:2].squeeze(2)
                    means2 = mv3[:, j0:j0 + 2, 0:1].squeeze(2)
                    cl = lambda a, b, pair=pair: cells[:, pair * 16 + a:
                                                       pair * 16 + b]
                    tl2 = E("scalar", lambda s, vars2=vars2, cl=cl: s.activation(
                        out=cl(0, 2), in_=vars2, func=AF.Ln, bias=eps5[:, 0:1]),
                        waits=[tag, t_eps5])
                    tr2_ = E("scalar", lambda s, cl=cl: s.activation(
                        out=cl(2, 4), in_=cl(0, 2), func=AF.Exp, scale=-0.5),
                        waits=[tl2])
                    tn2 = E("vector", lambda v, means2=means2, cl=cl:
                            v.scalar_tensor_tensor(
                                out=cl(4, 6), in0=means2, scalar=-1.0,
                                in1=cl(2, 4), op0=ALU.mult, op1=ALU.mult),
                            waits=[tr2_])
                    t_am_l = [None, None]
                    for jj in range(2):
                        j = j0 + jj
                        jr = j % 2
                        t_ln = E("vector", lambda v, j=j, jj=jj, cl=cl:
                                 v.tensor_scalar(
                                     out=xin[:, j * 1024:(j + 1) * 1024],
                                     in0=xin[:, j * 1024:(j + 1) * 1024],
                                     scalar1=cl(2 + jj, 3 + jj),
                                     scalar2=cl(4 + jj, 5 + jj),
                                     op0=ALU.mult, op1=ALU.add),
                                 waits=[tn2])
                        t_ln_tok[j] = t_ln
                        t_am_l[jj] = E("vector", lambda v, j=j, jj=jj, cl=cl:
                                       v.reduce_max(
                                           out=cl(6 + jj, 7 + jj),
                                           in_=xin[:, j * 1024:(j + 1) * 1024],
                                           axis=AX.X,
                                           apply_absolute_value=True),
                                       waits=[t_ln])
                    tr22 = E("vector", lambda v, cl=cl: v.tensor_mul(
                        out=cl(8, 10), in0=cl(2, 4), in1=cl(2, 4)), waits=[tn2])
                    tms = E("vector", lambda v, vars2=vars2, cl=cl:
                            v.tensor_tensor(
                                out=cl(0, 2), in0=vars2, in1=cl(8, 10),
                                op=ALU.mult), waits=[tr22, tl2])
                    tlm = E("scalar", lambda s, cl=cl: s.activation(
                        out=cl(0, 2), in_=cl(0, 2), func=AF.Ln,
                        bias=eps6[:, 0:1]), waits=[tms, t_eps6])
                    trq = E("scalar", lambda s, cl=cl: s.activation(
                        out=cl(10, 12), in_=cl(0, 2), func=AF.Exp, scale=-0.5),
                        waits=[tlm])
                    tamn = E("vector", lambda v, cl=cl: v.tensor_mul(
                        out=cl(12, 14), in0=cl(6, 8), in1=cl(10, 12)),
                        waits=[trq, t_am_l[1]])
                    tamc = E("vector", lambda v, cl=cl: v.tensor_scalar(
                        out=cl(12, 14), in0=cl(12, 14), scalar1=1e-5,
                        scalar2=None, op0=ALU.max), waits=[tamn])
                    trec = E("vector", lambda v, cl=cl: v.reciprocal(
                        out=cl(0, 2), in_=cl(12, 14)), waits=[tamc])
                    tqc2 = E("vector", lambda v, cl=cl: v.scalar_tensor_tensor(
                        out=cl(14, 16), in0=cl(0, 2), scalar=127.0,
                        in1=cl(10, 12), op0=ALU.mult, op1=ALU.mult),
                        waits=[trec])
                    tcc_l[pair] = E("vector", lambda v, j0=j0, cl=cl:
                                    v.scalar_tensor_tensor(
                                        out=ccols[:, j0:j0 + 2], in0=cl(12, 14),
                                        scalar=1.0 / 127.0,
                                        in1=wsrep[:, 0:1].broadcast_to([128, 2]),
                                        op0=ALU.mult, op1=ALU.mult),
                                    waits=[tqc2, t_wsrep, t_crow_prev[j0],
                                           t_crow_prev[j0 + 1]])
                    for jj in range(2):
                        j = j0 + jj
                        jr = j % 2
                        tq1 = E("vector", lambda v, j=j, jr=jr, jj=jj, cl=cl:
                                v.tensor_scalar(
                                    out=scr[:, jr * 1024:(jr + 1) * 1024],
                                    in0=xin[:, j * 1024:(j + 1) * 1024],
                                    scalar1=cl(14 + jj, 15 + jj),
                                    scalar2=1536.0,
                                    op0=ALU.mult, op1=ALU.add),
                                waits=[tqc2])
                        t_quant_j[j] = E("vector", lambda v, j=j, jr=jr:
                                         v.tensor_scalar(
                                             out=xntm[:].bitcast(F16)[
                                                 :, j * 1024:(j + 1) * 1024],
                                             in0=scr[:, jr * 1024:
                                                     (jr + 1) * 1024],
                                             scalar1=1408.0, scalar2=1663.0,
                                             op0=ALU.max, op1=ALU.min),
                                        waits=[tq1, t_xqslot[j]])
                return tcc_l

            def ph1b_xn(cc, extra_w):
                nonlocal t_xn_ev
                for j in range(NJ):
                    jr = j % 2
                    t_ln = t_ln_tok[j]
                    for half in range(2):
                        bank = ps[half]
                        trs_last = None
                        for kk in range(4):
                            k = half * 4 + kk
                            trs_last = E("tensor", lambda t, j=j, k=k, kk=kk,
                                         bank=bank: t.transpose(
                                             out=bank[:, kk * 128:
                                                      (kk + 1) * 128],
                                             in_=xin[:, j * 1024 + k * 128:
                                                     j * 1024 + (k + 1) * 128],
                                             identity=ident32[:]),
                                         waits=[t_ln, t_evac_bank_roll[half],
                                                t_gset])
                        dst = xnfm[:].rearrange("p (k t2) -> p k t2", k=NK)[
                            :, half * 4:(half + 1) * 4, j * 128:(j + 1) * 128]
                        t_evac_bank_roll[half] = E(
                            "scalar", lambda sc, bank=bank, dst=dst: sc.copy(
                                out=dst,
                                in_=bank[:].rearrange("p (a b) -> p a b", a=4)),
                            waits=[trs_last] + extra_w)
                        t_xn_ev = t_evac_bank_roll[half]
                        t_lnts[cc][j] = trs_last

            def ph1b_xq(cc, extra_w):
                nonlocal t_xq_ev
                for j in range(NJ):
                    tq16 = t_quant_j[j]
                    xq_trs = None
                    for half in range(2):
                        bank = ps[half]
                        trs_last = None
                        for kk in range(4):
                            k = half * 4 + kk
                            src = xntm[:].bitcast(F16)[
                                :, j * 1024 + k * 128:j * 1024 + (k + 1) * 128]
                            trs_last = E("tensor", lambda t, src=src, kk=kk,
                                         bank=bank: t.transpose(
                                             out=bank[:].bitcast(F16)[
                                                 :, kk * 128:(kk + 1) * 128],
                                             in_=src, identity=ident16[:]),
                                         waits=[tq16, t_evac_bank_roll[half],
                                                t_g16])
                        xq_trs = trs_last
                        dst = xqfm[:].rearrange("p (k t2) -> p k t2", k=NK)[
                            :, half * 4:(half + 1) * 4, j * 128:(j + 1) * 128]
                        t_evac_bank_roll[half] = E(
                            "scalar", lambda sc, bank=bank, dst=dst: sc.copy(
                                out=dst,
                                in_=bank[:].bitcast(F16)[:, 0:512].rearrange(
                                    "p (a b) -> p a b", a=4)),
                            waits=[trs_last] + extra_w)
                        t_xq_ev = t_evac_bank_roll[half]
                    t_xqslot[j] = xq_trs

            def ph2a1_mm(cc):
                mm_last = None
                for k in range(NK):
                    w = [t_xn_ev, t_dbc_ev[NK - 1]]
                    if k == 0:
                        w += [R["crepsb"] if cc > 0 else t_wdt_ev[NK - 2]]
                    mm_last = E("tensor", lambda t, k=k: t.matmul(
                        out=ps[2][0:96, :], lhsT=wdbc[:, k * 96:(k + 1) * 96],
                        rhs=xnfm[:, k * T:(k + 1) * T],
                        start=(k == 0), stop=(k == NK - 1)), waits=w)
                R["bcmm"] = mm_last

            def ph2a1_ev(cc):
                mm_last = R["bcmm"]
                R["dr"] = E("vector", lambda v: v.tensor_copy(
                    out=drones[0:64, :], in_=ps[2][0:64, :]),
                    waits=[mm_last, t_drow, R["dmm_last"]])
                R["bt"] = E("vector", lambda v: v.tensor_copy(
                    out=btctbf[0:32, :], in_=ps[2][64:96, :]),
                    waits=[mm_last, R["bcw"]])
                R["bcw"] = E("sync", lambda sy: sy.dma_start(
                    out=BCSCR, in_=btctbf[0:32, :]),
                    dma="qbcw", waits=[R["bt"]] + R["il_last"])

            def ph2a1_ilB(cc):
                wB = [R["bcw"]] + ([R["tb_last"]] if cc > 0 else [])
                ilB = [None, None]
                for i in range(8):
                    for half in range(2):
                        r = 2 * i + half
                        qe, qs = (("sync", "qilb_s") if half == 0 else
                                  ("gpsimd", "qilb_g"))
                        ilB[half] = E(qe, lambda q, r=r: q.dma_start(
                            out=bcil[:, r * T:(r + 1) * T],
                            in_=BCSCR[r:r + 1, :].broadcast_to([128, T])),
                            dma=qs, waits=wB)
                R["ilB"] = ilB

            def ph2a1_ilC(cc):
                wC = [R["bcw"]] + ([R["tym_last_vec"], R["tym_last_g"]]
                                   if cc > 0 else [])
                ilC = [None, None]
                for i in range(8):
                    for half in range(2):
                        rc = 16 + 2 * i + half
                        qe, qs = (("scalar", "qilc_sc") if half == 0 else
                                  ("sync", "qilc_s2"))
                        ilC[half] = E(qe, lambda q, rc=rc: q.dma_start(
                            out=bcil[:, rc * T:(rc + 1) * T],
                            in_=BCSCR[rc:rc + 1, :].broadcast_to([128, T])),
                            dma=qs, waits=wC)
                R["ilC"] = ilC
                R["il_last"] = R["ilB"] + ilC

            def ph2a2(cc, tcc_pair):
                t_crow = [None] * NJ
                for j in range(NJ):
                    t_crow[j] = E("sync", lambda s, j=j: s.dma_start(
                        out=crow[0:1, j * 128:(j + 1) * 128],
                        in_=ccols[:, j:j + 1].bitcast(F32R),
                        allow_slow_non_contiguous=True),
                        dma="qcrow",
                        waits=[tcc_pair[0] if j < 2 else tcc_pair[1]])
                    t_crow_prev[j] = t_crow[j]
                t_crep = E("tensor", lambda t: t.matmul(
                    out=ps[2][:], lhsT=onesrow[0:1, :], rhs=crow[0:1, :],
                    start=True, stop=True),
                    waits=[t_crow[NJ - 1], R["bt"], R["dr"], t_orow])
                R["crepsb"] = E("vector", lambda v: v.tensor_copy(
                    out=crepsb[:], in_=ps[2][:]),
                    waits=[t_crep, t_preg_roll[0], t_preg_roll[1]])

            if c == 0:
                tcc_pair0 = ph1a(0)
                ph1b_xn(0, [])
                ph1b_xq(0, [])
                ph2a1_mm(0)
                ph2a1_ev(0)
                ph2a1_ilB(0)
                ph2a1_ilC(0)
                R["tcc"] = tcc_pair0
                ph2a2(0, R["tcc"])
            t_crepsb = R["crepsb"]
            t_dr = R["dr"]

            t_deltaSTT = [None] * NK
            t_u = [None] * NK
            t_state_k = [None] * NK
            t_sfm_tok = [None] * NK
            t_tpg_m = [None] * NK
            t_ymult_g7 = [None] * NK
            scan_last_of_k = [None] * NK
            scan_toks = [None] * NK
            pend_sfm = [None]

            def ph2b(k):
                bank = ps[3 + k % 2]
                t_dmm = E("tensor", lambda t, k=k, bank=bank: t.matmul(
                    out=bank[:], lhsT=wdt[0:65, k * 128:(k + 1) * 128],
                    rhs=drones[0:65, :], start=True, stop=True),
                    waits=[R["dr"], t_wdt_ev[NK - 1], t_bdt,
                           ps34_reader[k % 2]])
                R["dmm_last"] = t_dmm
                sp0 = spbuf[:, 0:512]
                sp1 = spbuf[:, 512:1024]
                ta1 = E("scalar", lambda s, bank=bank: s.activation(
                    out=sp0, in_=bank[:], func=AF.Abs),
                    waits=[t_dmm, t_sp_roll[0]])
                ta2 = E("scalar", lambda s: s.activation(
                    out=sp1, in_=sp0, func=AF.Exp, scale=-1.0),
                    waits=[ta1, t_sp_roll[1]])
                ta3 = E("scalar", lambda s: s.activation(
                    out=sp0, in_=sp1, func=AF.Ln, bias=onecol[:, 0:1]),
                    waits=[ta2, t_one])
                t_sp_roll[1] = ta3
                t_deltaSTT[k] = E("vector", lambda v, k=k, bank=bank:
                                  v.scalar_tensor_tensor(
                                      out=delta[:, k * T:(k + 1) * T],
                                      in0=bank[:], scalar=0.0, in1=sp0,
                                      op0=ALU.max, op1=ALU.add), waits=[ta3])
                t_sp_roll[0] = t_deltaSTT[k]
                ps34_reader[k % 2] = t_deltaSTT[k]
                t_u[k] = E("gpsimd", lambda g, k=k: g.tensor_tensor(
                    out=ubuf[:, k * T:(k + 1) * T],
                    in0=delta[:, k * T:(k + 1) * T],
                    in1=xnfm[:, k * T:(k + 1) * T].bitcast(F32),
                    op=ALU.mult), waits=[t_deltaSTT[k]])

            def ph2b_dxn(k):
                t_dxn[k] = E("vector", lambda v, k=k: v.tensor_scalar(
                    out=dxnbuf[:, (k % 2) * T:(k % 2 + 1) * T],
                    in0=xnfm[:, k * T:(k + 1) * T].bitcast(F32),
                    scalar1=dcol[:, k:k + 1], scalar2=None, op0=ALU.mult),
                    waits=[t_xn_ev, t_dcol,
                           t_yred_of.get((k - 2) % NK)])

            def emit_exp(k, s_):
                i = k * S + s_
                ia = i % 3
                wa = [t_deltaSTT[k], t_acol]
                prev_a_user = t_scan_hist.get(("a", ia))
                if prev_a_user is not None:
                    wa.append(prev_a_user)
                return E("scalar", lambda s, k=k, s_=s_, ia=ia:
                         s.activation(
                             out=apool[:, ia * T:(ia + 1) * T],
                             in_=delta[:, k * T:(k + 1) * T],
                             func=AF.Exp,
                             scale=acol[:, k * S + s_:k * S + s_ + 1]),
                         waits=wa)

            def emit_scans(k, a_toks):
                nonlocal t_scan_all_prev
                scan_toks[k] = [None] * S
                tb = None
                for s_ in range(S):
                    a_toks.append(emit_exp(k, s_))
                    i = k * S + s_
                    ia = i % 3
                    if s_ % 2 == 0:
                        ibp = (i // 2) % 2
                        wb = [t_u[k]] + R["ilB"]
                        prev_b_user = t_scan_hist.get(("b", ibp))
                        if prev_b_user is not None:
                            wb.append(prev_b_user)
                        usrc = ubuf[:, k * T:(k + 1) * T].unsqueeze(1)\
                            .broadcast_to([128, 2, T])
                        tb = E("vector", lambda v, k=k, s_=s_, ibp=ibp,
                               usrc=usrc: v.tensor_tensor(
                                   out=bpool[:, ibp * 2 * T:(ibp + 1) * 2 * T]
                                   .rearrange("p (a b) -> p a b", a=2),
                                   in0=usrc,
                                   in1=bcil[:, s_ * T:(s_ + 2) * T]
                                   .rearrange("p (a b) -> p a b", a=2),
                                   op=ALU.mult), waits=wb)
                        R["tb_last"] = tb
                    ibp = (i // 2) % 2
                    boff = ibp * 2 * T + (s_ % 2) * T
                    ws = [a_toks[s_], tb, t_state]
                    if s_ == 0:
                        ws.append(t_state_k[k - 1] if k > 0 else t_state_gather)
                    off = cur0 + k * S + s_
                    tsc = E("vector", lambda v, s_=s_, ia=ia, boff=boff,
                            off=off: v.tensor_tensor_scan(
                                out=hall[:, s_ * T:(s_ + 1) * T],
                                data0=apool[:, ia * T:(ia + 1) * T],
                                data1=bpool[:, boff:boff + T],
                                initial=statebuf[:, off:off + 1],
                                op0=ALU.mult, op1=ALU.add), waits=ws)
                    t_scan_hist[("a", ia)] = tsc
                    if s_ % 2 == 1:
                        t_scan_hist[("b", ibp)] = tsc
                    scan_toks[k][s_] = tsc
                    if s_ == S - 1:
                        scan_last_of_k[k] = tsc
                        t_scan_all_prev = tsc
                    if s_ == 1 and k + 1 < NK:
                        if not (c > 0 and k <= 1):
                            ph2b(k + 1)
                        ph2b_dxn(k + 1)
                    if s_ == 2:
                        emit_sfm()
                    if s_ == 10 and R["pend_ev"] is not None:
                        cc2 = R["pend_ev"]
                        R["pend_ev"] = None
                        ph2a1_ev(cc2)

            def emit_ytail(k):
                nonlocal t_yp_reader_prev
                t_state_k[k] = E("gpsimd", lambda g, k=k: g.tensor_copy(
                    out=statebuf[:, nxt0 + k * S:nxt0 + (k + 1) * S],
                    in_=hall[:].rearrange("p (s t2) -> p s t2", s=S)
                    [:, :, T - 1:T].squeeze(2)), waits=[scan_last_of_k[k]])
                t_yred = None
                for g in range(8):
                    ip = g % 2
                    eng_y = "gpsimd" if g == 7 else "vector"
                    wy = [scan_toks[k][2 * g + 1]] + R["ilC"]
                    prev_p_user = t_scan_hist.get(("p", ip))
                    if prev_p_user is not None:
                        wy.append(prev_p_user)
                    tym = E(eng_y, lambda e, g=g, ip=ip: e.tensor_tensor(
                        out=ppool[:, ip * 1024:(ip + 1) * 1024],
                        in0=hall[:, g * 1024:(g + 1) * 1024],
                        in1=bcil[:, (16 + 2 * g) * T:(16 + 2 * g + 2) * T],
                        op=ALU.mult), waits=wy)
                    if g == 7:
                        t_ymult_g7[k] = tym
                        R["tym_last_g"] = tym
                    else:
                        R["tym_last_vec"] = tym
                    tok = None
                    for half in range(2):
                        w = [tym, t_gbf]
                        if g == 0 and half == 0:
                            w.append(t_yp_reader_prev)
                        tok = E("tensor", lambda t, ip=ip, half=half, g=g:
                                t.matmul(
                                    out=ps[7][:],
                                    lhsT=identbf[:],
                                    rhs=ppool[:, ip * 1024 + half * 512:
                                              ip * 1024 + (half + 1) * 512],
                                    start=(g == 0 and half == 0),
                                    stop=False), waits=w)
                    t_scan_hist[("p", ip)] = tok
                    t_yred = tok
                t_yred = E("tensor", lambda t, k=k: t.matmul(
                    out=ps[7][:], lhsT=identbf[:],
                    rhs=dxnbuf[:, (k % 2) * T:(k % 2 + 1) * T],
                    start=False, stop=True), waits=[t_dxn[k], t_yred])
                t_yred_of[k] = t_yred
                pend_sfm[0] = k

            def emit_sfm():
                nonlocal t_yp_reader_prev
                k = pend_sfm[0]
                if k is None:
                    return
                pend_sfm[0] = None
                sl = k % 3
                t_sfm_tok[k] = E("vector", lambda v, k=k, sl=sl: v.tensor_copy(
                    out=sfm[:, sl * T:(sl + 1) * T], in_=ps[7][:]),
                    waits=[t_yred_of[k], t_sg_slot[sl]])
                t_yp_reader_prev = t_sfm_tok[k]

            def ph4a(m):
                rot = m % 2
                bank = ps[5 + rot]
                xre = X[c * T:(c + 1) * T, m * 128:(m + 1) * 128].rearrange(
                    "(j p) w -> p j w", p=128)
                t_xres_m[m] = E("sync", lambda s, rot=rot, xre=xre: s.dma_start(
                    out=xresm[:, rot * 512:(rot + 1) * 512].rearrange(
                        "p (j w) -> p j w", j=NJ), in_=xre),
                    dma="qxres", waits=[t_outblk_m[m - 2]])
                mm = None
                for k2 in range(NK):
                    w = [t_xq_ev, t_setup_w]
                    if k2 == 0:
                        w += [t_preg_roll[rot]]
                    mm = E("tensor", lambda t, m=m, k2=k2, bank=bank: t.matmul(
                        out=bank[:],
                        lhsT=wffn[:, k2 * 1024 + m * 128:k2 * 1024 + (m + 1) * 128],
                        rhs=xqfm[:, k2 * T:(k2 + 1) * T],
                        start=(k2 == 0), stop=(k2 == NK - 1)), waits=w)
                R["ffn_last"] = mm
                t_tpg_m[m] = E("vector", lambda v, m=m, rot=rot, bank=bank:
                               v.scalar_tensor_tensor(
                                   out=preg[:, rot * 512:(rot + 1) * 512],
                                   in0=bank[:], scalar=cscol[:, m:m + 1],
                                   in1=crepsb[:], op0=ALU.add, op1=ALU.mult),
                               waits=[mm, t_crepsb, t_cs[NK - 1],
                                      t_silu_roll[rot]])
                t_preg_roll[rot] = t_tpg_m[m]

            def ph4c(m):
                rot = m % 2
                tsl = E("scalar", lambda s, m=m, rot=rot: s.activation(
                    out=gbuf[:, rot * 512:(rot + 1) * 512],
                    in_=preg[:, rot * 512:(rot + 1) * 512], func=AF.Silu,
                    bias=biascol[:, m:m + 1]),
                    waits=[t_tpg_m[m], t_bias, t_gbuf_roll[rot]])
                t_silu_roll[rot] = tsl
                tsg = E("gpsimd", lambda g, m=m, rot=rot: g.tensor_tensor(
                    out=sgbuf[:, rot * 512:(rot + 1) * 512],
                    in0=sfm[:, (m % 3) * T:(m % 3 + 1) * T],
                    in1=gbuf[:, rot * 512:(rot + 1) * 512], op=ALU.mult),
                    waits=[tsl, t_sfm_tok[m], t_sgw_roll[rot]])
                t_gbuf_roll[rot] = tsg
                t_sg_slot[m % 3] = tsg
                last_blk = None
                for j in range(NJ):
                    tbank = ps[3 + (m * NJ + j) % 2]
                    ttr = E("tensor", lambda t, rot=rot, j=j, tbank=tbank:
                            t.transpose(
                                out=tbank[:, 0:128],
                                in_=sgbuf[:, rot * 512 + j * 128:
                                          rot * 512 + (j + 1) * 128],
                                identity=ident32[:]),
                            waits=[tsg, ps34_reader[(m * NJ + j) % 2]])
                    last_blk = E("vector", lambda v, rot=rot, j=j, tbank=tbank:
                                 v.tensor_tensor(
                                     out=outm[:, rot * 512 + j * 128:
                                              rot * 512 + (j + 1) * 128],
                                     in0=xresm[:, rot * 512 + j * 128:
                                               rot * 512 + (j + 1) * 128],
                                     in1=tbank[:, 0:128], op=ALU.add),
                                 waits=[ttr, t_xres_m[m],
                                        t_outdma_m[m - 2] if j == 0 else None])
                    ps34_reader[(m * NJ + j) % 2] = last_blk
                t_sgw_roll[rot] = last_blk
                t_outblk_m[m] = last_blk
                yap = Y[c * T:(c + 1) * T, m * 128:(m + 1) * 128].rearrange(
                    "(j p) w -> p j w", p=128)
                t_outdma_m[m] = E("sync", lambda s, rot=rot, yap=yap:
                                  s.dma_start(
                                      out=yap,
                                      in_=outm[:, rot * 512:(rot + 1) * 512]
                                      .rearrange("p (j w) -> p j w", j=NJ)),
                                  dma="qout", waits=[last_blk])

            if c == 0:
                ph2b(0)
            else:
                t_deltaSTT[0], t_u[0] = R["pre0"]
                t_deltaSTT[1], t_u[1] = R["pre1"]
                t_deltaSTT[2], t_u[2] = R["pre2"]
            ph2b_dxn(0)
            for k in range(NK):
                if k >= 3 and k % 2 == 1:
                    ph4c(k - 3)
                    ph4c(k - 2)
                a_toks = []
                emit_scans(k, a_toks)
                emit_ytail(k)
                if c + 1 < NCH:
                    if k == 4:
                        R["tcc"] = ph1a(c + 1)
                    elif k == 6:
                        ph1b_xn(c + 1, [t_u[NK - 1], t_dxn[NK - 1]])
                        ph2a1_mm(c + 1)
                        R["pend_ev"] = c + 1
                    elif k == 7:
                        ph2a1_ilB(c + 1)
                        ph2b(0)
                        R["pre0"] = (t_deltaSTT[0], t_u[0])
                if k >= 1:
                    ph4a(k - 1)
            emit_sfm()
            ph4a(NK - 1)
            ph4c(NK - 2)
            ph4c(NK - 1)
            if c + 1 < NCH:
                ph2a2(c + 1, R["tcc"])
                ph2a1_ilC(c + 1)
                ph2b(1)
                ph2b(2)
                R["pre1"] = (t_deltaSTT[1], t_u[1])
                R["pre2"] = (t_deltaSTT[2], t_u[2])
                ph1b_xq(c + 1, [R["ffn_last"]])
            t_state_gather = t_state_k[NK - 1]
            t_ymult_last_prev_chunk = t_ymult_g7[NK - 1]
            t_outblk_m[-2] = t_outblk_m[NK - 2]
            t_outblk_m[-1] = t_outblk_m[NK - 1]
            t_outdma_m[-2] = t_outdma_m[NK - 2]
            t_outdma_m[-1] = t_outdma_m[NK - 1]

        E("sync", lambda s: s.nop(),
          waits=[t_outdma_m[NK - 2], t_outdma_m[NK - 1]])

        B.resolve()
        for name, meth in (("sync", block.sync), ("vector", block.vector),
                           ("scalar", block.scalar), ("tensor", block.tensor),
                           ("gpsimd", block.gpsimd)):
            meth(lambda eng, name=name: B.run(eng, name, sems))

    return nc


_NC_CACHE = None


def kernel(L0_attn_x, L0_attn_W_deltaBC, L0_attn_W_dt, L0_attn_b_dt,
           L0_attn_A_log, L0_attn_D_param, L0_ffn_w_int8, L0_ffn_weight_scale,
           L0_ffn_bias):
    global _NC_CACHE
    if _NC_CACHE is None:
        _NC_CACHE = build_program()
    nc = _NC_CACHE

    x = np.ascontiguousarray(np.asarray(L0_attn_x, dtype=np.float32))
    base = {
        "w_dbc": np.ascontiguousarray(np.asarray(L0_attn_W_deltaBC, np.float32)),
        "w_dt": np.ascontiguousarray(np.asarray(L0_attn_W_dt, np.float32)),
        "b_dt": np.ascontiguousarray(np.asarray(L0_attn_b_dt, np.float32)),
        "a_log": np.ascontiguousarray(np.asarray(L0_attn_A_log, np.float32)),
        "d_par": np.ascontiguousarray(np.asarray(L0_attn_D_param, np.float32)),
        "w_int": np.ascontiguousarray(np.asarray(L0_ffn_w_int8, np.int32)),
        "w_sc": np.asarray(L0_ffn_weight_scale, np.float32).reshape(1),
        "f_bias": np.ascontiguousarray(np.asarray(L0_ffn_bias, np.float32)),
    }
    in_maps = []
    for b in range(NCORE):
        m = dict(base)
        m["x"] = np.ascontiguousarray(x[b])
        in_maps.append(m)
    res = run_bass_kernel_spmd(nc, in_maps, core_ids=list(range(NCORE)))
    out = np.stack([np.asarray(r["y"], np.float32) for r in res.results], axis=0)
    return out


# version 25
# speedup vs baseline: 1.0200x; 1.0200x over previous
import numpy as np
import concourse.bass as bass
import concourse.mybir as mybir
from concourse.bass_utils import run_bass_kernel_spmd

F32 = mybir.dt.float32
F32R = mybir.dt.float32r
F16 = mybir.dt.float16
BF16 = mybir.dt.bfloat16
I32 = mybir.dt.int32
AF = mybir.ActivationFunctionType
ALU = mybir.AluOpType
AX = mybir.AxisListType

L = 4096
D = 1024
S = 16
T = 512
NCH = L // T
NJ = T // 128
NK = D // 128
NCORE = 8

ENGINES = ("sync", "vector", "scalar", "tensor", "gpsimd")
STREAMS = ("qxin", "qxres", "qout", "qcrow", "qbcw", "qw",
           "qilb_s", "qilb_g", "qilc_sc", "qilc_s2")
UNORDERED = {"qilb_s", "qilb_g", "qilc_sc", "qilc_s2"}


class _Tok:
    __slots__ = ("eng", "idx", "marked", "count_after")

    def __init__(self, eng, idx):
        self.eng = eng
        self.idx = idx
        self.marked = False
        self.count_after = None


class _Builder:

    def __init__(self):
        self.ops = {e: [] for e in ENGINES}
        self.stream_n = {q: 0 for q in STREAMS}

    def emit(self, eng, fn, waits=(), dma=None):
        tok = _Tok(eng, len(self.ops[eng]))
        wl = {}
        for w in waits:
            if w is None:
                continue
            k = w.eng
            if k not in wl or w.idx > wl[k].idx:
                wl[k] = w
        for w in wl.values():
            w.marked = True
        prewait = None
        if dma is not None:
            tok.eng = dma
            tok.idx = self.stream_n[dma]
            tok.marked = True
            tok.count_after = 16 * (self.stream_n[dma] + 1)
            if self.stream_n[dma] > 0 and dma not in UNORDERED:
                prewait = 16 * self.stream_n[dma]
            self.stream_n[dma] += 1
        self.ops[eng].append({"fn": fn, "waits": list(wl.values()), "dma": dma,
                              "tok": tok, "prewait": prewait})
        return tok

    def resolve(self):
        for e in ENGINES:
            c = 0
            for op in self.ops[e]:
                if op["dma"] is not None:
                    continue
                if op["tok"].marked:
                    c += 1
                op["tok"].count_after = c

    def run(self, eng_obj, name, sems):
        wm = {}

        def w(key, val):
            if wm.get(key, -1) >= val:
                return
            eng_obj.wait_ge(sems[key], val)
            wm[key] = val

        for op in self.ops[name]:
            if op["prewait"] is not None:
                w(op["dma"], op["prewait"])
            for wt in op["waits"]:
                w(wt.eng, wt.count_after)
            inst = op["fn"](eng_obj)
            if op["dma"] is not None:
                inst.then_inc(sems[op["dma"]], 16)
            elif op["tok"].marked:
                inst.then_inc(sems[op["tok"].eng], 1)


def build_program():
    nc = bass.Bass("TRN2", target_bir_lowering=False)
    X = nc.dram_tensor("x", [L, D], F32, kind="ExternalInput").ap()
    WDBC = nc.dram_tensor("w_dbc", [96, D], F32, kind="ExternalInput").ap()
    WDT = nc.dram_tensor("w_dt", [D, 64], F32, kind="ExternalInput").ap()
    BDT = nc.dram_tensor("b_dt", [D], F32, kind="ExternalInput").ap()
    ALOG = nc.dram_tensor("a_log", [D, S], F32, kind="ExternalInput").ap()
    DPAR = nc.dram_tensor("d_par", [D], F32, kind="ExternalInput").ap()
    WINT = nc.dram_tensor("w_int", [D, D], I32, kind="ExternalInput").ap()
    WSC = nc.dram_tensor("w_sc", [1], F32, kind="ExternalInput").ap()
    FBIAS = nc.dram_tensor("f_bias", [D], F32, kind="ExternalInput").ap()
    Y = nc.dram_tensor("y", [L, D], F32, kind="ExternalOutput").ap()
    BCSCR = nc.dram_tensor("bcscr", [32, T], BF16, kind="Internal").ap()

    from contextlib import ExitStack
    with ExitStack() as ctx:
        def sb(name, shape, dt):
            return ctx.enter_context(nc.sbuf_tensor(name, shape, dt))

        xin = sb("xin", [128, 4096], F32)
        xntm = sb("xntm", [128, 2048], F32)
        scr = sb("scr", [128, 2048], F32)
        spbuf = sb("spbuf", [128, 1024], F32)
        preg = sb("preg", [128, 1024], F32)
        sgbuf = sb("sgbuf", [128, 1024], F32)
        gbuf = sb("gbuf", [128, 1024], F32)
        sfm = sb("sfm", [128, 1536], F32)
        xnfm = sb("xnfm", [128, 4096], F32R)
        xqfm = sb("xqfm", [128, 4096], F16)
        delta = sb("delta", [128, 4096], BF16)
        ubuf = sb("ubuf", [128, 4096], BF16)
        drones = sb("drones", [65, T], F32R)
        btctbf = sb("btctbf", [32, T], BF16)
        bcil = sb("bcil", [128, 32 * T], BF16)
        apool = sb("apool", [128, 3 * T], F32)
        bpool = sb("bpool", [128, 4 * T], BF16)
        dxnbuf = sb("dxnbuf", [128, 2 * T], BF16)
        hall = sb("hall", [128, 16 * T], BF16)
        ppool = sb("ppool", [128, 2 * 1024], BF16)
        wffn = sb("wffn", [128, 8192], F16)
        wdbc = sb("wdbc", [128, 768], F32R)
        wdt = sb("wdt", [65, 1024], F32R)
        acol = sb("acol", [128, 128], F32)
        dcol = sb("dcol", [128, 8], F32)
        biascol = sb("biascol", [128, 8], F32)
        cscol = sb("cscol", [128, 8], F32)
        ccols = sb("ccols", [128, 4], F32)
        wsrep = sb("wsrep", [128, 1], F32)
        bnbuf = sb("bnbuf", [128, 48], F32)
        mv = sb("mv", [128, 8], F32)
        cells = sb("cells", [128, 32], F32)
        eps5 = sb("eps5", [128, 1], F32)
        eps6 = sb("eps6", [128, 1], F32)
        onecol = sb("onecol", [128, 1], F32)
        ident32 = sb("ident32", [128, 128], F32)
        ident16 = sb("ident16", [128, 128], F16)
        identbf = sb("identbf", [128, 128], BF16)
        onesrow = sb("onesrow", [1, 128], F32R)
        onescol16 = sb("onescol16", [128, 1], F16)
        crow = sb("crow", [1, T], F32R)
        crepsb = sb("crepsb", [128, T], F32)
        statebuf = sb("statebuf", [128, 256], F32)
        xresm = sb("xresm", [128, 1024], F32)
        outm = sb("outm", [128, 1024], F32)

        ps = [ctx.enter_context(nc.psum_tensor(f"ps{i}", [128, 512], F32))
              for i in range(8)]

        sems = {}
        for e in ENGINES + STREAMS:
            sems[e] = ctx.enter_context(nc.semaphore(f"sem_{e}"))
        block = ctx.enter_context(nc.Block())

        B = _Builder()
        E = B.emit

        def cell(j, i):
            return cells[:, j * 8 + i:j * 8 + i + 1]

        reg_i32 = xin[:, 0:1024].bitcast(I32)
        reg_f32 = xin[:, 1024:2048]
        reg_dbc = xin[0:96, 2048:3072]
        reg_wdt = xin[:, 3072:3584]
        reg_alog = xin[:, 3584:3712]

        t_gset = E("gpsimd", lambda g: g.memset(ident32[:], 1.0))
        t_gset = E("gpsimd", lambda g: g.affine_select(
            out=ident32[:], in_=ident32[:], pattern=[[1, 128]],
            compare_op=ALU.is_equal, fill=0.0, base=0, channel_multiplier=-1),
            waits=[t_gset])
        t_g16 = E("gpsimd", lambda g: g.memset(ident16[:], 1.0))
        t_g16 = E("gpsimd", lambda g: g.affine_select(
            out=ident16[:], in_=ident16[:], pattern=[[1, 128]],
            compare_op=ALU.is_equal, fill=0.0, base=0, channel_multiplier=-1),
            waits=[t_g16])
        t_gbf = E("gpsimd", lambda g: g.memset(identbf[:], 1.0))
        t_gbf = E("gpsimd", lambda g: g.affine_select(
            out=identbf[:], in_=identbf[:], pattern=[[1, 128]],
            compare_op=ALU.is_equal, fill=0.0, base=0, channel_multiplier=-1),
            waits=[t_gbf])
        t_eps5 = E("gpsimd", lambda g: g.memset(eps5[:], 1e-5))
        t_eps6 = E("gpsimd", lambda g: g.memset(eps6[:], 1e-6))
        t_one = E("gpsimd", lambda g: g.memset(onecol[:], 1.0))
        t_orow0 = E("gpsimd", lambda g: g.memset(scr[0:1, 0:640], 1.0))
        t_orow = E("vector", lambda v: v.tensor_copy(
            out=onesrow[:], in_=scr[0:1, 0:128]), waits=[t_orow0])
        t_oc16 = E("gpsimd", lambda g: g.memset(onescol16[:], 1.0))
        t_drow = E("vector", lambda v: v.tensor_copy(
            out=drones[64:65, :], in_=scr[0:1, 128:640]), waits=[t_orow0])
        t_state = E("gpsimd", lambda g: g.memset(statebuf[:], 0.0))

        t_dcol = E("sync", lambda s: s.dma_start(
            out=dcol[:], in_=DPAR.rearrange("(k p) -> p k", p=128),
            allow_slow_non_contiguous=True), dma="qw")
        t_bias = E("sync", lambda s: s.dma_start(
            out=biascol[:], in_=FBIAS.rearrange("(k p) -> p k", p=128),
            allow_slow_non_contiguous=True), dma="qw")
        t_bdt = E("sync", lambda s: s.dma_start(
            out=wdt[64:65, :], in_=BDT.rearrange("(o d) -> o d", o=1).bitcast(F32R)), dma="qw")
        t_aload = E("sync", lambda s: s.dma_start(
            out=reg_alog.rearrange("p (k s) -> p k s", k=NK),
            in_=ALOG.rearrange("(k p) s -> p k s", p=128)), dma="qw")
        t_dbcl = E("sync", lambda s: s.dma_start(out=reg_dbc, in_=WDBC), dma="qw")
        t_wdtl = E("sync", lambda s: s.dma_start(
            out=reg_wdt.rearrange("p (k r) -> p k r", k=NK),
            in_=WDT.rearrange("(k p) r -> p k r", p=128)), dma="qw")

        t_wsrep_dma = E("sync", lambda s: s.dma_start(
            out=wsrep[:], in_=WSC.rearrange("(o d) -> o d", o=1)
            .broadcast_to([128, 1])), dma="qw")
        t_wsrep = t_wsrep_dma

        t_ae = E("scalar", lambda s: s.activation(
            out=scr[:, 0:128], in_=reg_alog, func=AF.Exp),
            waits=[t_aload, t_orow])
        t_acol = E("vector", lambda v: v.tensor_scalar(
            out=acol[:], in0=scr[:, 0:128], scalar1=-1.0, scalar2=None,
            op0=ALU.mult), waits=[t_ae])

        t_dbc_ev = [None] * NK
        for k in range(NK):
            bank = ps[k % 2]
            war = t_dbc_ev[k - 2] if k >= 2 else None
            t_tr = E("tensor", lambda t, k=k, bank=bank: t.transpose(
                out=bank[:, 0:96], in_=reg_dbc[:, k * 128:(k + 1) * 128],
                identity=ident32[0:96, 0:96]),
                waits=[t_dbcl, t_gset, war])
            t_dbc_ev[k] = E("vector", lambda v, k=k, bank=bank: v.tensor_copy(
                out=wdbc[:, k * 96:(k + 1) * 96], in_=bank[:, 0:96]), waits=[t_tr])

        t_wdt_ev = [None] * NK
        for k in range(NK):
            bank = ps[2 + k % 2]
            war = t_wdt_ev[k - 2] if k >= 2 else None
            t_tr = E("tensor", lambda t, k=k, bank=bank: t.transpose(
                out=bank[0:64, 0:128], in_=reg_wdt[:, k * 64:(k + 1) * 64],
                identity=ident32[:]),
                waits=[t_wdtl, t_gset, war])
            t_wdt_ev[k] = E("vector", lambda v, k=k, bank=bank: v.tensor_copy(
                out=wdt[0:64, k * 128:(k + 1) * 128], in_=bank[0:64, 0:128]),
                waits=[t_tr])

        t_wload = None
        t_wcopy = None
        t_wev_last = None
        t_wtr_last = [None, None]
        for m in range(NK):
            t_wload = E("sync", lambda s, m=m: s.dma_start(
                out=reg_i32, in_=WINT[m * 128:(m + 1) * 128, :]),
                dma="qw", waits=[t_wcopy])
            t_wcopy = E("vector", lambda v: v.tensor_copy(out=reg_f32, in_=reg_i32),
                        waits=[t_wload, t_wtr_last[1] or t_wtr_last[0]])
            for half in range(2):
                bank = ps[4 + half]
                trs = []
                for kk in range(4):
                    k = half * 4 + kk
                    trs.append(E("tensor", lambda t, k=k, kk=kk, bank=bank: t.transpose(
                        out=bank[:, kk * 128:(kk + 1) * 128],
                        in_=reg_f32[:, k * 128:(k + 1) * 128],
                        identity=ident32[:]),
                        waits=[t_wcopy, t_wev_last, t_gset]))
                t_wtr_last[half] = trs[-1]
                dst = wffn[:].rearrange("p (k o) -> p k o", k=NK)[
                    :, half * 4:(half + 1) * 4, m * 128:(m + 1) * 128]
                t_wev_last = E("vector", lambda v, bank=bank, dst=dst: v.tensor_copy(
                    out=dst, in_=bank[:].rearrange("p (a b) -> p a b", a=4)),
                    waits=[trs[-1]])

        t_cs = [None] * NK
        for m in range(NK):
            mm_last = None
            for k in range(NK):
                mm_last = E("tensor", lambda t, m=m, k=k: t.matmul(
                    out=ps[6][:, 0:1],
                    lhsT=wffn[:, k * 1024 + m * 128:k * 1024 + (m + 1) * 128],
                    rhs=onescol16[:], start=(k == 0), stop=(k == NK - 1)),
                    waits=[t_wev_last, t_oc16, t_cs[m - 1] if m else None])
            t_cs[m] = E("vector", lambda v, m=m: v.tensor_scalar(
                out=cscol[:, m:m + 1], in0=ps[6][:, 0:1], scalar1=-1536.0,
                scalar2=None, op0=ALU.mult), waits=[mm_last])

        t_setup_w = t_wev_last
        t_setup_x = t_cs[NK - 1]
        t_setup_pe = t_wtr_last[1]
        t_scr_set = t_acol
        t_evac_bank_init = [t_dbc_ev[NK - 2], t_dbc_ev[NK - 1]]

        t_lnts = [[None] * NJ for _ in range(NCH + 1)]
        t_xin_dma = [None] * NJ
        t_xn_ev = None
        t_xq_ev = None
        t_state_gather = None
        t_sfm_roll = [t_wsrep, t_wsrep, t_wsrep]
        t_sg_slot = [None, None, None]
        t_preg_roll = [t_wev_last, t_cs[NK - 1]]
        t_silu_roll = [None, None]
        t_sgw_roll = [None, None]
        t_gbuf_roll = [None, None]
        ps34_reader = [t_wev_last, t_wev_last]
        t_sp_roll = [None, None]
        t_xqslot = [None] * NJ
        t_scrslot = [None, None]
        t_evac_bank_roll = list(t_evac_bank_init)
        t_crow_prev = [None] * NJ
        t_quant_j = [None] * NJ
        t_ln_tok = [None] * NJ
        t_cc_next = None
        t_dxn = [None] * NK
        t_yred_of = {}
        R = {"dr": None, "bt": None, "bcw": None, "crepsb": None,
             "dmm_last": None, "ffn_last": None, "tcc": None,
             "il_last": [], "ilB": [], "ilC": [],
             "tb_last": None, "tym_last_vec": None, "tym_last_g": None,
             "bcmm": None, "pend_ev": None, "pre0": None}
        t_outblk_m = [None] * (NK + 2)
        t_outdma_m = [None] * (NK + 2)
        t_xres_m = [None] * (NK + 2)
        t_scan_hist = {}
        t_scan_all_prev = None
        t_ymult_last_prev_chunk = None
        t_dbc_done_prev = None
        t_crepsb = None
        t_yp_reader_prev = t_wsrep

        for c in range(NCH):
            par = c % 2
            cur0 = par * 128
            nxt0 = (1 - par) * 128

            def ph1a(cc):
                for j in range(NJ):
                    w = ([t_lnts[cc - 1][j], t_quant_j[j]] if cc > 0 else
                         [t_setup_pe, t_setup_x, t_scr_set, t_ae])
                    t_xin_dma[j] = E("sync", lambda s, cc=cc, j=j: s.dma_start(
                        out=xin[:, j * 1024:(j + 1) * 1024],
                        in_=X[cc * T + j * 128:cc * T + (j + 1) * 128, :]),
                        dma="qxin", waits=w)
                tag = None
                for j in range(NJ):
                    E("vector", lambda v, j=j: v.bn_stats(
                        out=bnbuf[:, j * 12:j * 12 + 6],
                        in_=xin[:, j * 1024:j * 1024 + 512]), waits=[t_xin_dma[j]])
                    tb2 = E("vector", lambda v, j=j: v.bn_stats(
                        out=bnbuf[:, j * 12 + 6:j * 12 + 12],
                        in_=xin[:, j * 1024 + 512:j * 1024 + 1024]),
                        waits=[t_xin_dma[j]])
                    tag = E("vector", lambda v, j=j: v.bn_aggr(
                        out=mv[:, j * 2:j * 2 + 2],
                        in_=bnbuf[:, j * 12:j * 12 + 12]), waits=[tb2])
                mv3 = mv[:].rearrange("p (j two) -> p j two", two=2)
                tcc_l = [None, None]
                for pair in range(2):
                    j0 = pair * 2
                    vars2 = mv3[:, j0:j0 + 2, 1:2].squeeze(2)
                    means2 = mv3[:, j0:j0 + 2, 0:1].squeeze(2)
                    cl = lambda a, b, pair=pair: cells[:, pair * 16 + a:
                                                       pair * 16 + b]
                    tl2 = E("scalar", lambda s, vars2=vars2, cl=cl: s.activation(
                        out=cl(0, 2), in_=vars2, func=AF.Ln, bias=eps5[:, 0:1]),
                        waits=[tag, t_eps5])
                    tr2_ = E("scalar", lambda s, cl=cl: s.activation(
                        out=cl(2, 4), in_=cl(0, 2), func=AF.Exp, scale=-0.5),
                        waits=[tl2])
                    tn2 = E("vector", lambda v, means2=means2, cl=cl:
                            v.scalar_tensor_tensor(
                                out=cl(4, 6), in0=means2, scalar=-1.0,
                                in1=cl(2, 4), op0=ALU.mult, op1=ALU.mult),
                            waits=[tr2_])
                    t_am_l = [None, None]
                    for jj in range(2):
                        j = j0 + jj
                        jr = j % 2
                        t_ln = E("vector", lambda v, j=j, jj=jj, cl=cl:
                                 v.tensor_scalar(
                                     out=xin[:, j * 1024:(j + 1) * 1024],
                                     in0=xin[:, j * 1024:(j + 1) * 1024],
                                     scalar1=cl(2 + jj, 3 + jj),
                                     scalar2=cl(4 + jj, 5 + jj),
                                     op0=ALU.mult, op1=ALU.add),
                                 waits=[tn2])
                        t_ln_tok[j] = t_ln
                        t_am_l[jj] = E("vector", lambda v, j=j, jj=jj, cl=cl:
                                       v.reduce_max(
                                           out=cl(6 + jj, 7 + jj),
                                           in_=xin[:, j * 1024:(j + 1) * 1024],
                                           axis=AX.X,
                                           apply_absolute_value=True),
                                       waits=[t_ln])
                    tr22 = E("vector", lambda v, cl=cl: v.tensor_mul(
                        out=cl(8, 10), in0=cl(2, 4), in1=cl(2, 4)), waits=[tn2])
                    tms = E("vector", lambda v, vars2=vars2, cl=cl:
                            v.tensor_tensor(
                                out=cl(0, 2), in0=vars2, in1=cl(8, 10),
                                op=ALU.mult), waits=[tr22, tl2])
                    tlm = E("scalar", lambda s, cl=cl: s.activation(
                        out=cl(0, 2), in_=cl(0, 2), func=AF.Ln,
                        bias=eps6[:, 0:1]), waits=[tms, t_eps6])
                    trq = E("scalar", lambda s, cl=cl: s.activation(
                        out=cl(10, 12), in_=cl(0, 2), func=AF.Exp, scale=-0.5),
                        waits=[tlm])
                    tamn = E("vector", lambda v, cl=cl: v.tensor_mul(
                        out=cl(12, 14), in0=cl(6, 8), in1=cl(10, 12)),
                        waits=[trq, t_am_l[1]])
                    tamc = E("vector", lambda v, cl=cl: v.tensor_scalar(
                        out=cl(12, 14), in0=cl(12, 14), scalar1=1e-5,
                        scalar2=None, op0=ALU.max), waits=[tamn])
                    trec = E("vector", lambda v, cl=cl: v.reciprocal(
                        out=cl(0, 2), in_=cl(12, 14)), waits=[tamc])
                    tqc2 = E("vector", lambda v, cl=cl: v.scalar_tensor_tensor(
                        out=cl(14, 16), in0=cl(0, 2), scalar=127.0,
                        in1=cl(10, 12), op0=ALU.mult, op1=ALU.mult),
                        waits=[trec])
                    tcc_l[pair] = E("vector", lambda v, j0=j0, cl=cl:
                                    v.scalar_tensor_tensor(
                                        out=ccols[:, j0:j0 + 2], in0=cl(12, 14),
                                        scalar=1.0 / 127.0,
                                        in1=wsrep[:, 0:1].broadcast_to([128, 2]),
                                        op0=ALU.mult, op1=ALU.mult),
                                    waits=[tqc2, t_wsrep, t_crow_prev[j0],
                                           t_crow_prev[j0 + 1]])
                    for jj in range(2):
                        j = j0 + jj
                        jr = j % 2
                        tq1 = E("vector", lambda v, j=j, jr=jr, jj=jj, cl=cl:
                                v.tensor_scalar(
                                    out=scr[:, jr * 1024:(jr + 1) * 1024],
                                    in0=xin[:, j * 1024:(j + 1) * 1024],
                                    scalar1=cl(14 + jj, 15 + jj),
                                    scalar2=1536.0,
                                    op0=ALU.mult, op1=ALU.add),
                                waits=[tqc2])
                        t_quant_j[j] = E("vector", lambda v, j=j, jr=jr:
                                         v.tensor_scalar(
                                             out=xntm[:].bitcast(F16)[
                                                 :, j * 1024:(j + 1) * 1024],
                                             in0=scr[:, jr * 1024:
                                                     (jr + 1) * 1024],
                                             scalar1=1408.0, scalar2=1663.0,
                                             op0=ALU.max, op1=ALU.min),
                                        waits=[tq1, t_xqslot[j]])
                return tcc_l

            def ph1b_xn(cc, extra_w):
                nonlocal t_xn_ev
                for j in range(NJ):
                    jr = j % 2
                    t_ln = t_ln_tok[j]
                    for half in range(2):
                        bank = ps[half]
                        trs_last = None
                        for kk in range(4):
                            k = half * 4 + kk
                            trs_last = E("tensor", lambda t, j=j, k=k, kk=kk,
                                         bank=bank: t.transpose(
                                             out=bank[:, kk * 128:
                                                      (kk + 1) * 128],
                                             in_=xin[:, j * 1024 + k * 128:
                                                     j * 1024 + (k + 1) * 128],
                                             identity=ident32[:]),
                                         waits=[t_ln, t_evac_bank_roll[half],
                                                t_gset])
                        dst = xnfm[:].rearrange("p (k t2) -> p k t2", k=NK)[
                            :, half * 4:(half + 1) * 4, j * 128:(j + 1) * 128]
                        t_evac_bank_roll[half] = E(
                            "scalar", lambda sc, bank=bank, dst=dst: sc.copy(
                                out=dst,
                                in_=bank[:].rearrange("p (a b) -> p a b", a=4)),
                            waits=[trs_last] + extra_w)
                        t_xn_ev = t_evac_bank_roll[half]
                        t_lnts[cc][j] = trs_last

            def ph1b_xq(cc, extra_w):
                nonlocal t_xq_ev
                for j in range(NJ):
                    tq16 = t_quant_j[j]
                    xq_trs = None
                    for half in range(2):
                        bank = ps[half]
                        trs_last = None
                        for kk in range(4):
                            k = half * 4 + kk
                            src = xntm[:].bitcast(F16)[
                                :, j * 1024 + k * 128:j * 1024 + (k + 1) * 128]
                            trs_last = E("tensor", lambda t, src=src, kk=kk,
                                         bank=bank: t.transpose(
                                             out=bank[:].bitcast(F16)[
                                                 :, kk * 128:(kk + 1) * 128],
                                             in_=src, identity=ident16[:]),
                                         waits=[tq16, t_evac_bank_roll[half],
                                                t_g16])
                        xq_trs = trs_last
                        dst = xqfm[:].rearrange("p (k t2) -> p k t2", k=NK)[
                            :, half * 4:(half + 1) * 4, j * 128:(j + 1) * 128]
                        t_evac_bank_roll[half] = E(
                            "scalar", lambda sc, bank=bank, dst=dst: sc.copy(
                                out=dst,
                                in_=bank[:].bitcast(F16)[:, 0:512].rearrange(
                                    "p (a b) -> p a b", a=4)),
                            waits=[trs_last] + extra_w)
                        t_xq_ev = t_evac_bank_roll[half]
                    t_xqslot[j] = xq_trs

            def ph2a1_mm(cc):
                mm_last = None
                for k in range(NK):
                    w = [t_xn_ev, t_dbc_ev[NK - 1]]
                    if k == 0:
                        w += [R["crepsb"] if cc > 0 else t_wdt_ev[NK - 2]]
                    mm_last = E("tensor", lambda t, k=k: t.matmul(
                        out=ps[2][0:96, :], lhsT=wdbc[:, k * 96:(k + 1) * 96],
                        rhs=xnfm[:, k * T:(k + 1) * T],
                        start=(k == 0), stop=(k == NK - 1)), waits=w)
                R["bcmm"] = mm_last

            def ph2a1_ev(cc):
                mm_last = R["bcmm"]
                R["dr"] = E("vector", lambda v: v.tensor_copy(
                    out=drones[0:64, :], in_=ps[2][0:64, :]),
                    waits=[mm_last, t_drow, R["dmm_last"]])
                R["bt"] = E("vector", lambda v: v.tensor_copy(
                    out=btctbf[0:32, :], in_=ps[2][64:96, :]),
                    waits=[mm_last, R["bcw"]])
                R["bcw"] = E("sync", lambda sy: sy.dma_start(
                    out=BCSCR, in_=btctbf[0:32, :]),
                    dma="qbcw", waits=[R["bt"]] + R["il_last"])

            def ph2a1_ilB(cc):
                wB = [R["bcw"]] + ([R["tb_last"]] if cc > 0 else [])
                ilB = [None, None]
                for i in range(8):
                    for half in range(2):
                        r = 2 * i + half
                        qe, qs = (("sync", "qilb_s") if half == 0 else
                                  ("gpsimd", "qilb_g"))
                        ilB[half] = E(qe, lambda q, r=r: q.dma_start(
                            out=bcil[:, r * T:(r + 1) * T],
                            in_=BCSCR[r:r + 1, :].broadcast_to([128, T])),
                            dma=qs, waits=wB)
                R["ilB"] = ilB

            def ph2a1_ilC(cc):
                wC = [R["bcw"]] + ([R["tym_last_vec"], R["tym_last_g"]]
                                   if cc > 0 else [])
                ilC = [None, None]
                for i in range(8):
                    for half in range(2):
                        rc = 16 + 2 * i + half
                        qe, qs = (("scalar", "qilc_sc") if half == 0 else
                                  ("sync", "qilc_s2"))
                        ilC[half] = E(qe, lambda q, rc=rc: q.dma_start(
                            out=bcil[:, rc * T:(rc + 1) * T],
                            in_=BCSCR[rc:rc + 1, :].broadcast_to([128, T])),
                            dma=qs, waits=wC)
                R["ilC"] = ilC
                R["il_last"] = R["ilB"] + ilC

            def ph2a2(cc, tcc_pair):
                t_crow = [None] * NJ
                for j in range(NJ):
                    t_crow[j] = E("sync", lambda s, j=j: s.dma_start(
                        out=crow[0:1, j * 128:(j + 1) * 128],
                        in_=ccols[:, j:j + 1].bitcast(F32R),
                        allow_slow_non_contiguous=True),
                        dma="qcrow",
                        waits=[tcc_pair[0] if j < 2 else tcc_pair[1]])
                    t_crow_prev[j] = t_crow[j]
                t_crep = E("tensor", lambda t: t.matmul(
                    out=ps[2][:], lhsT=onesrow[0:1, :], rhs=crow[0:1, :],
                    start=True, stop=True),
                    waits=[t_crow[NJ - 1], R["bt"], R["dr"], t_orow])
                R["crepsb"] = E("vector", lambda v: v.tensor_copy(
                    out=crepsb[:], in_=ps[2][:]),
                    waits=[t_crep, t_preg_roll[0], t_preg_roll[1]])

            if c == 0:
                tcc_pair0 = ph1a(0)
                ph1b_xn(0, [])
                ph1b_xq(0, [])
                ph2a1_mm(0)
                ph2a1_ev(0)
                ph2a1_ilB(0)
                ph2a1_ilC(0)
                R["tcc"] = tcc_pair0
                ph2a2(0, R["tcc"])
            t_crepsb = R["crepsb"]
            t_dr = R["dr"]

            t_ta3 = [None] * NK
            t_deltaSTT = [None] * NK
            t_u = [None] * NK
            t_state_k = [None] * NK
            t_sfm_tok = [None] * NK
            t_tpg_m = [None] * NK
            t_ymult_g7 = [None] * NK
            scan_last_of_k = [None] * NK
            scan_toks = [None] * NK
            pend_sfm = [None]

            def ph2b_head(k):
                bank = ps[3 + k % 2]
                t_dmm = E("tensor", lambda t, k=k, bank=bank: t.matmul(
                    out=bank[:], lhsT=wdt[0:65, k * 128:(k + 1) * 128],
                    rhs=drones[0:65, :], start=True, stop=True),
                    waits=[R["dr"], t_wdt_ev[NK - 1], t_bdt,
                           ps34_reader[k % 2]])
                R["dmm_last"] = t_dmm
                sp0 = spbuf[:, 0:512]
                sp1 = spbuf[:, 512:1024]
                ta1 = E("scalar", lambda s, bank=bank: s.activation(
                    out=sp0, in_=bank[:], func=AF.Abs),
                    waits=[t_dmm, t_sp_roll[0]])
                ta2 = E("scalar", lambda s: s.activation(
                    out=sp1, in_=sp0, func=AF.Exp, scale=-1.0),
                    waits=[ta1, t_sp_roll[1]])
                ta3 = E("scalar", lambda s: s.activation(
                    out=sp0, in_=sp1, func=AF.Ln, bias=onecol[:, 0:1]),
                    waits=[ta2, t_one])
                t_sp_roll[1] = ta3
                t_ta3[k] = ta3

            def ph2b_tail(k):
                bank = ps[3 + k % 2]
                t_deltaSTT[k] = E("vector", lambda v, k=k, bank=bank:
                                  v.scalar_tensor_tensor(
                                      out=delta[:, k * T:(k + 1) * T],
                                      in0=bank[:], scalar=0.0, in1=sp0_ap(),
                                      op0=ALU.max, op1=ALU.add),
                                  waits=[t_ta3[k]])
                t_sp_roll[0] = t_deltaSTT[k]
                ps34_reader[k % 2] = t_deltaSTT[k]
                t_u[k] = E("gpsimd", lambda g, k=k: g.tensor_tensor(
                    out=ubuf[:, k * T:(k + 1) * T],
                    in0=delta[:, k * T:(k + 1) * T],
                    in1=xnfm[:, k * T:(k + 1) * T].bitcast(F32),
                    op=ALU.mult), waits=[t_deltaSTT[k]])

            def sp0_ap():
                return spbuf[:, 0:512]

            def ph2b(k):
                ph2b_head(k)
                ph2b_tail(k)

            def ph2b_dxn(k):
                t_dxn[k] = E("vector", lambda v, k=k: v.tensor_scalar(
                    out=dxnbuf[:, (k % 2) * T:(k % 2 + 1) * T],
                    in0=xnfm[:, k * T:(k + 1) * T].bitcast(F32),
                    scalar1=dcol[:, k:k + 1], scalar2=None, op0=ALU.mult),
                    waits=[t_xn_ev, t_dcol,
                           t_yred_of.get((k - 2) % NK)])

            def emit_exp(k, s_):
                i = k * S + s_
                ia = i % 3
                wa = [t_deltaSTT[k], t_acol]
                prev_a_user = t_scan_hist.get(("a", ia))
                if prev_a_user is not None:
                    wa.append(prev_a_user)
                return E("scalar", lambda s, k=k, s_=s_, ia=ia:
                         s.activation(
                             out=apool[:, ia * T:(ia + 1) * T],
                             in_=delta[:, k * T:(k + 1) * T],
                             func=AF.Exp,
                             scale=acol[:, k * S + s_:k * S + s_ + 1]),
                         waits=wa)

            def emit_scans(k, a_toks):
                nonlocal t_scan_all_prev
                scan_toks[k] = [None] * S
                tb = None
                for s_ in range(S):
                    a_toks.append(emit_exp(k, s_))
                    i = k * S + s_
                    ia = i % 3
                    if s_ % 2 == 0:
                        ibp = (i // 2) % 2
                        wb = [t_u[k]] + R["ilB"]
                        prev_b_user = t_scan_hist.get(("b", ibp))
                        if prev_b_user is not None:
                            wb.append(prev_b_user)
                        usrc = ubuf[:, k * T:(k + 1) * T].unsqueeze(1)\
                            .broadcast_to([128, 2, T])
                        tb = E("vector", lambda v, k=k, s_=s_, ibp=ibp,
                               usrc=usrc: v.tensor_tensor(
                                   out=bpool[:, ibp * 2 * T:(ibp + 1) * 2 * T]
                                   .rearrange("p (a b) -> p a b", a=2),
                                   in0=usrc,
                                   in1=bcil[:, s_ * T:(s_ + 2) * T]
                                   .rearrange("p (a b) -> p a b", a=2),
                                   op=ALU.mult), waits=wb)
                        R["tb_last"] = tb
                    ibp = (i // 2) % 2
                    boff = ibp * 2 * T + (s_ % 2) * T
                    ws = [a_toks[s_], tb, t_state]
                    if s_ == 0:
                        ws.append(t_state_k[k - 1] if k > 0 else t_state_gather)
                    off = cur0 + k * S + s_
                    tsc = E("vector", lambda v, s_=s_, ia=ia, boff=boff,
                            off=off: v.tensor_tensor_scan(
                                out=hall[:, s_ * T:(s_ + 1) * T],
                                data0=apool[:, ia * T:(ia + 1) * T],
                                data1=bpool[:, boff:boff + T],
                                initial=statebuf[:, off:off + 1],
                                op0=ALU.mult, op1=ALU.add), waits=ws)
                    t_scan_hist[("a", ia)] = tsc
                    if s_ % 2 == 1:
                        t_scan_hist[("b", ibp)] = tsc
                    scan_toks[k][s_] = tsc
                    if s_ == S - 1:
                        scan_last_of_k[k] = tsc
                        t_scan_all_prev = tsc
                    if s_ == 1 and k + 1 < NK:
                        ph2b_dxn(k + 1)
                    if s_ == 2:
                        emit_sfm()
                    if s_ == 10 and R["pend_ev"] is not None:
                        cc2 = R["pend_ev"]
                        R["pend_ev"] = None
                        ph2a1_ev(cc2)

            def emit_ytail(k):
                nonlocal t_yp_reader_prev
                nxt_ok = (k + 1 < NK) and not (c > 0 and k + 1 <= 2)
                if nxt_ok:
                    ph2b_head(k + 1)
                t_state_k[k] = E("gpsimd", lambda g, k=k: g.tensor_copy(
                    out=statebuf[:, nxt0 + k * S:nxt0 + (k + 1) * S],
                    in_=hall[:].rearrange("p (s t2) -> p s t2", s=S)
                    [:, :, T - 1:T].squeeze(2)), waits=[scan_last_of_k[k]])
                t_yred = None
                for g in range(8):
                    ip = g % 2
                    eng_y = "gpsimd" if g == 7 else "vector"
                    wy = [scan_toks[k][2 * g + 1]] + R["ilC"]
                    prev_p_user = t_scan_hist.get(("p", ip))
                    if prev_p_user is not None:
                        wy.append(prev_p_user)
                    tym = E(eng_y, lambda e, g=g, ip=ip: e.tensor_tensor(
                        out=ppool[:, ip * 1024:(ip + 1) * 1024],
                        in0=hall[:, g * 1024:(g + 1) * 1024],
                        in1=bcil[:, (16 + 2 * g) * T:(16 + 2 * g + 2) * T],
                        op=ALU.mult), waits=wy)
                    if g == 7:
                        t_ymult_g7[k] = tym
                        R["tym_last_g"] = tym
                    else:
                        R["tym_last_vec"] = tym
                    tok = None
                    for half in range(2):
                        w = [tym, t_gbf]
                        if g == 0 and half == 0:
                            w.append(t_yp_reader_prev)
                        tok = E("tensor", lambda t, ip=ip, half=half, g=g:
                                t.matmul(
                                    out=ps[7][:],
                                    lhsT=identbf[:],
                                    rhs=ppool[:, ip * 1024 + half * 512:
                                              ip * 1024 + (half + 1) * 512],
                                    start=(g == 0 and half == 0),
                                    stop=False), waits=w)
                    t_scan_hist[("p", ip)] = tok
                    t_yred = tok
                t_yred = E("tensor", lambda t, k=k: t.matmul(
                    out=ps[7][:], lhsT=identbf[:],
                    rhs=dxnbuf[:, (k % 2) * T:(k % 2 + 1) * T],
                    start=False, stop=True), waits=[t_dxn[k], t_yred])
                t_yred_of[k] = t_yred
                pend_sfm[0] = k
                if nxt_ok:
                    ph2b_tail(k + 1)

            def emit_sfm():
                nonlocal t_yp_reader_prev
                k = pend_sfm[0]
                if k is None:
                    return
                pend_sfm[0] = None
                sl = k % 3
                t_sfm_tok[k] = E("vector", lambda v, k=k, sl=sl: v.tensor_copy(
                    out=sfm[:, sl * T:(sl + 1) * T], in_=ps[7][:]),
                    waits=[t_yred_of[k], t_sg_slot[sl]])
                t_yp_reader_prev = t_sfm_tok[k]

            def ph4a(m):
                rot = m % 2
                bank = ps[5 + rot]
                xre = X[c * T:(c + 1) * T, m * 128:(m + 1) * 128].rearrange(
                    "(j p) w -> p j w", p=128)
                t_xres_m[m] = E("sync", lambda s, rot=rot, xre=xre: s.dma_start(
                    out=xresm[:, rot * 512:(rot + 1) * 512].rearrange(
                        "p (j w) -> p j w", j=NJ), in_=xre),
                    dma="qxres", waits=[t_outblk_m[m - 2]])
                mm = None
                for k2 in range(NK):
                    w = [t_xq_ev, t_setup_w]
                    if k2 == 0:
                        w += [t_preg_roll[rot]]
                    mm = E("tensor", lambda t, m=m, k2=k2, bank=bank: t.matmul(
                        out=bank[:],
                        lhsT=wffn[:, k2 * 1024 + m * 128:k2 * 1024 + (m + 1) * 128],
                        rhs=xqfm[:, k2 * T:(k2 + 1) * T],
                        start=(k2 == 0), stop=(k2 == NK - 1)), waits=w)
                R["ffn_last"] = mm
                t_tpg_m[m] = E("vector", lambda v, m=m, rot=rot, bank=bank:
                               v.scalar_tensor_tensor(
                                   out=preg[:, rot * 512:(rot + 1) * 512],
                                   in0=bank[:], scalar=cscol[:, m:m + 1],
                                   in1=crepsb[:], op0=ALU.add, op1=ALU.mult),
                               waits=[mm, t_crepsb, t_cs[NK - 1],
                                      t_silu_roll[rot]])
                t_preg_roll[rot] = t_tpg_m[m]

            def ph4c(m):
                rot = m % 2
                tsl = E("scalar", lambda s, m=m, rot=rot: s.activation(
                    out=gbuf[:, rot * 512:(rot + 1) * 512],
                    in_=preg[:, rot * 512:(rot + 1) * 512], func=AF.Silu,
                    bias=biascol[:, m:m + 1]),
                    waits=[t_tpg_m[m], t_bias, t_gbuf_roll[rot]])
                t_silu_roll[rot] = tsl
                tsg = E("gpsimd", lambda g, m=m, rot=rot: g.tensor_tensor(
                    out=sgbuf[:, rot * 512:(rot + 1) * 512],
                    in0=sfm[:, (m % 3) * T:(m % 3 + 1) * T],
                    in1=gbuf[:, rot * 512:(rot + 1) * 512], op=ALU.mult),
                    waits=[tsl, t_sfm_tok[m], t_sgw_roll[rot]])
                t_gbuf_roll[rot] = tsg
                t_sg_slot[m % 3] = tsg
                last_blk = None
                for j in range(NJ):
                    tbank = ps[3 + (m * NJ + j) % 2]
                    ttr = E("tensor", lambda t, rot=rot, j=j, tbank=tbank:
                            t.transpose(
                                out=tbank[:, 0:128],
                                in_=sgbuf[:, rot * 512 + j * 128:
                                          rot * 512 + (j + 1) * 128],
                                identity=ident32[:]),
                            waits=[tsg, ps34_reader[(m * NJ + j) % 2]])
                    last_blk = E("vector", lambda v, rot=rot, j=j, tbank=tbank:
                                 v.tensor_tensor(
                                     out=outm[:, rot * 512 + j * 128:
                                              rot * 512 + (j + 1) * 128],
                                     in0=xresm[:, rot * 512 + j * 128:
                                               rot * 512 + (j + 1) * 128],
                                     in1=tbank[:, 0:128], op=ALU.add),
                                 waits=[ttr, t_xres_m[m],
                                        t_outdma_m[m - 2] if j == 0 else None])
                    ps34_reader[(m * NJ + j) % 2] = last_blk
                t_sgw_roll[rot] = last_blk
                t_outblk_m[m] = last_blk
                yap = Y[c * T:(c + 1) * T, m * 128:(m + 1) * 128].rearrange(
                    "(j p) w -> p j w", p=128)
                t_outdma_m[m] = E("sync", lambda s, rot=rot, yap=yap:
                                  s.dma_start(
                                      out=yap,
                                      in_=outm[:, rot * 512:(rot + 1) * 512]
                                      .rearrange("p (j w) -> p j w", j=NJ)),
                                  dma="qout", waits=[last_blk])

            if c == 0:
                ph2b(0)
            else:
                t_deltaSTT[0], t_u[0] = R["pre0"]
                t_deltaSTT[1], t_u[1] = R["pre1"]
                t_deltaSTT[2], t_u[2] = R["pre2"]
            ph2b_dxn(0)
            for k in range(NK):
                if k >= 3 and k % 2 == 1:
                    ph4c(k - 3)
                    ph4c(k - 2)
                a_toks = []
                emit_scans(k, a_toks)
                emit_ytail(k)
                if c + 1 < NCH:
                    if k == 4:
                        R["tcc"] = ph1a(c + 1)
                    elif k == 6:
                        ph1b_xn(c + 1, [t_u[NK - 1], t_dxn[NK - 1]])
                        ph2a1_mm(c + 1)
                        R["pend_ev"] = c + 1
                    elif k == 7:
                        ph2a1_ilB(c + 1)
                        ph2b(0)
                        R["pre0"] = (t_deltaSTT[0], t_u[0])
                if k >= 1:
                    ph4a(k - 1)
            emit_sfm()
            ph4a(NK - 1)
            ph4c(NK - 2)
            ph4c(NK - 1)
            if c + 1 < NCH:
                ph2a2(c + 1, R["tcc"])
                ph2a1_ilC(c + 1)
                ph2b(1)
                ph2b(2)
                R["pre1"] = (t_deltaSTT[1], t_u[1])
                R["pre2"] = (t_deltaSTT[2], t_u[2])
                ph1b_xq(c + 1, [R["ffn_last"]])
            t_state_gather = t_state_k[NK - 1]
            t_ymult_last_prev_chunk = t_ymult_g7[NK - 1]
            t_outblk_m[-2] = t_outblk_m[NK - 2]
            t_outblk_m[-1] = t_outblk_m[NK - 1]
            t_outdma_m[-2] = t_outdma_m[NK - 2]
            t_outdma_m[-1] = t_outdma_m[NK - 1]

        E("sync", lambda s: s.nop(),
          waits=[t_outdma_m[NK - 2], t_outdma_m[NK - 1]])

        B.resolve()
        for name, meth in (("sync", block.sync), ("vector", block.vector),
                           ("scalar", block.scalar), ("tensor", block.tensor),
                           ("gpsimd", block.gpsimd)):
            meth(lambda eng, name=name: B.run(eng, name, sems))

    return nc


_NC_CACHE = None


def kernel(L0_attn_x, L0_attn_W_deltaBC, L0_attn_W_dt, L0_attn_b_dt,
           L0_attn_A_log, L0_attn_D_param, L0_ffn_w_int8, L0_ffn_weight_scale,
           L0_ffn_bias):
    global _NC_CACHE
    if _NC_CACHE is None:
        _NC_CACHE = build_program()
    nc = _NC_CACHE

    x = np.ascontiguousarray(np.asarray(L0_attn_x, dtype=np.float32))
    base = {
        "w_dbc": np.ascontiguousarray(np.asarray(L0_attn_W_deltaBC, np.float32)),
        "w_dt": np.ascontiguousarray(np.asarray(L0_attn_W_dt, np.float32)),
        "b_dt": np.ascontiguousarray(np.asarray(L0_attn_b_dt, np.float32)),
        "a_log": np.ascontiguousarray(np.asarray(L0_attn_A_log, np.float32)),
        "d_par": np.ascontiguousarray(np.asarray(L0_attn_D_param, np.float32)),
        "w_int": np.ascontiguousarray(np.asarray(L0_ffn_w_int8, np.int32)),
        "w_sc": np.asarray(L0_ffn_weight_scale, np.float32).reshape(1),
        "f_bias": np.ascontiguousarray(np.asarray(L0_ffn_bias, np.float32)),
    }
    in_maps = []
    for b in range(NCORE):
        m = dict(base)
        m["x"] = np.ascontiguousarray(x[b])
        in_maps.append(m)
    res = run_bass_kernel_spmd(nc, in_maps, core_ids=list(range(NCORE)))
    out = np.stack([np.asarray(r["y"], np.float32) for r in res.results], axis=0)
    return out


# version 26
# speedup vs baseline: 1.0209x; 1.0009x over previous
import numpy as np
import concourse.bass as bass
import concourse.mybir as mybir
from concourse.bass_utils import run_bass_kernel_spmd

F32 = mybir.dt.float32
F32R = mybir.dt.float32r
F16 = mybir.dt.float16
BF16 = mybir.dt.bfloat16
I32 = mybir.dt.int32
AF = mybir.ActivationFunctionType
ALU = mybir.AluOpType
AX = mybir.AxisListType

L = 4096
D = 1024
S = 16
T = 512
NCH = L // T
NJ = T // 128
NK = D // 128
NCORE = 8

ENGINES = ("sync", "vector", "scalar", "tensor", "gpsimd")
STREAMS = ("qxin", "qxres", "qout", "qcrow", "qbcw", "qw",
           "qilb_s", "qilb_g", "qilc_sc", "qilc_s2")
UNORDERED = {"qilb_s", "qilb_g", "qilc_sc", "qilc_s2"}


class _Tok:
    __slots__ = ("eng", "idx", "marked", "count_after")

    def __init__(self, eng, idx):
        self.eng = eng
        self.idx = idx
        self.marked = False
        self.count_after = None


class _Builder:

    def __init__(self):
        self.ops = {e: [] for e in ENGINES}
        self.stream_n = {q: 0 for q in STREAMS}

    def emit(self, eng, fn, waits=(), dma=None):
        tok = _Tok(eng, len(self.ops[eng]))
        wl = {}
        for w in waits:
            if w is None:
                continue
            k = w.eng
            if k not in wl or w.idx > wl[k].idx:
                wl[k] = w
        for w in wl.values():
            w.marked = True
        prewait = None
        if dma is not None:
            tok.eng = dma
            tok.idx = self.stream_n[dma]
            tok.marked = True
            tok.count_after = 16 * (self.stream_n[dma] + 1)
            if self.stream_n[dma] > 0 and dma not in UNORDERED:
                prewait = 16 * self.stream_n[dma]
            self.stream_n[dma] += 1
        self.ops[eng].append({"fn": fn, "waits": list(wl.values()), "dma": dma,
                              "tok": tok, "prewait": prewait})
        return tok

    def resolve(self):
        for e in ENGINES:
            c = 0
            for op in self.ops[e]:
                if op["dma"] is not None:
                    continue
                if op["tok"].marked:
                    c += 1
                op["tok"].count_after = c

    def run(self, eng_obj, name, sems):
        wm = {}

        def w(key, val):
            if wm.get(key, -1) >= val:
                return
            eng_obj.wait_ge(sems[key], val)
            wm[key] = val

        for op in self.ops[name]:
            if op["prewait"] is not None:
                w(op["dma"], op["prewait"])
            for wt in op["waits"]:
                w(wt.eng, wt.count_after)
            inst = op["fn"](eng_obj)
            if op["dma"] is not None:
                inst.then_inc(sems[op["dma"]], 16)
            elif op["tok"].marked:
                inst.then_inc(sems[op["tok"].eng], 1)


def build_program():
    nc = bass.Bass("TRN2", target_bir_lowering=False)
    X = nc.dram_tensor("x", [L, D], F32, kind="ExternalInput").ap()
    WDBC = nc.dram_tensor("w_dbc", [96, D], F32, kind="ExternalInput").ap()
    WDT = nc.dram_tensor("w_dt", [D, 64], F32, kind="ExternalInput").ap()
    BDT = nc.dram_tensor("b_dt", [D], F32, kind="ExternalInput").ap()
    ALOG = nc.dram_tensor("a_log", [D, S], F32, kind="ExternalInput").ap()
    DPAR = nc.dram_tensor("d_par", [D], F32, kind="ExternalInput").ap()
    WINT = nc.dram_tensor("w_int", [D, D], I32, kind="ExternalInput").ap()
    WSC = nc.dram_tensor("w_sc", [1], F32, kind="ExternalInput").ap()
    FBIAS = nc.dram_tensor("f_bias", [D], F32, kind="ExternalInput").ap()
    Y = nc.dram_tensor("y", [L, D], F32, kind="ExternalOutput").ap()
    BCSCR = nc.dram_tensor("bcscr", [32, T], BF16, kind="Internal").ap()

    from contextlib import ExitStack
    with ExitStack() as ctx:
        def sb(name, shape, dt):
            return ctx.enter_context(nc.sbuf_tensor(name, shape, dt))

        xin = sb("xin", [128, 4096], F32)
        xntm = sb("xntm", [128, 2048], F32)
        scr = sb("scr", [128, 2048], F32)
        spbuf = sb("spbuf", [128, 1024], F32)
        preg = sb("preg", [128, 1024], F32)
        sgbuf = sb("sgbuf", [128, 1024], F32)
        gbuf = sb("gbuf", [128, 1024], F32)
        sfm = sb("sfm", [128, 1536], F32)
        xnfm = sb("xnfm", [128, 4096], F32R)
        xqfm = sb("xqfm", [128, 4096], F16)
        delta = sb("delta", [128, 4096], BF16)
        ubuf = sb("ubuf", [128, 4096], BF16)
        drones = sb("drones", [65, T], F32R)
        btctbf = sb("btctbf", [32, T], BF16)
        bcil = sb("bcil", [128, 32 * T], BF16)
        apool = sb("apool", [128, 3 * T], F32)
        bpool = sb("bpool", [128, 4 * T], BF16)
        dxnbuf = sb("dxnbuf", [128, 2 * T], BF16)
        hall = sb("hall", [128, 16 * T], BF16)
        ppool = sb("ppool", [128, 2 * 1024], BF16)
        wffn = sb("wffn", [128, 8192], F16)
        wdbc = sb("wdbc", [128, 768], F32R)
        wdt = sb("wdt", [65, 1024], F32R)
        acol = sb("acol", [128, 128], F32)
        dcol = sb("dcol", [128, 8], F32)
        biascol = sb("biascol", [128, 8], F32)
        cscol = sb("cscol", [128, 8], F32)
        ccols = sb("ccols", [128, 4], F32)
        wsrep = sb("wsrep", [128, 1], F32)
        bnbuf = sb("bnbuf", [128, 48], F32)
        mv = sb("mv", [128, 8], F32)
        cells = sb("cells", [128, 32], F32)
        eps5 = sb("eps5", [128, 1], F32)
        eps6 = sb("eps6", [128, 1], F32)
        onecol = sb("onecol", [128, 1], F32)
        ident32 = sb("ident32", [128, 128], F32)
        ident16 = sb("ident16", [128, 128], F16)
        identbf = sb("identbf", [128, 128], BF16)
        onesrow = sb("onesrow", [1, 128], F32R)
        onescol16 = sb("onescol16", [128, 1], F16)
        crow = sb("crow", [1, T], F32R)
        crepsb = sb("crepsb", [128, T], F32)
        statebuf = sb("statebuf", [128, 256], F32)
        xresm = sb("xresm", [128, 1024], F32)
        outm = sb("outm", [128, 1024], F32)

        ps = [ctx.enter_context(nc.psum_tensor(f"ps{i}", [128, 512], F32))
              for i in range(8)]

        sems = {}
        for e in ENGINES + STREAMS:
            sems[e] = ctx.enter_context(nc.semaphore(f"sem_{e}"))
        block = ctx.enter_context(nc.Block())

        B = _Builder()
        E = B.emit

        def cell(j, i):
            return cells[:, j * 8 + i:j * 8 + i + 1]

        reg_i32 = xin[:, 0:1024].bitcast(I32)
        reg_f32 = xin[:, 1024:2048]
        reg_dbc = xin[0:96, 2048:3072]
        reg_wdt = xin[:, 3072:3584]
        reg_alog = xin[:, 3584:3712]

        t_gset = E("gpsimd", lambda g: g.memset(ident32[:], 1.0))
        t_gset = E("gpsimd", lambda g: g.affine_select(
            out=ident32[:], in_=ident32[:], pattern=[[1, 128]],
            compare_op=ALU.is_equal, fill=0.0, base=0, channel_multiplier=-1),
            waits=[t_gset])
        t_g16 = E("gpsimd", lambda g: g.memset(ident16[:], 1.0))
        t_g16 = E("gpsimd", lambda g: g.affine_select(
            out=ident16[:], in_=ident16[:], pattern=[[1, 128]],
            compare_op=ALU.is_equal, fill=0.0, base=0, channel_multiplier=-1),
            waits=[t_g16])
        t_gbf = E("gpsimd", lambda g: g.memset(identbf[:], 1.0))
        t_gbf = E("gpsimd", lambda g: g.affine_select(
            out=identbf[:], in_=identbf[:], pattern=[[1, 128]],
            compare_op=ALU.is_equal, fill=0.0, base=0, channel_multiplier=-1),
            waits=[t_gbf])
        t_eps5 = E("gpsimd", lambda g: g.memset(eps5[:], 1e-5))
        t_eps6 = E("gpsimd", lambda g: g.memset(eps6[:], 1e-6))
        t_one = E("gpsimd", lambda g: g.memset(onecol[:], 1.0))
        t_orow0 = E("gpsimd", lambda g: g.memset(scr[0:1, 0:640], 1.0))
        t_orow = E("vector", lambda v: v.tensor_copy(
            out=onesrow[:], in_=scr[0:1, 0:128]), waits=[t_orow0])
        t_oc16 = E("gpsimd", lambda g: g.memset(onescol16[:], 1.0))
        t_drow = E("vector", lambda v: v.tensor_copy(
            out=drones[64:65, :], in_=scr[0:1, 128:640]), waits=[t_orow0])
        t_state = E("gpsimd", lambda g: g.memset(statebuf[:], 0.0))

        t_dcol = E("sync", lambda s: s.dma_start(
            out=dcol[:], in_=DPAR.rearrange("(k p) -> p k", p=128),
            allow_slow_non_contiguous=True), dma="qw")
        t_bias = E("sync", lambda s: s.dma_start(
            out=biascol[:], in_=FBIAS.rearrange("(k p) -> p k", p=128),
            allow_slow_non_contiguous=True), dma="qw")
        t_bdt = E("sync", lambda s: s.dma_start(
            out=wdt[64:65, :], in_=BDT.rearrange("(o d) -> o d", o=1).bitcast(F32R)), dma="qw")
        t_aload = E("sync", lambda s: s.dma_start(
            out=reg_alog.rearrange("p (k s) -> p k s", k=NK),
            in_=ALOG.rearrange("(k p) s -> p k s", p=128)), dma="qw")
        t_dbcl = E("sync", lambda s: s.dma_start(out=reg_dbc, in_=WDBC), dma="qw")
        t_wdtl = E("sync", lambda s: s.dma_start(
            out=reg_wdt.rearrange("p (k r) -> p k r", k=NK),
            in_=WDT.rearrange("(k p) r -> p k r", p=128)), dma="qw")

        t_wsrep_dma = E("sync", lambda s: s.dma_start(
            out=wsrep[:], in_=WSC.rearrange("(o d) -> o d", o=1)
            .broadcast_to([128, 1])), dma="qw")
        t_wsrep = t_wsrep_dma

        t_ae = E("scalar", lambda s: s.activation(
            out=scr[:, 0:128], in_=reg_alog, func=AF.Exp),
            waits=[t_aload, t_orow])
        t_acol = E("vector", lambda v: v.tensor_scalar(
            out=acol[:], in0=scr[:, 0:128], scalar1=-1.0, scalar2=None,
            op0=ALU.mult), waits=[t_ae])

        t_dbc_ev = [None] * NK
        for k in range(NK):
            bank = ps[k % 2]
            war = t_dbc_ev[k - 2] if k >= 2 else None
            t_tr = E("tensor", lambda t, k=k, bank=bank: t.transpose(
                out=bank[:, 0:96], in_=reg_dbc[:, k * 128:(k + 1) * 128],
                identity=ident32[0:96, 0:96]),
                waits=[t_dbcl, t_gset, war])
            t_dbc_ev[k] = E("vector", lambda v, k=k, bank=bank: v.tensor_copy(
                out=wdbc[:, k * 96:(k + 1) * 96], in_=bank[:, 0:96]), waits=[t_tr])

        t_wdt_ev = [None] * NK
        for k in range(NK):
            bank = ps[2 + k % 2]
            war = t_wdt_ev[k - 2] if k >= 2 else None
            t_tr = E("tensor", lambda t, k=k, bank=bank: t.transpose(
                out=bank[0:64, 0:128], in_=reg_wdt[:, k * 64:(k + 1) * 64],
                identity=ident32[:]),
                waits=[t_wdtl, t_gset, war])
            t_wdt_ev[k] = E("vector", lambda v, k=k, bank=bank: v.tensor_copy(
                out=wdt[0:64, k * 128:(k + 1) * 128], in_=bank[0:64, 0:128]),
                waits=[t_tr])

        t_wload = None
        t_wcopy = None
        t_wev_last = None
        t_wtr_last = [None, None]
        for m in range(NK):
            t_wload = E("sync", lambda s, m=m: s.dma_start(
                out=reg_i32, in_=WINT[m * 128:(m + 1) * 128, :]),
                dma="qw", waits=[t_wcopy])
            t_wcopy = E("vector", lambda v: v.tensor_copy(out=reg_f32, in_=reg_i32),
                        waits=[t_wload, t_wtr_last[1] or t_wtr_last[0]])
            for half in range(2):
                bank = ps[4 + half]
                trs = []
                for kk in range(4):
                    k = half * 4 + kk
                    trs.append(E("tensor", lambda t, k=k, kk=kk, bank=bank: t.transpose(
                        out=bank[:, kk * 128:(kk + 1) * 128],
                        in_=reg_f32[:, k * 128:(k + 1) * 128],
                        identity=ident32[:]),
                        waits=[t_wcopy, t_wev_last, t_gset]))
                t_wtr_last[half] = trs[-1]
                dst = wffn[:].rearrange("p (k o) -> p k o", k=NK)[
                    :, half * 4:(half + 1) * 4, m * 128:(m + 1) * 128]
                t_wev_last = E("vector", lambda v, bank=bank, dst=dst: v.tensor_copy(
                    out=dst, in_=bank[:].rearrange("p (a b) -> p a b", a=4)),
                    waits=[trs[-1]])

        t_cs = [None] * NK
        for m in range(NK):
            mm_last = None
            for k in range(NK):
                mm_last = E("tensor", lambda t, m=m, k=k: t.matmul(
                    out=ps[6][:, 0:1],
                    lhsT=wffn[:, k * 1024 + m * 128:k * 1024 + (m + 1) * 128],
                    rhs=onescol16[:], start=(k == 0), stop=(k == NK - 1)),
                    waits=[t_wev_last, t_oc16, t_cs[m - 1] if m else None])
            t_cs[m] = E("vector", lambda v, m=m: v.tensor_scalar(
                out=cscol[:, m:m + 1], in0=ps[6][:, 0:1], scalar1=-1536.0,
                scalar2=None, op0=ALU.mult), waits=[mm_last])

        t_setup_w = t_wev_last
        t_setup_x = t_cs[NK - 1]
        t_setup_pe = t_wtr_last[1]
        t_scr_set = t_acol
        t_evac_bank_init = [t_dbc_ev[NK - 2], t_dbc_ev[NK - 1]]

        t_lnts = [[None] * NJ for _ in range(NCH + 1)]
        t_xin_dma = [None] * NJ
        t_xn_ev = None
        t_xq_ev = None
        t_state_gather = None
        t_sfm_roll = [t_wsrep, t_wsrep, t_wsrep]
        t_sg_slot = [None, None, None]
        t_preg_roll = [t_wev_last, t_cs[NK - 1]]
        t_silu_roll = [None, None]
        t_sgw_roll = [None, None]
        t_gbuf_roll = [None, None]
        ps34_reader = [t_wev_last, t_wev_last]
        t_sp_roll = [None, None]
        t_xqslot = [None] * NJ
        t_scrslot = [None, None]
        t_evac_bank_roll = list(t_evac_bank_init)
        t_crow_prev = [None] * NJ
        t_quant_j = [None] * NJ
        t_ln_tok = [None] * NJ
        t_cc_next = None
        t_dxn = [None] * NK
        t_yred_of = {}
        R = {"dr": None, "bt": None, "bcw": None, "crepsb": None,
             "dmm_last": None, "ffn_last": None, "tcc": None,
             "il_last": [], "ilB": [], "ilC": [],
             "tb_last": None, "tym_last_vec": None, "tym_last_g": None,
             "bcmm": None, "pend_ev": None, "pre0": None}
        t_outblk_m = [None] * (NK + 2)
        t_outdma_m = [None] * (NK + 2)
        t_xres_m = [None] * (NK + 2)
        t_scan_hist = {}
        t_scan_all_prev = None
        t_ymult_last_prev_chunk = None
        t_dbc_done_prev = None
        t_crepsb = None
        t_yp_reader_prev = t_wsrep

        for c in range(NCH):
            par = c % 2
            cur0 = par * 128
            nxt0 = (1 - par) * 128

            def ph1a(cc):
                for j in range(NJ):
                    w = ([t_lnts[cc - 1][j], t_quant_j[j]] if cc > 0 else
                         [t_setup_pe, t_setup_x, t_scr_set, t_ae])
                    t_xin_dma[j] = E("sync", lambda s, cc=cc, j=j: s.dma_start(
                        out=xin[:, j * 1024:(j + 1) * 1024],
                        in_=X[cc * T + j * 128:cc * T + (j + 1) * 128, :]),
                        dma="qxin", waits=w)
                tag = None
                for j in range(NJ):
                    E("vector", lambda v, j=j: v.bn_stats(
                        out=bnbuf[:, j * 12:j * 12 + 6],
                        in_=xin[:, j * 1024:j * 1024 + 512]), waits=[t_xin_dma[j]])
                    tb2 = E("vector", lambda v, j=j: v.bn_stats(
                        out=bnbuf[:, j * 12 + 6:j * 12 + 12],
                        in_=xin[:, j * 1024 + 512:j * 1024 + 1024]),
                        waits=[t_xin_dma[j]])
                    tag = E("vector", lambda v, j=j: v.bn_aggr(
                        out=mv[:, j * 2:j * 2 + 2],
                        in_=bnbuf[:, j * 12:j * 12 + 12]), waits=[tb2])
                mv3 = mv[:].rearrange("p (j two) -> p j two", two=2)
                tcc_l = [None, None]
                for pair in range(2):
                    j0 = pair * 2
                    vars2 = mv3[:, j0:j0 + 2, 1:2].squeeze(2)
                    means2 = mv3[:, j0:j0 + 2, 0:1].squeeze(2)
                    cl = lambda a, b, pair=pair: cells[:, pair * 16 + a:
                                                       pair * 16 + b]
                    tl2 = E("scalar", lambda s, vars2=vars2, cl=cl: s.activation(
                        out=cl(0, 2), in_=vars2, func=AF.Ln, bias=eps5[:, 0:1]),
                        waits=[tag, t_eps5])
                    tr2_ = E("scalar", lambda s, cl=cl: s.activation(
                        out=cl(2, 4), in_=cl(0, 2), func=AF.Exp, scale=-0.5),
                        waits=[tl2])
                    tn2 = E("vector", lambda v, means2=means2, cl=cl:
                            v.scalar_tensor_tensor(
                                out=cl(4, 6), in0=means2, scalar=-1.0,
                                in1=cl(2, 4), op0=ALU.mult, op1=ALU.mult),
                            waits=[tr2_])
                    t_am_l = [None, None]
                    for jj in range(2):
                        j = j0 + jj
                        jr = j % 2
                        t_ln = E("vector", lambda v, j=j, jj=jj, cl=cl:
                                 v.tensor_scalar(
                                     out=xin[:, j * 1024:(j + 1) * 1024],
                                     in0=xin[:, j * 1024:(j + 1) * 1024],
                                     scalar1=cl(2 + jj, 3 + jj),
                                     scalar2=cl(4 + jj, 5 + jj),
                                     op0=ALU.mult, op1=ALU.add),
                                 waits=[tn2])
                        t_ln_tok[j] = t_ln
                        t_am_l[jj] = E("vector", lambda v, j=j, jj=jj, cl=cl:
                                       v.reduce_max(
                                           out=cl(6 + jj, 7 + jj),
                                           in_=xin[:, j * 1024:(j + 1) * 1024],
                                           axis=AX.X,
                                           apply_absolute_value=True),
                                       waits=[t_ln])
                    tr22 = E("vector", lambda v, cl=cl: v.tensor_mul(
                        out=cl(8, 10), in0=cl(2, 4), in1=cl(2, 4)), waits=[tn2])
                    tms = E("vector", lambda v, vars2=vars2, cl=cl:
                            v.tensor_tensor(
                                out=cl(0, 2), in0=vars2, in1=cl(8, 10),
                                op=ALU.mult), waits=[tr22, tl2])
                    tlm = E("scalar", lambda s, cl=cl: s.activation(
                        out=cl(0, 2), in_=cl(0, 2), func=AF.Ln,
                        bias=eps6[:, 0:1]), waits=[tms, t_eps6])
                    trq = E("scalar", lambda s, cl=cl: s.activation(
                        out=cl(10, 12), in_=cl(0, 2), func=AF.Exp, scale=-0.5),
                        waits=[tlm])
                    tamn = E("vector", lambda v, cl=cl: v.tensor_mul(
                        out=cl(12, 14), in0=cl(6, 8), in1=cl(10, 12)),
                        waits=[trq, t_am_l[1]])
                    tamc = E("vector", lambda v, cl=cl: v.tensor_scalar(
                        out=cl(12, 14), in0=cl(12, 14), scalar1=1e-5,
                        scalar2=None, op0=ALU.max), waits=[tamn])
                    trec = E("vector", lambda v, cl=cl: v.reciprocal(
                        out=cl(0, 2), in_=cl(12, 14)), waits=[tamc])
                    tqc2 = E("vector", lambda v, cl=cl: v.scalar_tensor_tensor(
                        out=cl(14, 16), in0=cl(0, 2), scalar=127.0,
                        in1=cl(10, 12), op0=ALU.mult, op1=ALU.mult),
                        waits=[trec])
                    tcc_l[pair] = E("vector", lambda v, j0=j0, cl=cl:
                                    v.scalar_tensor_tensor(
                                        out=ccols[:, j0:j0 + 2], in0=cl(12, 14),
                                        scalar=1.0 / 127.0,
                                        in1=wsrep[:, 0:1].broadcast_to([128, 2]),
                                        op0=ALU.mult, op1=ALU.mult),
                                    waits=[tqc2, t_wsrep, t_crow_prev[j0],
                                           t_crow_prev[j0 + 1]])
                    for jj in range(2):
                        j = j0 + jj
                        jr = j % 2
                        tq1 = E("vector", lambda v, j=j, jr=jr, jj=jj, cl=cl:
                                v.tensor_scalar(
                                    out=scr[:, jr * 1024:(jr + 1) * 1024],
                                    in0=xin[:, j * 1024:(j + 1) * 1024],
                                    scalar1=cl(14 + jj, 15 + jj),
                                    scalar2=1536.0,
                                    op0=ALU.mult, op1=ALU.add),
                                waits=[tqc2])
                        t_quant_j[j] = E("vector", lambda v, j=j, jr=jr:
                                         v.tensor_scalar(
                                             out=xntm[:].bitcast(F16)[
                                                 :, j * 1024:(j + 1) * 1024],
                                             in0=scr[:, jr * 1024:
                                                     (jr + 1) * 1024],
                                             scalar1=1408.0, scalar2=1663.0,
                                             op0=ALU.max, op1=ALU.min),
                                        waits=[tq1, t_xqslot[j]])
                return tcc_l

            def ph1b_xn(cc, extra_w):
                nonlocal t_xn_ev
                for j in range(NJ):
                    jr = j % 2
                    t_ln = t_ln_tok[j]
                    for half in range(2):
                        bank = ps[half]
                        trs_last = None
                        for kk in range(4):
                            k = half * 4 + kk
                            trs_last = E("tensor", lambda t, j=j, k=k, kk=kk,
                                         bank=bank: t.transpose(
                                             out=bank[:, kk * 128:
                                                      (kk + 1) * 128],
                                             in_=xin[:, j * 1024 + k * 128:
                                                     j * 1024 + (k + 1) * 128],
                                             identity=ident32[:]),
                                         waits=[t_ln, t_evac_bank_roll[half],
                                                t_gset])
                        dst = xnfm[:].rearrange("p (k t2) -> p k t2", k=NK)[
                            :, half * 4:(half + 1) * 4, j * 128:(j + 1) * 128]
                        t_evac_bank_roll[half] = E(
                            "scalar", lambda sc, bank=bank, dst=dst: sc.copy(
                                out=dst,
                                in_=bank[:].rearrange("p (a b) -> p a b", a=4)),
                            waits=[trs_last] + extra_w)
                        t_xn_ev = t_evac_bank_roll[half]
                        t_lnts[cc][j] = trs_last

            def ph1b_xq(cc, extra_w):
                nonlocal t_xq_ev
                for j in range(NJ):
                    tq16 = t_quant_j[j]
                    xq_trs = None
                    for half in range(2):
                        bank = ps[half]
                        trs_last = None
                        for kk in range(4):
                            k = half * 4 + kk
                            src = xntm[:].bitcast(F16)[
                                :, j * 1024 + k * 128:j * 1024 + (k + 1) * 128]
                            trs_last = E("tensor", lambda t, src=src, kk=kk,
                                         bank=bank: t.transpose(
                                             out=bank[:].bitcast(F16)[
                                                 :, kk * 128:(kk + 1) * 128],
                                             in_=src, identity=ident16[:]),
                                         waits=[tq16, t_evac_bank_roll[half],
                                                t_g16])
                        xq_trs = trs_last
                        dst = xqfm[:].rearrange("p (k t2) -> p k t2", k=NK)[
                            :, half * 4:(half + 1) * 4, j * 128:(j + 1) * 128]
                        t_evac_bank_roll[half] = E(
                            "scalar", lambda sc, bank=bank, dst=dst: sc.copy(
                                out=dst,
                                in_=bank[:].bitcast(F16)[:, 0:512].rearrange(
                                    "p (a b) -> p a b", a=4)),
                            waits=[trs_last] + extra_w)
                        t_xq_ev = t_evac_bank_roll[half]
                    t_xqslot[j] = xq_trs

            def ph2a1_mm(cc):
                mm_last = None
                for k in range(NK):
                    w = [t_xn_ev, t_dbc_ev[NK - 1]]
                    if k == 0:
                        w += [R["crepsb"] if cc > 0 else t_wdt_ev[NK - 2]]
                    mm_last = E("tensor", lambda t, k=k: t.matmul(
                        out=ps[2][0:96, :], lhsT=wdbc[:, k * 96:(k + 1) * 96],
                        rhs=xnfm[:, k * T:(k + 1) * T],
                        start=(k == 0), stop=(k == NK - 1)), waits=w)
                R["bcmm"] = mm_last

            def ph2a1_ev(cc):
                mm_last = R["bcmm"]
                R["dr"] = E("vector", lambda v: v.tensor_copy(
                    out=drones[0:64, :], in_=ps[2][0:64, :]),
                    waits=[mm_last, t_drow, R["dmm_last"]])
                R["bt"] = E("vector", lambda v: v.tensor_copy(
                    out=btctbf[0:32, :], in_=ps[2][64:96, :]),
                    waits=[mm_last, R["bcw"]])
                R["bcw"] = E("sync", lambda sy: sy.dma_start(
                    out=BCSCR, in_=btctbf[0:32, :]),
                    dma="qbcw", waits=[R["bt"]] + R["il_last"])

            def ph2a1_ilB(cc):
                wB = [R["bcw"]] + ([R["tb_last"]] if cc > 0 else [])
                ilB = [None, None]
                for i in range(8):
                    for half in range(2):
                        r = 2 * i + half
                        qe, qs = (("sync", "qilb_s") if half == 0 else
                                  ("gpsimd", "qilb_g"))
                        ilB[half] = E(qe, lambda q, r=r: q.dma_start(
                            out=bcil[:, r * T:(r + 1) * T],
                            in_=BCSCR[r:r + 1, :].broadcast_to([128, T])),
                            dma=qs, waits=wB)
                R["ilB"] = ilB

            def ph2a1_ilC(cc):
                wC = [R["bcw"]] + ([R["tym_last_vec"], R["tym_last_g"]]
                                   if cc > 0 else [])
                ilC = [None, None]
                for i in range(8):
                    for half in range(2):
                        rc = 16 + 2 * i + half
                        qe, qs = (("scalar", "qilc_sc") if half == 0 else
                                  ("sync", "qilc_s2"))
                        ilC[half] = E(qe, lambda q, rc=rc: q.dma_start(
                            out=bcil[:, rc * T:(rc + 1) * T],
                            in_=BCSCR[rc:rc + 1, :].broadcast_to([128, T])),
                            dma=qs, waits=wC)
                R["ilC"] = ilC
                R["il_last"] = R["ilB"] + ilC

            def ph2a2(cc, tcc_pair):
                t_crow = [None] * NJ
                for j in range(NJ):
                    t_crow[j] = E("sync", lambda s, j=j: s.dma_start(
                        out=crow[0:1, j * 128:(j + 1) * 128],
                        in_=ccols[:, j:j + 1].bitcast(F32R),
                        allow_slow_non_contiguous=True),
                        dma="qcrow",
                        waits=[tcc_pair[0] if j < 2 else tcc_pair[1]])
                    t_crow_prev[j] = t_crow[j]
                t_crep = E("tensor", lambda t: t.matmul(
                    out=ps[2][:], lhsT=onesrow[0:1, :], rhs=crow[0:1, :],
                    start=True, stop=True),
                    waits=[t_crow[NJ - 1], R["bt"], R["dr"], t_orow])
                R["crepsb"] = E("vector", lambda v: v.tensor_copy(
                    out=crepsb[:], in_=ps[2][:]),
                    waits=[t_crep, t_preg_roll[0], t_preg_roll[1]])

            if c == 0:
                tcc_pair0 = ph1a(0)
                ph1b_xn(0, [])
                ph1b_xq(0, [])
                ph2a1_mm(0)
                ph2a1_ev(0)
                ph2a1_ilB(0)
                ph2a1_ilC(0)
                R["tcc"] = tcc_pair0
                ph2a2(0, R["tcc"])
            t_crepsb = R["crepsb"]
            t_dr = R["dr"]

            t_ta3 = [None] * NK
            t_deltaSTT = [None] * NK
            t_u = [None] * NK
            t_state_k = [None] * NK
            t_sfm_tok = [None] * NK
            t_tpg_m = [None] * NK
            t_ymult_g7 = [None] * NK
            scan_last_of_k = [None] * NK
            scan_toks = [None] * NK
            pend_sfm = [None]

            def ph2b_head(k):
                bank = ps[3 + k % 2]
                t_dmm = E("tensor", lambda t, k=k, bank=bank: t.matmul(
                    out=bank[:], lhsT=wdt[0:65, k * 128:(k + 1) * 128],
                    rhs=drones[0:65, :], start=True, stop=True),
                    waits=[R["dr"], t_wdt_ev[NK - 1], t_bdt,
                           ps34_reader[k % 2]])
                R["dmm_last"] = t_dmm
                sp0 = spbuf[:, 0:512]
                sp1 = spbuf[:, 512:1024]
                ta1 = E("scalar", lambda s, bank=bank: s.activation(
                    out=sp0, in_=bank[:], func=AF.Abs),
                    waits=[t_dmm, t_sp_roll[0]])
                ta2 = E("scalar", lambda s: s.activation(
                    out=sp1, in_=sp0, func=AF.Exp, scale=-1.0),
                    waits=[ta1, t_sp_roll[1]])
                ta3 = E("scalar", lambda s: s.activation(
                    out=sp0, in_=sp1, func=AF.Ln, bias=onecol[:, 0:1]),
                    waits=[ta2, t_one])
                t_sp_roll[1] = ta3
                t_ta3[k] = ta3

            def ph2b_tail(k):
                bank = ps[3 + k % 2]
                t_deltaSTT[k] = E("vector", lambda v, k=k, bank=bank:
                                  v.scalar_tensor_tensor(
                                      out=delta[:, k * T:(k + 1) * T],
                                      in0=bank[:], scalar=0.0, in1=sp0_ap(),
                                      op0=ALU.max, op1=ALU.add),
                                  waits=[t_ta3[k]])
                t_sp_roll[0] = t_deltaSTT[k]
                ps34_reader[k % 2] = t_deltaSTT[k]
                t_u[k] = E("gpsimd", lambda g, k=k: g.tensor_tensor(
                    out=ubuf[:, k * T:(k + 1) * T],
                    in0=delta[:, k * T:(k + 1) * T],
                    in1=xnfm[:, k * T:(k + 1) * T].bitcast(F32),
                    op=ALU.mult), waits=[t_deltaSTT[k]])

            def sp0_ap():
                return spbuf[:, 0:512]

            def ph2b(k):
                ph2b_head(k)
                ph2b_tail(k)

            def ph2b_dxn(k):
                t_dxn[k] = E("vector", lambda v, k=k: v.tensor_scalar(
                    out=dxnbuf[:, (k % 2) * T:(k % 2 + 1) * T],
                    in0=xnfm[:, k * T:(k + 1) * T].bitcast(F32),
                    scalar1=dcol[:, k:k + 1], scalar2=None, op0=ALU.mult),
                    waits=[t_xn_ev, t_dcol,
                           t_yred_of.get((k - 2) % NK)])

            def emit_exp(k, s_):
                i = k * S + s_
                ia = i % 3
                wa = [t_deltaSTT[k], t_acol]
                prev_a_user = t_scan_hist.get(("a", ia))
                if prev_a_user is not None:
                    wa.append(prev_a_user)
                return E("scalar", lambda s, k=k, s_=s_, ia=ia:
                         s.activation(
                             out=apool[:, ia * T:(ia + 1) * T],
                             in_=delta[:, k * T:(k + 1) * T],
                             func=AF.Exp,
                             scale=acol[:, k * S + s_:k * S + s_ + 1]),
                         waits=wa)

            def emit_scans(k, a_toks):
                nonlocal t_scan_all_prev
                scan_toks[k] = [None] * S
                tb = None
                for s_ in range(S):
                    a_toks.append(emit_exp(k, s_))
                    i = k * S + s_
                    ia = i % 3
                    if s_ % 2 == 0:
                        ibp = (i // 2) % 2
                        wb = [t_u[k]] + R["ilB"]
                        prev_b_user = t_scan_hist.get(("b", ibp))
                        if prev_b_user is not None:
                            wb.append(prev_b_user)
                        usrc = ubuf[:, k * T:(k + 1) * T].unsqueeze(1)\
                            .broadcast_to([128, 2, T])
                        tb = E("vector", lambda v, k=k, s_=s_, ibp=ibp,
                               usrc=usrc: v.tensor_tensor(
                                   out=bpool[:, ibp * 2 * T:(ibp + 1) * 2 * T]
                                   .rearrange("p (a b) -> p a b", a=2),
                                   in0=usrc,
                                   in1=bcil[:, s_ * T:(s_ + 2) * T]
                                   .rearrange("p (a b) -> p a b", a=2),
                                   op=ALU.mult), waits=wb)
                        R["tb_last"] = tb
                    ibp = (i // 2) % 2
                    boff = ibp * 2 * T + (s_ % 2) * T
                    ws = [a_toks[s_], tb, t_state]
                    if s_ == 0:
                        ws.append(t_state_k[k - 1] if k > 0 else t_state_gather)
                    off = cur0 + k * S + s_
                    tsc = E("vector", lambda v, s_=s_, ia=ia, boff=boff,
                            off=off: v.tensor_tensor_scan(
                                out=hall[:, s_ * T:(s_ + 1) * T],
                                data0=apool[:, ia * T:(ia + 1) * T],
                                data1=bpool[:, boff:boff + T],
                                initial=statebuf[:, off:off + 1],
                                op0=ALU.mult, op1=ALU.add), waits=ws)
                    t_scan_hist[("a", ia)] = tsc
                    if s_ % 2 == 1:
                        t_scan_hist[("b", ibp)] = tsc
                    scan_toks[k][s_] = tsc
                    if s_ == S - 1:
                        scan_last_of_k[k] = tsc
                        t_scan_all_prev = tsc
                    if s_ == 1 and k + 1 < NK:
                        ph2b_dxn(k + 1)
                    if s_ == 8:
                        emit_sfm()
                    if s_ == 10 and R["pend_ev"] is not None:
                        cc2 = R["pend_ev"]
                        R["pend_ev"] = None
                        ph2a1_ev(cc2)

            def emit_ytail(k):
                nonlocal t_yp_reader_prev
                nxt_ok = (k + 1 < NK) and not (c > 0 and k + 1 <= 2)
                if nxt_ok:
                    ph2b_head(k + 1)
                t_state_k[k] = E("gpsimd", lambda g, k=k: g.tensor_copy(
                    out=statebuf[:, nxt0 + k * S:nxt0 + (k + 1) * S],
                    in_=hall[:].rearrange("p (s t2) -> p s t2", s=S)
                    [:, :, T - 1:T].squeeze(2)), waits=[scan_last_of_k[k]])
                t_yred = None
                for g in range(8):
                    ip = g % 2
                    eng_y = "gpsimd" if g == 7 else "vector"
                    wy = [scan_toks[k][2 * g + 1]] + R["ilC"]
                    prev_p_user = t_scan_hist.get(("p", ip))
                    if prev_p_user is not None:
                        wy.append(prev_p_user)
                    tym = E(eng_y, lambda e, g=g, ip=ip: e.tensor_tensor(
                        out=ppool[:, ip * 1024:(ip + 1) * 1024],
                        in0=hall[:, g * 1024:(g + 1) * 1024],
                        in1=bcil[:, (16 + 2 * g) * T:(16 + 2 * g + 2) * T],
                        op=ALU.mult), waits=wy)
                    if g == 7:
                        t_ymult_g7[k] = tym
                        R["tym_last_g"] = tym
                    else:
                        R["tym_last_vec"] = tym
                    tok = None
                    for half in range(2):
                        w = [tym, t_gbf]
                        if g == 0 and half == 0:
                            w.append(t_yp_reader_prev)
                        tok = E("tensor", lambda t, ip=ip, half=half, g=g:
                                t.matmul(
                                    out=ps[7][:],
                                    lhsT=identbf[:],
                                    rhs=ppool[:, ip * 1024 + half * 512:
                                              ip * 1024 + (half + 1) * 512],
                                    start=(g == 0 and half == 0),
                                    stop=False), waits=w)
                    t_scan_hist[("p", ip)] = tok
                    t_yred = tok
                t_yred = E("tensor", lambda t, k=k: t.matmul(
                    out=ps[7][:], lhsT=identbf[:],
                    rhs=dxnbuf[:, (k % 2) * T:(k % 2 + 1) * T],
                    start=False, stop=True), waits=[t_dxn[k], t_yred])
                t_yred_of[k] = t_yred
                pend_sfm[0] = k
                if nxt_ok:
                    ph2b_tail(k + 1)

            def emit_sfm():
                nonlocal t_yp_reader_prev
                k = pend_sfm[0]
                if k is None:
                    return
                pend_sfm[0] = None
                sl = k % 3
                t_sfm_tok[k] = E("vector", lambda v, k=k, sl=sl: v.tensor_copy(
                    out=sfm[:, sl * T:(sl + 1) * T], in_=ps[7][:]),
                    waits=[t_yred_of[k], t_sg_slot[sl]])
                t_yp_reader_prev = t_sfm_tok[k]

            def ph4a(m):
                rot = m % 2
                bank = ps[5 + rot]
                xre = X[c * T:(c + 1) * T, m * 128:(m + 1) * 128].rearrange(
                    "(j p) w -> p j w", p=128)
                t_xres_m[m] = E("sync", lambda s, rot=rot, xre=xre: s.dma_start(
                    out=xresm[:, rot * 512:(rot + 1) * 512].rearrange(
                        "p (j w) -> p j w", j=NJ), in_=xre),
                    dma="qxres", waits=[t_outblk_m[m - 2]])
                mm = None
                for k2 in range(NK):
                    w = [t_xq_ev, t_setup_w]
                    if k2 == 0:
                        w += [t_preg_roll[rot]]
                    mm = E("tensor", lambda t, m=m, k2=k2, bank=bank: t.matmul(
                        out=bank[:],
                        lhsT=wffn[:, k2 * 1024 + m * 128:k2 * 1024 + (m + 1) * 128],
                        rhs=xqfm[:, k2 * T:(k2 + 1) * T],
                        start=(k2 == 0), stop=(k2 == NK - 1)), waits=w)
                R["ffn_last"] = mm
                t_tpg_m[m] = E("vector", lambda v, m=m, rot=rot, bank=bank:
                               v.scalar_tensor_tensor(
                                   out=preg[:, rot * 512:(rot + 1) * 512],
                                   in0=bank[:], scalar=cscol[:, m:m + 1],
                                   in1=crepsb[:], op0=ALU.add, op1=ALU.mult),
                               waits=[mm, t_crepsb, t_cs[NK - 1],
                                      t_silu_roll[rot]])
                t_preg_roll[rot] = t_tpg_m[m]

            def ph4c(m):
                rot = m % 2
                tsl = E("scalar", lambda s, m=m, rot=rot: s.activation(
                    out=gbuf[:, rot * 512:(rot + 1) * 512],
                    in_=preg[:, rot * 512:(rot + 1) * 512], func=AF.Silu,
                    bias=biascol[:, m:m + 1]),
                    waits=[t_tpg_m[m], t_bias, t_gbuf_roll[rot]])
                t_silu_roll[rot] = tsl
                tsg = E("gpsimd", lambda g, m=m, rot=rot: g.tensor_tensor(
                    out=sgbuf[:, rot * 512:(rot + 1) * 512],
                    in0=sfm[:, (m % 3) * T:(m % 3 + 1) * T],
                    in1=gbuf[:, rot * 512:(rot + 1) * 512], op=ALU.mult),
                    waits=[tsl, t_sfm_tok[m], t_sgw_roll[rot]])
                t_gbuf_roll[rot] = tsg
                t_sg_slot[m % 3] = tsg
                last_blk = None
                for j in range(NJ):
                    tbank = ps[3 + (m * NJ + j) % 2]
                    ttr = E("tensor", lambda t, rot=rot, j=j, tbank=tbank:
                            t.transpose(
                                out=tbank[:, 0:128],
                                in_=sgbuf[:, rot * 512 + j * 128:
                                          rot * 512 + (j + 1) * 128],
                                identity=ident32[:]),
                            waits=[tsg, ps34_reader[(m * NJ + j) % 2]])
                    last_blk = E("vector", lambda v, rot=rot, j=j, tbank=tbank:
                                 v.tensor_tensor(
                                     out=outm[:, rot * 512 + j * 128:
                                              rot * 512 + (j + 1) * 128],
                                     in0=xresm[:, rot * 512 + j * 128:
                                               rot * 512 + (j + 1) * 128],
                                     in1=tbank[:, 0:128], op=ALU.add),
                                 waits=[ttr, t_xres_m[m],
                                        t_outdma_m[m - 2] if j == 0 else None])
                    ps34_reader[(m * NJ + j) % 2] = last_blk
                t_sgw_roll[rot] = last_blk
                t_outblk_m[m] = last_blk
                yap = Y[c * T:(c + 1) * T, m * 128:(m + 1) * 128].rearrange(
                    "(j p) w -> p j w", p=128)
                t_outdma_m[m] = E("sync", lambda s, rot=rot, yap=yap:
                                  s.dma_start(
                                      out=yap,
                                      in_=outm[:, rot * 512:(rot + 1) * 512]
                                      .rearrange("p (j w) -> p j w", j=NJ)),
                                  dma="qout", waits=[last_blk])

            if c == 0:
                ph2b(0)
            else:
                t_deltaSTT[0], t_u[0] = R["pre0"]
                t_deltaSTT[1], t_u[1] = R["pre1"]
                t_deltaSTT[2], t_u[2] = R["pre2"]
            ph2b_dxn(0)
            for k in range(NK):
                if k >= 3 and k % 2 == 1:
                    ph4c(k - 3)
                    ph4c(k - 2)
                a_toks = []
                emit_scans(k, a_toks)
                emit_ytail(k)
                if c + 1 < NCH:
                    if k == 4:
                        R["tcc"] = ph1a(c + 1)
                    elif k == 6:
                        ph1b_xn(c + 1, [t_u[NK - 1], t_dxn[NK - 1]])
                        ph2a1_mm(c + 1)
                        R["pend_ev"] = c + 1
                    elif k == 7:
                        ph2a1_ilB(c + 1)
                        ph2b(0)
                        R["pre0"] = (t_deltaSTT[0], t_u[0])
                if k >= 1:
                    ph4a(k - 1)
            emit_sfm()
            ph4a(NK - 1)
            ph4c(NK - 2)
            ph4c(NK - 1)
            if c + 1 < NCH:
                ph2a2(c + 1, R["tcc"])
                ph2a1_ilC(c + 1)
                ph2b(1)
                ph2b(2)
                R["pre1"] = (t_deltaSTT[1], t_u[1])
                R["pre2"] = (t_deltaSTT[2], t_u[2])
                ph1b_xq(c + 1, [R["ffn_last"]])
            t_state_gather = t_state_k[NK - 1]
            t_ymult_last_prev_chunk = t_ymult_g7[NK - 1]
            t_outblk_m[-2] = t_outblk_m[NK - 2]
            t_outblk_m[-1] = t_outblk_m[NK - 1]
            t_outdma_m[-2] = t_outdma_m[NK - 2]
            t_outdma_m[-1] = t_outdma_m[NK - 1]

        E("sync", lambda s: s.nop(),
          waits=[t_outdma_m[NK - 2], t_outdma_m[NK - 1]])

        B.resolve()
        for name, meth in (("sync", block.sync), ("vector", block.vector),
                           ("scalar", block.scalar), ("tensor", block.tensor),
                           ("gpsimd", block.gpsimd)):
            meth(lambda eng, name=name: B.run(eng, name, sems))

    return nc


_NC_CACHE = None


def kernel(L0_attn_x, L0_attn_W_deltaBC, L0_attn_W_dt, L0_attn_b_dt,
           L0_attn_A_log, L0_attn_D_param, L0_ffn_w_int8, L0_ffn_weight_scale,
           L0_ffn_bias):
    global _NC_CACHE
    if _NC_CACHE is None:
        _NC_CACHE = build_program()
    nc = _NC_CACHE

    x = np.ascontiguousarray(np.asarray(L0_attn_x, dtype=np.float32))
    base = {
        "w_dbc": np.ascontiguousarray(np.asarray(L0_attn_W_deltaBC, np.float32)),
        "w_dt": np.ascontiguousarray(np.asarray(L0_attn_W_dt, np.float32)),
        "b_dt": np.ascontiguousarray(np.asarray(L0_attn_b_dt, np.float32)),
        "a_log": np.ascontiguousarray(np.asarray(L0_attn_A_log, np.float32)),
        "d_par": np.ascontiguousarray(np.asarray(L0_attn_D_param, np.float32)),
        "w_int": np.ascontiguousarray(np.asarray(L0_ffn_w_int8, np.int32)),
        "w_sc": np.asarray(L0_ffn_weight_scale, np.float32).reshape(1),
        "f_bias": np.ascontiguousarray(np.asarray(L0_ffn_bias, np.float32)),
    }
    in_maps = []
    for b in range(NCORE):
        m = dict(base)
        m["x"] = np.ascontiguousarray(x[b])
        in_maps.append(m)
    res = run_bass_kernel_spmd(nc, in_maps, core_ids=list(range(NCORE)))
    out = np.stack([np.asarray(r["y"], np.float32) for r in res.results], axis=0)
    return out


# version 27
# speedup vs baseline: 1.0642x; 1.0424x over previous
import numpy as np
import concourse.bass as bass
import concourse.mybir as mybir
from concourse.bass_utils import run_bass_kernel_spmd

F32 = mybir.dt.float32
F32R = mybir.dt.float32r
F16 = mybir.dt.float16
BF16 = mybir.dt.bfloat16
I32 = mybir.dt.int32
AF = mybir.ActivationFunctionType
ALU = mybir.AluOpType
AX = mybir.AxisListType

L = 4096
D = 1024
S = 16
T = 512
NCH = L // T
NJ = T // 128
NK = D // 128
NCORE = 8

ENGINES = ("sync", "vector", "scalar", "tensor", "gpsimd")
STREAMS = ("qxin", "qxres", "qout", "qcrow", "qbcw", "qw",
           "qilb_s", "qilb_g", "qilc_sc", "qilc_s2")
UNORDERED = {"qilb_s", "qilb_g", "qilc_sc", "qilc_s2"}


class _Tok:
    __slots__ = ("eng", "idx", "marked", "count_after")

    def __init__(self, eng, idx):
        self.eng = eng
        self.idx = idx
        self.marked = False
        self.count_after = None


class _Builder:

    def __init__(self):
        self.ops = {e: [] for e in ENGINES}
        self.stream_n = {q: 0 for q in STREAMS}

    def emit(self, eng, fn, waits=(), dma=None):
        tok = _Tok(eng, len(self.ops[eng]))
        wl = {}
        for w in waits:
            if w is None:
                continue
            k = w.eng
            if k not in wl or w.idx > wl[k].idx:
                wl[k] = w
        for w in wl.values():
            w.marked = True
        prewait = None
        if dma is not None:
            tok.eng = dma
            tok.idx = self.stream_n[dma]
            tok.marked = True
            tok.count_after = 16 * (self.stream_n[dma] + 1)
            if self.stream_n[dma] > 0 and dma not in UNORDERED:
                prewait = 16 * self.stream_n[dma]
            self.stream_n[dma] += 1
        self.ops[eng].append({"fn": fn, "waits": list(wl.values()), "dma": dma,
                              "tok": tok, "prewait": prewait})
        return tok

    def resolve(self):
        for e in ENGINES:
            c = 0
            for op in self.ops[e]:
                if op["dma"] is not None:
                    continue
                if op["tok"].marked:
                    c += 1
                op["tok"].count_after = c

    def run(self, eng_obj, name, sems):
        wm = {}

        def w(key, val):
            if wm.get(key, -1) >= val:
                return
            eng_obj.wait_ge(sems[key], val)
            wm[key] = val

        for op in self.ops[name]:
            if op["prewait"] is not None:
                w(op["dma"], op["prewait"])
            for wt in op["waits"]:
                w(wt.eng, wt.count_after)
            inst = op["fn"](eng_obj)
            if op["dma"] is not None:
                inst.then_inc(sems[op["dma"]], 16)
            elif op["tok"].marked:
                inst.then_inc(sems[op["tok"].eng], 1)


def build_program():
    nc = bass.Bass("TRN2", target_bir_lowering=False)
    X = nc.dram_tensor("x", [L, D], F32, kind="ExternalInput").ap()
    WDBC = nc.dram_tensor("w_dbc", [96, D], F32, kind="ExternalInput").ap()
    WDT = nc.dram_tensor("w_dt", [D, 64], F32, kind="ExternalInput").ap()
    BDT = nc.dram_tensor("b_dt", [D], F32, kind="ExternalInput").ap()
    ALOG = nc.dram_tensor("a_log", [D, S], F32, kind="ExternalInput").ap()
    DPAR = nc.dram_tensor("d_par", [D], F32, kind="ExternalInput").ap()
    WINT = nc.dram_tensor("w_int", [D, D], I32, kind="ExternalInput").ap()
    WSC = nc.dram_tensor("w_sc", [1], F32, kind="ExternalInput").ap()
    FBIAS = nc.dram_tensor("f_bias", [D], F32, kind="ExternalInput").ap()
    Y = nc.dram_tensor("y", [L, D], F32, kind="ExternalOutput").ap()
    BCSCR = nc.dram_tensor("bcscr", [32, T], BF16, kind="Internal").ap()

    from contextlib import ExitStack
    with ExitStack() as ctx:
        def sb(name, shape, dt):
            return ctx.enter_context(nc.sbuf_tensor(name, shape, dt))

        xin = sb("xin", [128, 4096], F32)
        xntm = sb("xntm", [128, 2048], F32)
        scr = sb("scr", [128, 2048], F32)
        spbuf = sb("spbuf", [128, 1024], F32)
        preg = sb("preg", [128, 1024], F32)
        sgbuf = sb("sgbuf", [128, 1024], F32)
        gbuf = sb("gbuf", [128, 1024], F32)
        sfm = sb("sfm", [128, 1536], F32)
        xnfm = sb("xnfm", [128, 4096], F32R)
        xqfm = sb("xqfm", [128, 4096], F16)
        delta = sb("delta", [128, 4096], BF16)
        ubuf = sb("ubuf", [128, 4096], BF16)
        drones = sb("drones", [65, T], F32R)
        btctbf = sb("btctbf", [32, T], BF16)
        bcil = sb("bcil", [128, 32 * T], BF16)
        apool = sb("apool", [128, 3 * T], F32)
        bpool = sb("bpool", [128, 4 * T], BF16)
        dxnbuf = sb("dxnbuf", [128, 2 * T], BF16)
        hall = sb("hall", [128, 16 * T], BF16)
        ppool = sb("ppool", [128, 2 * 1024], BF16)
        wffn = sb("wffn", [128, 8192], F16)
        wdbc = sb("wdbc", [128, 768], F32R)
        wdt = sb("wdt", [65, 1024], F32R)
        acol = sb("acol", [128, 128], F32)
        dcol = sb("dcol", [128, 8], F32)
        biascol = sb("biascol", [128, 8], F32)
        cscol = sb("cscol", [128, 8], F32)
        ccols = sb("ccols", [128, 4], F32)
        wsrep = sb("wsrep", [128, 1], F32)
        bnbuf = sb("bnbuf", [128, 48], F32)
        mv = sb("mv", [128, 8], F32)
        cells = sb("cells", [128, 32], F32)
        eps5 = sb("eps5", [128, 1], F32)
        eps6 = sb("eps6", [128, 1], F32)
        onecol = sb("onecol", [128, 1], F32)
        ident32 = sb("ident32", [128, 128], F32)
        ident16 = sb("ident16", [128, 128], F16)
        identbf = sb("identbf", [128, 128], BF16)
        onesrow = sb("onesrow", [1, 128], F32R)
        onescol16 = sb("onescol16", [128, 1], F16)
        crow = sb("crow", [1, T], F32R)
        crepsb = sb("crepsb", [128, T], F32)
        statebuf = sb("statebuf", [128, 256], F32)
        xresm = sb("xresm", [128, 1024], F32)
        outm = sb("outm", [128, 1024], F32)

        ps = [ctx.enter_context(nc.psum_tensor(f"ps{i}", [128, 512], F32))
              for i in range(8)]

        sems = {}
        for e in ENGINES + STREAMS:
            sems[e] = ctx.enter_context(nc.semaphore(f"sem_{e}"))
        block = ctx.enter_context(nc.Block())

        B = _Builder()
        E = B.emit

        def cell(j, i):
            return cells[:, j * 8 + i:j * 8 + i + 1]

        reg_i32 = xin[:, 0:1024].bitcast(I32)
        reg_f32 = xin[:, 1024:2048]
        reg_dbc = xin[0:96, 2048:3072]
        reg_wdt = xin[:, 3072:3584]
        reg_alog = xin[:, 3584:3712]

        t_gset = E("gpsimd", lambda g: g.memset(ident32[:], 1.0))
        t_gset = E("gpsimd", lambda g: g.affine_select(
            out=ident32[:], in_=ident32[:], pattern=[[1, 128]],
            compare_op=ALU.is_equal, fill=0.0, base=0, channel_multiplier=-1),
            waits=[t_gset])
        t_g16 = E("gpsimd", lambda g: g.memset(ident16[:], 1.0))
        t_g16 = E("gpsimd", lambda g: g.affine_select(
            out=ident16[:], in_=ident16[:], pattern=[[1, 128]],
            compare_op=ALU.is_equal, fill=0.0, base=0, channel_multiplier=-1),
            waits=[t_g16])
        t_gbf = E("gpsimd", lambda g: g.memset(identbf[:], 1.0))
        t_gbf = E("gpsimd", lambda g: g.affine_select(
            out=identbf[:], in_=identbf[:], pattern=[[1, 128]],
            compare_op=ALU.is_equal, fill=0.0, base=0, channel_multiplier=-1),
            waits=[t_gbf])
        t_eps5 = E("gpsimd", lambda g: g.memset(eps5[:], 1e-5))
        t_eps6 = E("gpsimd", lambda g: g.memset(eps6[:], 1e-6))
        t_one = E("gpsimd", lambda g: g.memset(onecol[:], 1.0))
        t_orow0 = E("gpsimd", lambda g: g.memset(scr[0:1, 0:640], 1.0))
        t_orow = E("vector", lambda v: v.tensor_copy(
            out=onesrow[:], in_=scr[0:1, 0:128]), waits=[t_orow0])
        t_oc16 = E("gpsimd", lambda g: g.memset(onescol16[:], 1.0))
        t_drow = E("vector", lambda v: v.tensor_copy(
            out=drones[64:65, :], in_=scr[0:1, 128:640]), waits=[t_orow0])
        t_state = E("gpsimd", lambda g: g.memset(statebuf[:], 0.0))

        t_dcol = E("sync", lambda s: s.dma_start(
            out=dcol[:], in_=DPAR.rearrange("(k p) -> p k", p=128),
            allow_slow_non_contiguous=True), dma="qw")
        t_bias = E("sync", lambda s: s.dma_start(
            out=biascol[:], in_=FBIAS.rearrange("(k p) -> p k", p=128),
            allow_slow_non_contiguous=True), dma="qw")
        t_bdt = E("sync", lambda s: s.dma_start(
            out=wdt[64:65, :], in_=BDT.rearrange("(o d) -> o d", o=1).bitcast(F32R)), dma="qw")
        t_aload = E("sync", lambda s: s.dma_start(
            out=reg_alog.rearrange("p (k s) -> p k s", k=NK),
            in_=ALOG.rearrange("(k p) s -> p k s", p=128)), dma="qw")
        t_dbcl = E("sync", lambda s: s.dma_start(out=reg_dbc, in_=WDBC), dma="qw")
        t_wdtl = E("sync", lambda s: s.dma_start(
            out=reg_wdt.rearrange("p (k r) -> p k r", k=NK),
            in_=WDT.rearrange("(k p) r -> p k r", p=128)), dma="qw")

        t_wsrep_dma = E("sync", lambda s: s.dma_start(
            out=wsrep[:], in_=WSC.rearrange("(o d) -> o d", o=1)
            .broadcast_to([128, 1])), dma="qw")
        t_wsrep = t_wsrep_dma

        t_ae = E("scalar", lambda s: s.activation(
            out=scr[:, 0:128], in_=reg_alog, func=AF.Exp),
            waits=[t_aload, t_orow])
        t_acol = E("vector", lambda v: v.tensor_scalar(
            out=acol[:], in0=scr[:, 0:128], scalar1=-1.0, scalar2=None,
            op0=ALU.mult), waits=[t_ae])

        t_dbc_ev = [None] * NK
        for k in range(NK):
            bank = ps[k % 2]
            war = t_dbc_ev[k - 2] if k >= 2 else None
            t_tr = E("tensor", lambda t, k=k, bank=bank: t.transpose(
                out=bank[:, 0:96], in_=reg_dbc[:, k * 128:(k + 1) * 128],
                identity=ident32[0:96, 0:96]),
                waits=[t_dbcl, t_gset, war])
            t_dbc_ev[k] = E("vector", lambda v, k=k, bank=bank: v.tensor_copy(
                out=wdbc[:, k * 96:(k + 1) * 96], in_=bank[:, 0:96]), waits=[t_tr])

        t_wdt_ev = [None] * NK
        for k in range(NK):
            bank = ps[2 + k % 2]
            war = t_wdt_ev[k - 2] if k >= 2 else None
            t_tr = E("tensor", lambda t, k=k, bank=bank: t.transpose(
                out=bank[0:64, 0:128], in_=reg_wdt[:, k * 64:(k + 1) * 64],
                identity=ident32[:]),
                waits=[t_wdtl, t_gset, war])
            t_wdt_ev[k] = E("vector", lambda v, k=k, bank=bank: v.tensor_copy(
                out=wdt[0:64, k * 128:(k + 1) * 128], in_=bank[0:64, 0:128]),
                waits=[t_tr])

        t_wload = None
        t_wcopy = None
        t_wev_last = None
        t_wtr_last = [None, None]
        for m in range(NK):
            t_wload = E("sync", lambda s, m=m: s.dma_start(
                out=reg_i32, in_=WINT[m * 128:(m + 1) * 128, :]),
                dma="qw", waits=[t_wcopy])
            t_wcopy = E("vector", lambda v: v.tensor_copy(out=reg_f32, in_=reg_i32),
                        waits=[t_wload, t_wtr_last[1] or t_wtr_last[0]])
            for half in range(2):
                bank = ps[4 + half]
                trs = []
                for kk in range(4):
                    k = half * 4 + kk
                    trs.append(E("tensor", lambda t, k=k, kk=kk, bank=bank: t.transpose(
                        out=bank[:, kk * 128:(kk + 1) * 128],
                        in_=reg_f32[:, k * 128:(k + 1) * 128],
                        identity=ident32[:]),
                        waits=[t_wcopy, t_wev_last, t_gset]))
                t_wtr_last[half] = trs[-1]
                dst = wffn[:].rearrange("p (k o) -> p k o", k=NK)[
                    :, half * 4:(half + 1) * 4, m * 128:(m + 1) * 128]
                t_wev_last = E("vector", lambda v, bank=bank, dst=dst: v.tensor_copy(
                    out=dst, in_=bank[:].rearrange("p (a b) -> p a b", a=4)),
                    waits=[trs[-1]])

        t_cs = [None] * NK
        for m in range(NK):
            mm_last = None
            for k in range(NK):
                mm_last = E("tensor", lambda t, m=m, k=k: t.matmul(
                    out=ps[6][:, 0:1],
                    lhsT=wffn[:, k * 1024 + m * 128:k * 1024 + (m + 1) * 128],
                    rhs=onescol16[:], start=(k == 0), stop=(k == NK - 1)),
                    waits=[t_wev_last, t_oc16, t_cs[m - 1] if m else None])
            t_cs[m] = E("vector", lambda v, m=m: v.tensor_scalar(
                out=cscol[:, m:m + 1], in0=ps[6][:, 0:1], scalar1=-1536.0,
                scalar2=None, op0=ALU.mult), waits=[mm_last])

        t_setup_w = t_wev_last
        t_setup_x = t_cs[NK - 1]
        t_setup_pe = t_wtr_last[1]
        t_scr_set = t_acol
        t_evac_bank_init = [t_dbc_ev[NK - 2], t_dbc_ev[NK - 1]]

        t_lnts = [[None] * NJ for _ in range(NCH + 1)]
        t_xin_dma = [None] * NJ
        t_xn_ev = None
        t_xq_ev = None
        t_state_gather = None
        t_sfm_roll = [t_wsrep, t_wsrep, t_wsrep]
        t_sg_slot = [None, None, None]
        t_preg_roll = [t_wev_last, t_cs[NK - 1]]
        t_silu_roll = [None, None]
        t_sgw_roll = [None, None]
        t_gbuf_roll = [None, None]
        ps34_reader = [t_wev_last, t_wev_last]
        t_sp_roll = [None, None]
        t_xqslot = [None] * NJ
        t_scrslot = [None, None]
        t_evac_bank_roll = list(t_evac_bank_init)
        t_crow_prev = [None] * NJ
        t_quant_j = [None] * NJ
        t_ln_tok = [None] * NJ
        t_cc_next = None
        t_dxn = [None] * NK
        t_yred_of = {}
        R = {"dr": None, "bt": None, "bcw": None, "crepsb": None,
             "dmm_last": None, "ffn_last": None, "tcc": None,
             "il_last": [], "ilB": [], "ilC": [],
             "tb_last": None, "tym_last_vec": None, "tym_last_g": None,
             "bcmm": None, "pend_ev": None, "pre0": None}
        t_outblk_m = [None] * (NK + 2)
        t_outdma_m = [None] * (NK + 2)
        t_xres_m = [None] * (NK + 2)
        t_scan_hist = {}
        t_scan_all_prev = None
        t_ymult_last_prev_chunk = None
        t_dbc_done_prev = None
        t_crepsb = None
        t_yp_reader_prev = t_wsrep

        for c in range(NCH):
            par = c % 2
            cur0 = par * 128
            nxt0 = (1 - par) * 128

            def ph1a(cc):
                for j in range(NJ):
                    w = ([t_lnts[cc - 1][j], t_quant_j[j]] if cc > 0 else
                         [t_setup_pe, t_setup_x, t_scr_set, t_ae])
                    t_xin_dma[j] = E("sync", lambda s, cc=cc, j=j: s.dma_start(
                        out=xin[:, j * 1024:(j + 1) * 1024],
                        in_=X[cc * T + j * 128:cc * T + (j + 1) * 128, :]),
                        dma="qxin", waits=w)
                tag = None
                for j in range(NJ):
                    E("vector", lambda v, j=j: v.bn_stats(
                        out=bnbuf[:, j * 12:j * 12 + 6],
                        in_=xin[:, j * 1024:j * 1024 + 512]), waits=[t_xin_dma[j]])
                    tb2 = E("vector", lambda v, j=j: v.bn_stats(
                        out=bnbuf[:, j * 12 + 6:j * 12 + 12],
                        in_=xin[:, j * 1024 + 512:j * 1024 + 1024]),
                        waits=[t_xin_dma[j]])
                    tag = E("vector", lambda v, j=j: v.bn_aggr(
                        out=mv[:, j * 2:j * 2 + 2],
                        in_=bnbuf[:, j * 12:j * 12 + 12]), waits=[tb2])
                mv3 = mv[:].rearrange("p (j two) -> p j two", two=2)
                tcc_l = [None, None]
                for pair in range(2):
                    j0 = pair * 2
                    vars2 = mv3[:, j0:j0 + 2, 1:2].squeeze(2)
                    means2 = mv3[:, j0:j0 + 2, 0:1].squeeze(2)
                    cl = lambda a, b, pair=pair: cells[:, pair * 16 + a:
                                                       pair * 16 + b]
                    tl2 = E("scalar", lambda s, vars2=vars2, cl=cl: s.activation(
                        out=cl(0, 2), in_=vars2, func=AF.Ln, bias=eps5[:, 0:1]),
                        waits=[tag, t_eps5])
                    tr2_ = E("scalar", lambda s, cl=cl: s.activation(
                        out=cl(2, 4), in_=cl(0, 2), func=AF.Exp, scale=-0.5),
                        waits=[tl2])
                    tn2 = E("vector", lambda v, means2=means2, cl=cl:
                            v.scalar_tensor_tensor(
                                out=cl(4, 6), in0=means2, scalar=-1.0,
                                in1=cl(2, 4), op0=ALU.mult, op1=ALU.mult),
                            waits=[tr2_])
                    t_am_l = [None, None]
                    for jj in range(2):
                        j = j0 + jj
                        jr = j % 2
                        t_ln = E("vector", lambda v, j=j, jj=jj, cl=cl:
                                 v.tensor_scalar(
                                     out=xin[:, j * 1024:(j + 1) * 1024],
                                     in0=xin[:, j * 1024:(j + 1) * 1024],
                                     scalar1=cl(2 + jj, 3 + jj),
                                     scalar2=cl(4 + jj, 5 + jj),
                                     op0=ALU.mult, op1=ALU.add),
                                 waits=[tn2])
                        t_ln_tok[j] = t_ln
                        t_am_l[jj] = E("vector", lambda v, j=j, jj=jj, cl=cl:
                                       v.reduce_max(
                                           out=cl(6 + jj, 7 + jj),
                                           in_=xin[:, j * 1024:(j + 1) * 1024],
                                           axis=AX.X,
                                           apply_absolute_value=True),
                                       waits=[t_ln])
                    tr22 = E("vector", lambda v, cl=cl: v.tensor_mul(
                        out=cl(8, 10), in0=cl(2, 4), in1=cl(2, 4)), waits=[tn2])
                    tms = E("vector", lambda v, vars2=vars2, cl=cl:
                            v.tensor_tensor(
                                out=cl(0, 2), in0=vars2, in1=cl(8, 10),
                                op=ALU.mult), waits=[tr22, tl2])
                    tlm = E("scalar", lambda s, cl=cl: s.activation(
                        out=cl(0, 2), in_=cl(0, 2), func=AF.Ln,
                        bias=eps6[:, 0:1]), waits=[tms, t_eps6])
                    trq = E("scalar", lambda s, cl=cl: s.activation(
                        out=cl(10, 12), in_=cl(0, 2), func=AF.Exp, scale=-0.5),
                        waits=[tlm])
                    tamn = E("vector", lambda v, cl=cl: v.tensor_mul(
                        out=cl(12, 14), in0=cl(6, 8), in1=cl(10, 12)),
                        waits=[trq, t_am_l[1]])
                    tamc = E("vector", lambda v, cl=cl: v.tensor_scalar(
                        out=cl(12, 14), in0=cl(12, 14), scalar1=1e-5,
                        scalar2=None, op0=ALU.max), waits=[tamn])
                    trec = E("vector", lambda v, cl=cl: v.reciprocal(
                        out=cl(0, 2), in_=cl(12, 14)), waits=[tamc])
                    tqc2 = E("vector", lambda v, cl=cl: v.scalar_tensor_tensor(
                        out=cl(14, 16), in0=cl(0, 2), scalar=127.0,
                        in1=cl(10, 12), op0=ALU.mult, op1=ALU.mult),
                        waits=[trec])
                    tcc_l[pair] = E("vector", lambda v, j0=j0, cl=cl:
                                    v.scalar_tensor_tensor(
                                        out=ccols[:, j0:j0 + 2], in0=cl(12, 14),
                                        scalar=1.0 / 127.0,
                                        in1=wsrep[:, 0:1].broadcast_to([128, 2]),
                                        op0=ALU.mult, op1=ALU.mult),
                                    waits=[tqc2, t_wsrep, t_crow_prev[j0],
                                           t_crow_prev[j0 + 1]])
                    for jj in range(2):
                        j = j0 + jj
                        jr = j % 2
                        tq1 = E("vector", lambda v, j=j, jr=jr, jj=jj, cl=cl:
                                v.tensor_scalar(
                                    out=scr[:, jr * 1024:(jr + 1) * 1024],
                                    in0=xin[:, j * 1024:(j + 1) * 1024],
                                    scalar1=cl(14 + jj, 15 + jj),
                                    scalar2=1536.0,
                                    op0=ALU.mult, op1=ALU.add),
                                waits=[tqc2])
                        t_quant_j[j] = E("vector", lambda v, j=j, jr=jr:
                                         v.tensor_scalar(
                                             out=xntm[:].bitcast(F16)[
                                                 :, j * 1024:(j + 1) * 1024],
                                             in0=scr[:, jr * 1024:
                                                     (jr + 1) * 1024],
                                             scalar1=1408.0, scalar2=1663.0,
                                             op0=ALU.max, op1=ALU.min),
                                        waits=[tq1, t_xqslot[j]])
                return tcc_l

            def ph1b_xn(cc, extra_w):
                nonlocal t_xn_ev
                for j in range(NJ):
                    jr = j % 2
                    t_ln = t_ln_tok[j]
                    for half in range(2):
                        bank = ps[half]
                        trs_last = None
                        for kk in range(4):
                            k = half * 4 + kk
                            trs_last = E("tensor", lambda t, j=j, k=k, kk=kk,
                                         bank=bank: t.transpose(
                                             out=bank[:, kk * 128:
                                                      (kk + 1) * 128],
                                             in_=xin[:, j * 1024 + k * 128:
                                                     j * 1024 + (k + 1) * 128],
                                             identity=ident32[:]),
                                         waits=[t_ln, t_evac_bank_roll[half],
                                                t_gset])
                        dst = xnfm[:].rearrange("p (k t2) -> p k t2", k=NK)[
                            :, half * 4:(half + 1) * 4, j * 128:(j + 1) * 128]
                        t_evac_bank_roll[half] = E(
                            "scalar", lambda sc, bank=bank, dst=dst: sc.copy(
                                out=dst,
                                in_=bank[:].rearrange("p (a b) -> p a b", a=4)),
                            waits=[trs_last] + extra_w)
                        t_xn_ev = t_evac_bank_roll[half]
                        t_lnts[cc][j] = trs_last

            def ph1b_xq(cc, extra_w):
                nonlocal t_xq_ev
                for j in range(NJ):
                    tq16 = t_quant_j[j]
                    xq_trs = None
                    for half in range(2):
                        bank = ps[half]
                        trs_last = None
                        for kk in range(4):
                            k = half * 4 + kk
                            src = xntm[:].bitcast(F16)[
                                :, j * 1024 + k * 128:j * 1024 + (k + 1) * 128]
                            trs_last = E("tensor", lambda t, src=src, kk=kk,
                                         bank=bank: t.transpose(
                                             out=bank[:].bitcast(F16)[
                                                 :, kk * 128:(kk + 1) * 128],
                                             in_=src, identity=ident16[:]),
                                         waits=[tq16, t_evac_bank_roll[half],
                                                t_g16])
                        xq_trs = trs_last
                        dst = xqfm[:].rearrange("p (k t2) -> p k t2", k=NK)[
                            :, half * 4:(half + 1) * 4, j * 128:(j + 1) * 128]
                        t_evac_bank_roll[half] = E(
                            "scalar", lambda sc, bank=bank, dst=dst: sc.copy(
                                out=dst,
                                in_=bank[:].bitcast(F16)[:, 0:512].rearrange(
                                    "p (a b) -> p a b", a=4)),
                            waits=[trs_last] + extra_w)
                        t_xq_ev = t_evac_bank_roll[half]
                    t_xqslot[j] = xq_trs

            def ph2a1_mm(cc):
                mm_last = None
                for k in range(NK):
                    w = [t_xn_ev, t_dbc_ev[NK - 1]]
                    if k == 0:
                        w += [R["crepsb"] if cc > 0 else t_wdt_ev[NK - 2]]
                    mm_last = E("tensor", lambda t, k=k: t.matmul(
                        out=ps[2][0:96, :], lhsT=wdbc[:, k * 96:(k + 1) * 96],
                        rhs=xnfm[:, k * T:(k + 1) * T],
                        start=(k == 0), stop=(k == NK - 1)), waits=w)
                R["bcmm"] = mm_last

            def ph2a1_ev(cc):
                mm_last = R["bcmm"]
                R["dr"] = E("vector", lambda v: v.tensor_copy(
                    out=drones[0:64, :], in_=ps[2][0:64, :]),
                    waits=[mm_last, t_drow, R["dmm_last"]])
                R["bt"] = E("vector", lambda v: v.tensor_copy(
                    out=btctbf[0:32, :], in_=ps[2][64:96, :]),
                    waits=[mm_last, R["bcw"]])
                R["bcw"] = E("sync", lambda sy: sy.dma_start(
                    out=BCSCR, in_=btctbf[0:32, :]),
                    dma="qbcw", waits=[R["bt"]] + R["il_last"])

            def ph2a1_ilB(cc):
                wB = [R["bcw"]] + ([R["tb_last"]] if cc > 0 else [])
                ilB = [None, None]
                for i in range(8):
                    for half in range(2):
                        r = 2 * i + half
                        qe, qs = (("sync", "qilb_s") if half == 0 else
                                  ("gpsimd", "qilb_g"))
                        ilB[half] = E(qe, lambda q, r=r: q.dma_start(
                            out=bcil[:, r * T:(r + 1) * T],
                            in_=BCSCR[r:r + 1, :].broadcast_to([128, T])),
                            dma=qs, waits=wB)
                R["ilB"] = ilB

            def ph2a1_ilC(cc):
                wC = [R["bcw"]] + ([R["tym_last_vec"], R["tym_last_g"]]
                                   if cc > 0 else [])
                ilC = [None, None]
                for i in range(8):
                    for half in range(2):
                        rc = 16 + 2 * i + half
                        qe, qs = (("scalar", "qilc_sc") if half == 0 else
                                  ("sync", "qilc_s2"))
                        ilC[half] = E(qe, lambda q, rc=rc: q.dma_start(
                            out=bcil[:, rc * T:(rc + 1) * T],
                            in_=BCSCR[rc:rc + 1, :].broadcast_to([128, T])),
                            dma=qs, waits=wC)
                R["ilC"] = ilC
                R["il_last"] = R["ilB"] + ilC

            def ph2a2(cc, tcc_pair):
                t_crow = [None] * NJ
                for j in range(NJ):
                    t_crow[j] = E("sync", lambda s, j=j: s.dma_start(
                        out=crow[0:1, j * 128:(j + 1) * 128],
                        in_=ccols[:, j:j + 1].bitcast(F32R),
                        allow_slow_non_contiguous=True),
                        dma="qcrow",
                        waits=[tcc_pair[0] if j < 2 else tcc_pair[1]])
                    t_crow_prev[j] = t_crow[j]
                t_crep = E("tensor", lambda t: t.matmul(
                    out=ps[2][:], lhsT=onesrow[0:1, :], rhs=crow[0:1, :],
                    start=True, stop=True),
                    waits=[t_crow[NJ - 1], R["bt"], R["dr"], t_orow])
                R["crepsb"] = E("vector", lambda v: v.tensor_copy(
                    out=crepsb[:], in_=ps[2][:]),
                    waits=[t_crep, t_preg_roll[0], t_preg_roll[1]])

            if c == 0:
                tcc_pair0 = ph1a(0)
                ph1b_xn(0, [])
                ph1b_xq(0, [])
                ph2a1_mm(0)
                ph2a1_ev(0)
                ph2a1_ilB(0)
                ph2a1_ilC(0)
                R["tcc"] = tcc_pair0
                ph2a2(0, R["tcc"])
            t_crepsb = R["crepsb"]
            t_dr = R["dr"]

            t_ta3 = [None] * NK
            t_deltaSTT = [None] * NK
            t_u = [None] * NK
            t_state_k = [None] * NK
            t_sfm_tok = [None] * NK
            t_tpg_m = [None] * NK
            t_ymult_g7 = [None] * NK
            scan_last_of_k = [None] * NK
            scan_toks = [None] * NK
            pend_sfm = [None]

            def ph2b_head(k):
                bank = ps[3 + k % 2]
                t_dmm = E("tensor", lambda t, k=k, bank=bank: t.matmul(
                    out=bank[:], lhsT=wdt[0:65, k * 128:(k + 1) * 128],
                    rhs=drones[0:65, :], start=True, stop=True),
                    waits=[R["dr"], t_wdt_ev[NK - 1], t_bdt,
                           ps34_reader[k % 2]])
                R["dmm_last"] = t_dmm
                sp0 = spbuf[:, 0:512]
                sp1 = spbuf[:, 512:1024]
                ta1 = E("scalar", lambda s, bank=bank: s.activation(
                    out=sp0, in_=bank[:], func=AF.Abs),
                    waits=[t_dmm, t_sp_roll[0]])
                ta2 = E("scalar", lambda s: s.activation(
                    out=sp1, in_=sp0, func=AF.Exp, scale=-1.0),
                    waits=[ta1, t_sp_roll[1]])
                ta3 = E("scalar", lambda s: s.activation(
                    out=sp0, in_=sp1, func=AF.Ln, bias=onecol[:, 0:1]),
                    waits=[ta2, t_one])
                t_sp_roll[1] = ta3
                t_ta3[k] = ta3

            def ph2b_tail(k):
                bank = ps[3 + k % 2]
                t_deltaSTT[k] = E("vector", lambda v, k=k, bank=bank:
                                  v.scalar_tensor_tensor(
                                      out=delta[:, k * T:(k + 1) * T],
                                      in0=bank[:], scalar=0.0, in1=sp0_ap(),
                                      op0=ALU.max, op1=ALU.add),
                                  waits=[t_ta3[k]])
                t_sp_roll[0] = t_deltaSTT[k]
                ps34_reader[k % 2] = t_deltaSTT[k]
                t_u[k] = E("gpsimd", lambda g, k=k: g.tensor_tensor(
                    out=ubuf[:, k * T:(k + 1) * T],
                    in0=delta[:, k * T:(k + 1) * T],
                    in1=xnfm[:, k * T:(k + 1) * T].bitcast(F32),
                    op=ALU.mult), waits=[t_deltaSTT[k]])

            def sp0_ap():
                return spbuf[:, 0:512]

            def ph2b(k):
                ph2b_head(k)
                ph2b_tail(k)

            def ph2b_dxn(k):
                t_dxn[k] = E("vector", lambda v, k=k: v.tensor_scalar(
                    out=dxnbuf[:, (k % 2) * T:(k % 2 + 1) * T],
                    in0=xnfm[:, k * T:(k + 1) * T].bitcast(F32),
                    scalar1=dcol[:, k:k + 1], scalar2=None, op0=ALU.mult),
                    waits=[t_xn_ev, t_dcol,
                           t_yred_of.get((k - 2) % NK)])

            def emit_exp(k, s_):
                i = k * S + s_
                ia = i % 3
                wa = [t_deltaSTT[k], t_acol]
                prev_a_user = t_scan_hist.get(("a", ia))
                if prev_a_user is not None:
                    wa.append(prev_a_user)
                return E("scalar", lambda s, k=k, s_=s_, ia=ia:
                         s.activation(
                             out=apool[:, ia * T:(ia + 1) * T],
                             in_=delta[:, k * T:(k + 1) * T],
                             func=AF.Exp,
                             scale=acol[:, k * S + s_:k * S + s_ + 1]),
                         waits=wa)

            def emit_scans(k, a_toks):
                nonlocal t_scan_all_prev
                scan_toks[k] = [None] * S
                tb = None
                for s_ in range(S):
                    a_toks.append(emit_exp(k, s_))
                    i = k * S + s_
                    ia = i % 3
                    if s_ % 2 == 0:
                        ibp = (i // 2) % 2
                        wb = [t_u[k]] + R["ilB"]
                        prev_b_user = t_scan_hist.get(("b", ibp))
                        if prev_b_user is not None:
                            wb.append(prev_b_user)
                        usrc = ubuf[:, k * T:(k + 1) * T].unsqueeze(1)\
                            .broadcast_to([128, 2, T])
                        tb = E("vector", lambda v, k=k, s_=s_, ibp=ibp,
                               usrc=usrc: v.tensor_tensor(
                                   out=bpool[:, ibp * 2 * T:(ibp + 1) * 2 * T]
                                   .rearrange("p (a b) -> p a b", a=2),
                                   in0=usrc,
                                   in1=bcil[:, s_ * T:(s_ + 2) * T]
                                   .rearrange("p (a b) -> p a b", a=2),
                                   op=ALU.mult), waits=wb)
                        R["tb_last"] = tb
                    ibp = (i // 2) % 2
                    boff = ibp * 2 * T + (s_ % 2) * T
                    ws = [a_toks[s_], tb, t_state]
                    if s_ == 0:
                        ws.append(t_state_k[k - 1] if k > 0 else t_state_gather)
                    off = cur0 + k * S + s_
                    tsc = E("vector", lambda v, s_=s_, ia=ia, boff=boff,
                            off=off: v.tensor_tensor_scan(
                                out=hall[:, s_ * T:(s_ + 1) * T],
                                data0=apool[:, ia * T:(ia + 1) * T],
                                data1=bpool[:, boff:boff + T],
                                initial=statebuf[:, off:off + 1],
                                op0=ALU.mult, op1=ALU.add), waits=ws)
                    t_scan_hist[("a", ia)] = tsc
                    if s_ % 2 == 1:
                        t_scan_hist[("b", ibp)] = tsc
                    scan_toks[k][s_] = tsc
                    if s_ == S - 1:
                        scan_last_of_k[k] = tsc
                        t_scan_all_prev = tsc
                    if s_ == 1 and k + 1 < NK:
                        ph2b_dxn(k + 1)
                    if s_ == 2:
                        emit_sfm()
                    if s_ == 10 and R["pend_ev"] is not None:
                        cc2 = R["pend_ev"]
                        R["pend_ev"] = None
                        ph2a1_ev(cc2)

            def emit_ytail(k):
                nonlocal t_yp_reader_prev
                nxt_ok = (k + 1 < NK) and not (c > 0 and k + 1 <= 2)
                if nxt_ok:
                    ph2b_head(k + 1)
                t_state_k[k] = E("gpsimd", lambda g, k=k: g.tensor_copy(
                    out=statebuf[:, nxt0 + k * S:nxt0 + (k + 1) * S],
                    in_=hall[:].rearrange("p (s t2) -> p s t2", s=S)
                    [:, :, T - 1:T].squeeze(2)), waits=[scan_last_of_k[k]])
                t_yred = None
                for g in range(8):
                    ip = g % 2
                    eng_y = "gpsimd" if g == 7 else "vector"
                    wy = [scan_toks[k][2 * g + 1]] + R["ilC"]
                    prev_p_user = t_scan_hist.get(("p", ip))
                    if prev_p_user is not None:
                        wy.append(prev_p_user)
                    tym = E(eng_y, lambda e, g=g, ip=ip: e.tensor_tensor(
                        out=ppool[:, ip * 1024:(ip + 1) * 1024],
                        in0=hall[:, g * 1024:(g + 1) * 1024],
                        in1=bcil[:, (16 + 2 * g) * T:(16 + 2 * g + 2) * T],
                        op=ALU.mult), waits=wy)
                    if g == 7:
                        t_ymult_g7[k] = tym
                        R["tym_last_g"] = tym
                    else:
                        R["tym_last_vec"] = tym
                    tok = None
                    for half in range(2):
                        w = [tym, t_gbf]
                        if g == 0 and half == 0:
                            w.append(t_yp_reader_prev)
                        tok = E("tensor", lambda t, ip=ip, half=half, g=g:
                                t.matmul(
                                    out=ps[7][:],
                                    lhsT=identbf[:],
                                    rhs=ppool[:, ip * 1024 + half * 512:
                                              ip * 1024 + (half + 1) * 512],
                                    start=(g == 0 and half == 0),
                                    stop=False), waits=w)
                    t_scan_hist[("p", ip)] = tok
                    t_yred = tok
                t_yred = E("tensor", lambda t, k=k: t.matmul(
                    out=ps[7][:], lhsT=identbf[:],
                    rhs=dxnbuf[:, (k % 2) * T:(k % 2 + 1) * T],
                    start=False, stop=True), waits=[t_dxn[k], t_yred])
                t_yred_of[k] = t_yred
                pend_sfm[0] = k
                if nxt_ok:
                    ph2b_tail(k + 1)

            def emit_sfm():
                nonlocal t_yp_reader_prev
                k = pend_sfm[0]
                if k is None:
                    return
                pend_sfm[0] = None
                sl = k % 3
                t_sfm_tok[k] = E("vector", lambda v, k=k, sl=sl: v.tensor_copy(
                    out=sfm[:, sl * T:(sl + 1) * T], in_=ps[7][:]),
                    waits=[t_yred_of[k], t_sg_slot[sl]])
                t_yp_reader_prev = t_sfm_tok[k]

            def ph4a(m):
                rot = m % 2
                bank = ps[5 + rot]
                xre = X[c * T:(c + 1) * T, m * 128:(m + 1) * 128].rearrange(
                    "(j p) w -> p j w", p=128)
                t_xres_m[m] = E("sync", lambda s, rot=rot, xre=xre: s.dma_start(
                    out=xresm[:, rot * 512:(rot + 1) * 512].rearrange(
                        "p (j w) -> p j w", j=NJ), in_=xre),
                    dma="qxres", waits=[t_outblk_m[m - 2]])
                mm = None
                for k2 in range(NK):
                    w = [t_xq_ev, t_setup_w]
                    if k2 == 0:
                        w += [t_preg_roll[rot]]
                    mm = E("tensor", lambda t, m=m, k2=k2, bank=bank: t.matmul(
                        out=bank[:],
                        lhsT=wffn[:, k2 * 1024 + m * 128:k2 * 1024 + (m + 1) * 128],
                        rhs=xqfm[:, k2 * T:(k2 + 1) * T],
                        start=(k2 == 0), stop=(k2 == NK - 1)), waits=w)
                R["ffn_last"] = mm
                t_tpg_m[m] = E("vector", lambda v, m=m, rot=rot, bank=bank:
                               v.scalar_tensor_tensor(
                                   out=preg[:, rot * 512:(rot + 1) * 512],
                                   in0=bank[:], scalar=cscol[:, m:m + 1],
                                   in1=crepsb[:], op0=ALU.add, op1=ALU.mult),
                               waits=[mm, t_crepsb, t_cs[NK - 1],
                                      t_silu_roll[rot]])
                t_preg_roll[rot] = t_tpg_m[m]

            def ph4c(m):
                rot = m % 2
                tsl = E("scalar", lambda s, m=m, rot=rot: s.activation(
                    out=gbuf[:, rot * 512:(rot + 1) * 512],
                    in_=preg[:, rot * 512:(rot + 1) * 512], func=AF.Silu,
                    bias=biascol[:, m:m + 1]),
                    waits=[t_tpg_m[m], t_bias, t_gbuf_roll[rot]])
                t_silu_roll[rot] = tsl
                tsg = E("gpsimd", lambda g, m=m, rot=rot: g.tensor_tensor(
                    out=sgbuf[:, rot * 512:(rot + 1) * 512],
                    in0=sfm[:, (m % 3) * T:(m % 3 + 1) * T],
                    in1=gbuf[:, rot * 512:(rot + 1) * 512], op=ALU.mult),
                    waits=[tsl, t_sfm_tok[m], t_sgw_roll[rot]])
                t_gbuf_roll[rot] = tsg
                t_sg_slot[m % 3] = tsg
                last_blk = None
                for j in range(NJ):
                    tbank = ps[3 + (m * NJ + j) % 2]
                    ttr = E("tensor", lambda t, rot=rot, j=j, tbank=tbank:
                            t.transpose(
                                out=tbank[:, 0:128],
                                in_=sgbuf[:, rot * 512 + j * 128:
                                          rot * 512 + (j + 1) * 128],
                                identity=ident32[:]),
                            waits=[tsg, ps34_reader[(m * NJ + j) % 2]])
                    last_blk = E("vector", lambda v, rot=rot, j=j, tbank=tbank:
                                 v.tensor_tensor(
                                     out=outm[:, rot * 512 + j * 128:
                                              rot * 512 + (j + 1) * 128],
                                     in0=xresm[:, rot * 512 + j * 128:
                                               rot * 512 + (j + 1) * 128],
                                     in1=tbank[:, 0:128], op=ALU.add),
                                 waits=[ttr, t_xres_m[m],
                                        t_outdma_m[m - 2] if j == 0 else None])
                    ps34_reader[(m * NJ + j) % 2] = last_blk
                t_sgw_roll[rot] = last_blk
                t_outblk_m[m] = last_blk
                yap = Y[c * T:(c + 1) * T, m * 128:(m + 1) * 128].rearrange(
                    "(j p) w -> p j w", p=128)
                t_outdma_m[m] = E("sync", lambda s, rot=rot, yap=yap:
                                  s.dma_start(
                                      out=yap,
                                      in_=outm[:, rot * 512:(rot + 1) * 512]
                                      .rearrange("p (j w) -> p j w", j=NJ)),
                                  dma="qout", waits=[last_blk])

            if c == 0:
                ph2b(0)
            else:
                t_deltaSTT[0], t_u[0] = R["pre0"]
                t_deltaSTT[1], t_u[1] = R["pre1"]
                t_deltaSTT[2], t_u[2] = R["pre2"]
            ph2b_dxn(0)
            for k in range(NK):
                if k >= 3 and k % 2 == 1:
                    ph4c(k - 3)
                    ph4c(k - 2)
                a_toks = []
                emit_scans(k, a_toks)
                if k >= 1:
                    ph4a(k - 1)
                emit_ytail(k)
                if c + 1 < NCH:
                    if k == 4:
                        R["tcc"] = ph1a(c + 1)
                    elif k == 6:
                        ph1b_xn(c + 1, [t_u[NK - 1], t_dxn[NK - 1]])
                        ph2a1_mm(c + 1)
                        R["pend_ev"] = c + 1
                    elif k == 7:
                        ph2a1_ilB(c + 1)
                        ph2b(0)
                        R["pre0"] = (t_deltaSTT[0], t_u[0])
            emit_sfm()
            ph4a(NK - 1)
            ph4c(NK - 2)
            ph4c(NK - 1)
            if c + 1 < NCH:
                ph2a2(c + 1, R["tcc"])
                ph2a1_ilC(c + 1)
                ph2b(1)
                ph2b(2)
                R["pre1"] = (t_deltaSTT[1], t_u[1])
                R["pre2"] = (t_deltaSTT[2], t_u[2])
                ph1b_xq(c + 1, [R["ffn_last"]])
            t_state_gather = t_state_k[NK - 1]
            t_ymult_last_prev_chunk = t_ymult_g7[NK - 1]
            t_outblk_m[-2] = t_outblk_m[NK - 2]
            t_outblk_m[-1] = t_outblk_m[NK - 1]
            t_outdma_m[-2] = t_outdma_m[NK - 2]
            t_outdma_m[-1] = t_outdma_m[NK - 1]

        E("sync", lambda s: s.nop(),
          waits=[t_outdma_m[NK - 2], t_outdma_m[NK - 1]])

        B.resolve()
        for name, meth in (("sync", block.sync), ("vector", block.vector),
                           ("scalar", block.scalar), ("tensor", block.tensor),
                           ("gpsimd", block.gpsimd)):
            meth(lambda eng, name=name: B.run(eng, name, sems))

    return nc


_NC_CACHE = None


def kernel(L0_attn_x, L0_attn_W_deltaBC, L0_attn_W_dt, L0_attn_b_dt,
           L0_attn_A_log, L0_attn_D_param, L0_ffn_w_int8, L0_ffn_weight_scale,
           L0_ffn_bias):
    global _NC_CACHE
    if _NC_CACHE is None:
        _NC_CACHE = build_program()
    nc = _NC_CACHE

    x = np.ascontiguousarray(np.asarray(L0_attn_x, dtype=np.float32))
    base = {
        "w_dbc": np.ascontiguousarray(np.asarray(L0_attn_W_deltaBC, np.float32)),
        "w_dt": np.ascontiguousarray(np.asarray(L0_attn_W_dt, np.float32)),
        "b_dt": np.ascontiguousarray(np.asarray(L0_attn_b_dt, np.float32)),
        "a_log": np.ascontiguousarray(np.asarray(L0_attn_A_log, np.float32)),
        "d_par": np.ascontiguousarray(np.asarray(L0_attn_D_param, np.float32)),
        "w_int": np.ascontiguousarray(np.asarray(L0_ffn_w_int8, np.int32)),
        "w_sc": np.asarray(L0_ffn_weight_scale, np.float32).reshape(1),
        "f_bias": np.ascontiguousarray(np.asarray(L0_ffn_bias, np.float32)),
    }
    in_maps = []
    for b in range(NCORE):
        m = dict(base)
        m["x"] = np.ascontiguousarray(x[b])
        in_maps.append(m)
    res = run_bass_kernel_spmd(nc, in_maps, core_ids=list(range(NCORE)))
    out = np.stack([np.asarray(r["y"], np.float32) for r in res.results], axis=0)
    return out


# version 28
# speedup vs baseline: 1.0658x; 1.0015x over previous
import numpy as np
import concourse.bass as bass
import concourse.mybir as mybir
from concourse.bass_utils import run_bass_kernel_spmd

F32 = mybir.dt.float32
F32R = mybir.dt.float32r
F16 = mybir.dt.float16
BF16 = mybir.dt.bfloat16
I32 = mybir.dt.int32
AF = mybir.ActivationFunctionType
ALU = mybir.AluOpType
AX = mybir.AxisListType

L = 4096
D = 1024
S = 16
T = 512
NCH = L // T
NJ = T // 128
NK = D // 128
NCORE = 8

ENGINES = ("sync", "vector", "scalar", "tensor", "gpsimd")
STREAMS = ("qxin", "qxres", "qout", "qcrow", "qbcw", "qw",
           "qilb_s", "qilb_g", "qilc_sc", "qilc_s2")
UNORDERED = {"qilb_s", "qilb_g", "qilc_sc", "qilc_s2"}


class _Tok:
    __slots__ = ("eng", "idx", "marked", "count_after")

    def __init__(self, eng, idx):
        self.eng = eng
        self.idx = idx
        self.marked = False
        self.count_after = None


class _Builder:

    def __init__(self):
        self.ops = {e: [] for e in ENGINES}
        self.stream_n = {q: 0 for q in STREAMS}

    def emit(self, eng, fn, waits=(), dma=None):
        tok = _Tok(eng, len(self.ops[eng]))
        wl = {}
        for w in waits:
            if w is None:
                continue
            k = w.eng
            if k not in wl or w.idx > wl[k].idx:
                wl[k] = w
        for w in wl.values():
            w.marked = True
        prewait = None
        if dma is not None:
            tok.eng = dma
            tok.idx = self.stream_n[dma]
            tok.marked = True
            tok.count_after = 16 * (self.stream_n[dma] + 1)
            if self.stream_n[dma] > 0 and dma not in UNORDERED:
                prewait = 16 * self.stream_n[dma]
            self.stream_n[dma] += 1
        self.ops[eng].append({"fn": fn, "waits": list(wl.values()), "dma": dma,
                              "tok": tok, "prewait": prewait})
        return tok

    def resolve(self):
        for e in ENGINES:
            c = 0
            for op in self.ops[e]:
                if op["dma"] is not None:
                    continue
                if op["tok"].marked:
                    c += 1
                op["tok"].count_after = c

    def run(self, eng_obj, name, sems):
        wm = {}

        def w(key, val):
            if wm.get(key, -1) >= val:
                return
            eng_obj.wait_ge(sems[key], val)
            wm[key] = val

        for op in self.ops[name]:
            if op["prewait"] is not None:
                w(op["dma"], op["prewait"])
            for wt in op["waits"]:
                w(wt.eng, wt.count_after)
            inst = op["fn"](eng_obj)
            if op["dma"] is not None:
                inst.then_inc(sems[op["dma"]], 16)
            elif op["tok"].marked:
                inst.then_inc(sems[op["tok"].eng], 1)


def build_program():
    nc = bass.Bass("TRN2", target_bir_lowering=False)
    X = nc.dram_tensor("x", [L, D], F32, kind="ExternalInput").ap()
    WDBC = nc.dram_tensor("w_dbc", [96, D], F32, kind="ExternalInput").ap()
    WDT = nc.dram_tensor("w_dt", [D, 64], F32, kind="ExternalInput").ap()
    BDT = nc.dram_tensor("b_dt", [D], F32, kind="ExternalInput").ap()
    ALOG = nc.dram_tensor("a_log", [D, S], F32, kind="ExternalInput").ap()
    DPAR = nc.dram_tensor("d_par", [D], F32, kind="ExternalInput").ap()
    WINT = nc.dram_tensor("w_int", [D, D], I32, kind="ExternalInput").ap()
    WSC = nc.dram_tensor("w_sc", [1], F32, kind="ExternalInput").ap()
    FBIAS = nc.dram_tensor("f_bias", [D], F32, kind="ExternalInput").ap()
    Y = nc.dram_tensor("y", [L, D], F32, kind="ExternalOutput").ap()
    BCSCR = nc.dram_tensor("bcscr", [32, T], BF16, kind="Internal").ap()

    from contextlib import ExitStack
    with ExitStack() as ctx:
        def sb(name, shape, dt):
            return ctx.enter_context(nc.sbuf_tensor(name, shape, dt))

        xin = sb("xin", [128, 4096], F32)
        xntm = sb("xntm", [128, 2048], F32)
        scr = sb("scr", [128, 2048], F32)
        spbuf = sb("spbuf", [128, 1024], F32)
        preg = sb("preg", [128, 1024], F32)
        sgbuf = sb("sgbuf", [128, 1024], F32)
        gbuf = sb("gbuf", [128, 1024], F32)
        sfm = sb("sfm", [128, 1536], F32)
        xnfm = sb("xnfm", [128, 4096], F32R)
        xqfm = sb("xqfm", [128, 4096], F16)
        delta = sb("delta", [128, 4096], BF16)
        ubuf = sb("ubuf", [128, 4096], BF16)
        drones = sb("drones", [65, T], F32R)
        btctbf = sb("btctbf", [32, T], BF16)
        bcil = sb("bcil", [128, 32 * T], BF16)
        apool = sb("apool", [128, 3 * T], F32)
        bpool = sb("bpool", [128, 4 * T], BF16)
        dxnbuf = sb("dxnbuf", [128, 2 * T], BF16)
        hall = sb("hall", [128, 16 * T], BF16)
        ppool = sb("ppool", [128, 2 * 1024], BF16)
        wffn = sb("wffn", [128, 8192], F16)
        wdbc = sb("wdbc", [128, 768], F32R)
        wdt = sb("wdt", [65, 1024], F32R)
        acol = sb("acol", [128, 128], F32)
        dcol = sb("dcol", [128, 8], F32)
        biascol = sb("biascol", [128, 8], F32)
        cscol = sb("cscol", [128, 8], F32)
        ccols = sb("ccols", [128, 4], F32)
        wsrep = sb("wsrep", [128, 1], F32)
        bnbuf = sb("bnbuf", [128, 48], F32)
        mv = sb("mv", [128, 8], F32)
        cells = sb("cells", [128, 32], F32)
        eps5 = sb("eps5", [128, 1], F32)
        eps6 = sb("eps6", [128, 1], F32)
        onecol = sb("onecol", [128, 1], F32)
        ident32 = sb("ident32", [128, 128], F32)
        ident16 = sb("ident16", [128, 128], F16)
        identbf = sb("identbf", [128, 128], BF16)
        onesrow = sb("onesrow", [1, 128], F32R)
        onescol16 = sb("onescol16", [128, 1], F16)
        crow = sb("crow", [1, T], F32R)
        crepsb = sb("crepsb", [128, T], F32)
        statebuf = sb("statebuf", [128, 256], F32)
        xresm = sb("xresm", [128, 1024], F32)
        outm = sb("outm", [128, 1024], F32)

        ps = [ctx.enter_context(nc.psum_tensor(f"ps{i}", [128, 512], F32))
              for i in range(8)]

        sems = {}
        for e in ENGINES + STREAMS:
            sems[e] = ctx.enter_context(nc.semaphore(f"sem_{e}"))
        block = ctx.enter_context(nc.Block())

        B = _Builder()
        E = B.emit

        def cell(j, i):
            return cells[:, j * 8 + i:j * 8 + i + 1]

        reg_i32 = xin[:, 0:1024].bitcast(I32)
        reg_f32 = xin[:, 1024:2048]
        reg_dbc = xin[0:96, 2048:3072]
        reg_wdt = xin[:, 3072:3584]
        reg_alog = xin[:, 3584:3712]

        t_gset = E("gpsimd", lambda g: g.memset(ident32[:], 1.0))
        t_gset = E("gpsimd", lambda g: g.affine_select(
            out=ident32[:], in_=ident32[:], pattern=[[1, 128]],
            compare_op=ALU.is_equal, fill=0.0, base=0, channel_multiplier=-1),
            waits=[t_gset])
        t_g16 = E("gpsimd", lambda g: g.memset(ident16[:], 1.0))
        t_g16 = E("gpsimd", lambda g: g.affine_select(
            out=ident16[:], in_=ident16[:], pattern=[[1, 128]],
            compare_op=ALU.is_equal, fill=0.0, base=0, channel_multiplier=-1),
            waits=[t_g16])
        t_gbf = E("gpsimd", lambda g: g.memset(identbf[:], 1.0))
        t_gbf = E("gpsimd", lambda g: g.affine_select(
            out=identbf[:], in_=identbf[:], pattern=[[1, 128]],
            compare_op=ALU.is_equal, fill=0.0, base=0, channel_multiplier=-1),
            waits=[t_gbf])
        t_eps5 = E("gpsimd", lambda g: g.memset(eps5[:], 1e-5))
        t_eps6 = E("gpsimd", lambda g: g.memset(eps6[:], 1e-6))
        t_one = E("gpsimd", lambda g: g.memset(onecol[:], 1.0))
        t_orow0 = E("gpsimd", lambda g: g.memset(scr[0:1, 0:640], 1.0))
        t_orow = E("vector", lambda v: v.tensor_copy(
            out=onesrow[:], in_=scr[0:1, 0:128]), waits=[t_orow0])
        t_oc16 = E("gpsimd", lambda g: g.memset(onescol16[:], 1.0))
        t_drow = E("vector", lambda v: v.tensor_copy(
            out=drones[64:65, :], in_=scr[0:1, 128:640]), waits=[t_orow0])
        t_state = E("gpsimd", lambda g: g.memset(statebuf[:], 0.0))

        t_dcol = E("sync", lambda s: s.dma_start(
            out=dcol[:], in_=DPAR.rearrange("(k p) -> p k", p=128),
            allow_slow_non_contiguous=True), dma="qw")
        t_bias = E("sync", lambda s: s.dma_start(
            out=biascol[:], in_=FBIAS.rearrange("(k p) -> p k", p=128),
            allow_slow_non_contiguous=True), dma="qw")
        t_bdt = E("sync", lambda s: s.dma_start(
            out=wdt[64:65, :], in_=BDT.rearrange("(o d) -> o d", o=1).bitcast(F32R)), dma="qw")
        t_aload = E("sync", lambda s: s.dma_start(
            out=reg_alog.rearrange("p (k s) -> p k s", k=NK),
            in_=ALOG.rearrange("(k p) s -> p k s", p=128)), dma="qw")
        t_dbcl = E("sync", lambda s: s.dma_start(out=reg_dbc, in_=WDBC), dma="qw")
        t_wdtl = E("sync", lambda s: s.dma_start(
            out=reg_wdt.rearrange("p (k r) -> p k r", k=NK),
            in_=WDT.rearrange("(k p) r -> p k r", p=128)), dma="qw")

        t_wsrep_dma = E("sync", lambda s: s.dma_start(
            out=wsrep[:], in_=WSC.rearrange("(o d) -> o d", o=1)
            .broadcast_to([128, 1])), dma="qw")
        t_wsrep = t_wsrep_dma

        t_ae = E("scalar", lambda s: s.activation(
            out=scr[:, 0:128], in_=reg_alog, func=AF.Exp),
            waits=[t_aload, t_orow])
        t_acol = E("vector", lambda v: v.tensor_scalar(
            out=acol[:], in0=scr[:, 0:128], scalar1=-1.0, scalar2=None,
            op0=ALU.mult), waits=[t_ae])

        t_dbc_ev = [None] * NK
        for k in range(NK):
            bank = ps[k % 2]
            war = t_dbc_ev[k - 2] if k >= 2 else None
            t_tr = E("tensor", lambda t, k=k, bank=bank: t.transpose(
                out=bank[:, 0:96], in_=reg_dbc[:, k * 128:(k + 1) * 128],
                identity=ident32[0:96, 0:96]),
                waits=[t_dbcl, t_gset, war])
            t_dbc_ev[k] = E("vector", lambda v, k=k, bank=bank: v.tensor_copy(
                out=wdbc[:, k * 96:(k + 1) * 96], in_=bank[:, 0:96]), waits=[t_tr])

        t_wdt_ev = [None] * NK
        for k in range(NK):
            bank = ps[2 + k % 2]
            war = t_wdt_ev[k - 2] if k >= 2 else None
            t_tr = E("tensor", lambda t, k=k, bank=bank: t.transpose(
                out=bank[0:64, 0:128], in_=reg_wdt[:, k * 64:(k + 1) * 64],
                identity=ident32[:]),
                waits=[t_wdtl, t_gset, war])
            t_wdt_ev[k] = E("vector", lambda v, k=k, bank=bank: v.tensor_copy(
                out=wdt[0:64, k * 128:(k + 1) * 128], in_=bank[0:64, 0:128]),
                waits=[t_tr])

        t_wload = None
        t_wcopy = None
        t_wev_last = None
        t_wtr_last = [None, None]
        for m in range(NK):
            t_wload = E("sync", lambda s, m=m: s.dma_start(
                out=reg_i32, in_=WINT[m * 128:(m + 1) * 128, :]),
                dma="qw", waits=[t_wcopy])
            t_wcopy = E("vector", lambda v: v.tensor_copy(out=reg_f32, in_=reg_i32),
                        waits=[t_wload, t_wtr_last[1] or t_wtr_last[0]])
            for half in range(2):
                bank = ps[4 + half]
                trs = []
                for kk in range(4):
                    k = half * 4 + kk
                    trs.append(E("tensor", lambda t, k=k, kk=kk, bank=bank: t.transpose(
                        out=bank[:, kk * 128:(kk + 1) * 128],
                        in_=reg_f32[:, k * 128:(k + 1) * 128],
                        identity=ident32[:]),
                        waits=[t_wcopy, t_wev_last, t_gset]))
                t_wtr_last[half] = trs[-1]
                dst = wffn[:].rearrange("p (k o) -> p k o", k=NK)[
                    :, half * 4:(half + 1) * 4, m * 128:(m + 1) * 128]
                t_wev_last = E("vector", lambda v, bank=bank, dst=dst: v.tensor_copy(
                    out=dst, in_=bank[:].rearrange("p (a b) -> p a b", a=4)),
                    waits=[trs[-1]])

        t_cs = [None] * NK
        for m in range(NK):
            mm_last = None
            for k in range(NK):
                mm_last = E("tensor", lambda t, m=m, k=k: t.matmul(
                    out=ps[6][:, 0:1],
                    lhsT=wffn[:, k * 1024 + m * 128:k * 1024 + (m + 1) * 128],
                    rhs=onescol16[:], start=(k == 0), stop=(k == NK - 1)),
                    waits=[t_wev_last, t_oc16, t_cs[m - 1] if m else None])
            t_cs[m] = E("vector", lambda v, m=m: v.tensor_scalar(
                out=cscol[:, m:m + 1], in0=ps[6][:, 0:1], scalar1=-1536.0,
                scalar2=None, op0=ALU.mult), waits=[mm_last])

        t_setup_w = t_wev_last
        t_setup_x = t_cs[NK - 1]
        t_setup_pe = t_wtr_last[1]
        t_scr_set = t_acol
        t_evac_bank_init = [t_dbc_ev[NK - 2], t_dbc_ev[NK - 1]]

        t_lnts = [[None] * NJ for _ in range(NCH + 1)]
        t_xin_dma = [None] * NJ
        t_xn_ev = None
        t_xq_ev = None
        t_state_gather = None
        t_sfm_roll = [t_wsrep, t_wsrep, t_wsrep]
        t_sg_slot = [None, None, None]
        t_preg_roll = [t_wev_last, t_cs[NK - 1]]
        t_silu_roll = [None, None]
        t_sgw_roll = [None, None]
        t_gbuf_roll = [None, None]
        ps34_reader = [t_wev_last, t_wev_last]
        t_sp_roll = [None, None]
        t_xqslot = [None] * NJ
        t_scrslot = [None, None]
        t_evac_bank_roll = list(t_evac_bank_init)
        t_crow_prev = [None] * NJ
        t_quant_j = [None] * NJ
        t_ln_tok = [None] * NJ
        t_cc_next = None
        t_dxn = [None] * NK
        t_yred_of = {}
        R = {"dr": None, "bt": None, "bcw": None, "crepsb": None,
             "dmm_last": None, "ffn_last": None, "tcc": None,
             "il_last": [], "ilB": [], "ilC": [],
             "tb_last": None, "tym_last_vec": None, "tym_last_g": None,
             "bcmm": None, "pend_ev": None, "pre0": None}
        t_outblk_m = [None] * (NK + 2)
        t_outdma_m = [None] * (NK + 2)
        t_xres_m = [None] * (NK + 2)
        t_scan_hist = {}
        t_scan_all_prev = None
        t_ymult_last_prev_chunk = None
        t_dbc_done_prev = None
        t_crepsb = None
        t_yp_reader_prev = t_wsrep

        for c in range(NCH):
            par = c % 2
            cur0 = par * 128
            nxt0 = (1 - par) * 128

            def ph1a(cc):
                for j in range(NJ):
                    w = ([t_lnts[cc - 1][j], t_quant_j[j]] if cc > 0 else
                         [t_setup_pe, t_setup_x, t_scr_set, t_ae])
                    t_xin_dma[j] = E("sync", lambda s, cc=cc, j=j: s.dma_start(
                        out=xin[:, j * 1024:(j + 1) * 1024],
                        in_=X[cc * T + j * 128:cc * T + (j + 1) * 128, :]),
                        dma="qxin", waits=w)
                tag = None
                for j in range(NJ):
                    E("vector", lambda v, j=j: v.bn_stats(
                        out=bnbuf[:, j * 12:j * 12 + 6],
                        in_=xin[:, j * 1024:j * 1024 + 512]), waits=[t_xin_dma[j]])
                    tb2 = E("vector", lambda v, j=j: v.bn_stats(
                        out=bnbuf[:, j * 12 + 6:j * 12 + 12],
                        in_=xin[:, j * 1024 + 512:j * 1024 + 1024]),
                        waits=[t_xin_dma[j]])
                    tag = E("vector", lambda v, j=j: v.bn_aggr(
                        out=mv[:, j * 2:j * 2 + 2],
                        in_=bnbuf[:, j * 12:j * 12 + 12]), waits=[tb2])
                mv3 = mv[:].rearrange("p (j two) -> p j two", two=2)
                tcc_l = [None, None]
                for pair in range(2):
                    j0 = pair * 2
                    vars2 = mv3[:, j0:j0 + 2, 1:2].squeeze(2)
                    means2 = mv3[:, j0:j0 + 2, 0:1].squeeze(2)
                    cl = lambda a, b, pair=pair: cells[:, pair * 16 + a:
                                                       pair * 16 + b]
                    tl2 = E("scalar", lambda s, vars2=vars2, cl=cl: s.activation(
                        out=cl(0, 2), in_=vars2, func=AF.Ln, bias=eps5[:, 0:1]),
                        waits=[tag, t_eps5])
                    tr2_ = E("scalar", lambda s, cl=cl: s.activation(
                        out=cl(2, 4), in_=cl(0, 2), func=AF.Exp, scale=-0.5),
                        waits=[tl2])
                    tn2 = E("vector", lambda v, means2=means2, cl=cl:
                            v.scalar_tensor_tensor(
                                out=cl(4, 6), in0=means2, scalar=-1.0,
                                in1=cl(2, 4), op0=ALU.mult, op1=ALU.mult),
                            waits=[tr2_])
                    t_am_l = [None, None]
                    for jj in range(2):
                        j = j0 + jj
                        jr = j % 2
                        t_ln = E("vector", lambda v, j=j, jj=jj, cl=cl:
                                 v.tensor_scalar(
                                     out=xin[:, j * 1024:(j + 1) * 1024],
                                     in0=xin[:, j * 1024:(j + 1) * 1024],
                                     scalar1=cl(2 + jj, 3 + jj),
                                     scalar2=cl(4 + jj, 5 + jj),
                                     op0=ALU.mult, op1=ALU.add),
                                 waits=[tn2])
                        t_ln_tok[j] = t_ln
                        t_am_l[jj] = E("vector", lambda v, j=j, jj=jj, cl=cl:
                                       v.reduce_max(
                                           out=cl(6 + jj, 7 + jj),
                                           in_=xin[:, j * 1024:(j + 1) * 1024],
                                           axis=AX.X,
                                           apply_absolute_value=True),
                                       waits=[t_ln])
                    tr22 = E("vector", lambda v, cl=cl: v.tensor_mul(
                        out=cl(8, 10), in0=cl(2, 4), in1=cl(2, 4)), waits=[tn2])
                    tms = E("vector", lambda v, vars2=vars2, cl=cl:
                            v.tensor_tensor(
                                out=cl(0, 2), in0=vars2, in1=cl(8, 10),
                                op=ALU.mult), waits=[tr22, tl2])
                    tlm = E("scalar", lambda s, cl=cl: s.activation(
                        out=cl(0, 2), in_=cl(0, 2), func=AF.Ln,
                        bias=eps6[:, 0:1]), waits=[tms, t_eps6])
                    trq = E("scalar", lambda s, cl=cl: s.activation(
                        out=cl(10, 12), in_=cl(0, 2), func=AF.Exp, scale=-0.5),
                        waits=[tlm])
                    tamn = E("vector", lambda v, cl=cl: v.tensor_mul(
                        out=cl(12, 14), in0=cl(6, 8), in1=cl(10, 12)),
                        waits=[trq, t_am_l[1]])
                    tamc = E("vector", lambda v, cl=cl: v.tensor_scalar(
                        out=cl(12, 14), in0=cl(12, 14), scalar1=1e-5,
                        scalar2=None, op0=ALU.max), waits=[tamn])
                    trec = E("vector", lambda v, cl=cl: v.reciprocal(
                        out=cl(0, 2), in_=cl(12, 14)), waits=[tamc])
                    tqc2 = E("vector", lambda v, cl=cl: v.scalar_tensor_tensor(
                        out=cl(14, 16), in0=cl(0, 2), scalar=127.0,
                        in1=cl(10, 12), op0=ALU.mult, op1=ALU.mult),
                        waits=[trec])
                    tcc_l[pair] = E("vector", lambda v, j0=j0, cl=cl:
                                    v.scalar_tensor_tensor(
                                        out=ccols[:, j0:j0 + 2], in0=cl(12, 14),
                                        scalar=1.0 / 127.0,
                                        in1=wsrep[:, 0:1].broadcast_to([128, 2]),
                                        op0=ALU.mult, op1=ALU.mult),
                                    waits=[tqc2, t_wsrep, t_crow_prev[j0],
                                           t_crow_prev[j0 + 1]])
                    for jj in range(2):
                        j = j0 + jj
                        jr = j % 2
                        tq1 = E("vector", lambda v, j=j, jr=jr, jj=jj, cl=cl:
                                v.tensor_scalar(
                                    out=scr[:, jr * 1024:(jr + 1) * 1024],
                                    in0=xin[:, j * 1024:(j + 1) * 1024],
                                    scalar1=cl(14 + jj, 15 + jj),
                                    scalar2=1536.0,
                                    op0=ALU.mult, op1=ALU.add),
                                waits=[tqc2])
                        t_quant_j[j] = E("vector", lambda v, j=j, jr=jr:
                                         v.tensor_scalar(
                                             out=xntm[:].bitcast(F16)[
                                                 :, j * 1024:(j + 1) * 1024],
                                             in0=scr[:, jr * 1024:
                                                     (jr + 1) * 1024],
                                             scalar1=1408.0, scalar2=1663.0,
                                             op0=ALU.max, op1=ALU.min),
                                        waits=[tq1, t_xqslot[j]])
                return tcc_l

            def ph1b_xn(cc, extra_w):
                nonlocal t_xn_ev
                for j in range(NJ):
                    jr = j % 2
                    t_ln = t_ln_tok[j]
                    for half in range(2):
                        bank = ps[half]
                        trs_last = None
                        for kk in range(4):
                            k = half * 4 + kk
                            trs_last = E("tensor", lambda t, j=j, k=k, kk=kk,
                                         bank=bank: t.transpose(
                                             out=bank[:, kk * 128:
                                                      (kk + 1) * 128],
                                             in_=xin[:, j * 1024 + k * 128:
                                                     j * 1024 + (k + 1) * 128],
                                             identity=ident32[:]),
                                         waits=[t_ln, t_evac_bank_roll[half],
                                                t_gset])
                        dst = xnfm[:].rearrange("p (k t2) -> p k t2", k=NK)[
                            :, half * 4:(half + 1) * 4, j * 128:(j + 1) * 128]
                        t_evac_bank_roll[half] = E(
                            "scalar", lambda sc, bank=bank, dst=dst: sc.copy(
                                out=dst,
                                in_=bank[:].rearrange("p (a b) -> p a b", a=4)),
                            waits=[trs_last] + extra_w)
                        t_xn_ev = t_evac_bank_roll[half]
                        t_lnts[cc][j] = trs_last

            def ph1b_xq(cc, extra_w):
                nonlocal t_xq_ev
                for j in range(NJ):
                    tq16 = t_quant_j[j]
                    xq_trs = None
                    for half in range(2):
                        bank = ps[half]
                        trs_last = None
                        for kk in range(4):
                            k = half * 4 + kk
                            src = xntm[:].bitcast(F16)[
                                :, j * 1024 + k * 128:j * 1024 + (k + 1) * 128]
                            trs_last = E("tensor", lambda t, src=src, kk=kk,
                                         bank=bank: t.transpose(
                                             out=bank[:].bitcast(F16)[
                                                 :, kk * 128:(kk + 1) * 128],
                                             in_=src, identity=ident16[:]),
                                         waits=[tq16, t_evac_bank_roll[half],
                                                t_g16])
                        xq_trs = trs_last
                        dst = xqfm[:].rearrange("p (k t2) -> p k t2", k=NK)[
                            :, half * 4:(half + 1) * 4, j * 128:(j + 1) * 128]
                        t_evac_bank_roll[half] = E(
                            "scalar", lambda sc, bank=bank, dst=dst: sc.copy(
                                out=dst,
                                in_=bank[:].bitcast(F16)[:, 0:512].rearrange(
                                    "p (a b) -> p a b", a=4)),
                            waits=[trs_last] + extra_w)
                        t_xq_ev = t_evac_bank_roll[half]
                    t_xqslot[j] = xq_trs

            def ph2a1_mm(cc):
                mm_last = None
                for k in range(NK):
                    w = [t_xn_ev, t_dbc_ev[NK - 1]]
                    if k == 0:
                        w += [R["crepsb"] if cc > 0 else t_wdt_ev[NK - 2]]
                    mm_last = E("tensor", lambda t, k=k: t.matmul(
                        out=ps[2][0:96, :], lhsT=wdbc[:, k * 96:(k + 1) * 96],
                        rhs=xnfm[:, k * T:(k + 1) * T],
                        start=(k == 0), stop=(k == NK - 1)), waits=w)
                R["bcmm"] = mm_last

            def ph2a1_ev(cc):
                mm_last = R["bcmm"]
                R["dr"] = E("vector", lambda v: v.tensor_copy(
                    out=drones[0:64, :], in_=ps[2][0:64, :]),
                    waits=[mm_last, t_drow, R["dmm_last"]])
                R["bt"] = E("vector", lambda v: v.tensor_copy(
                    out=btctbf[0:32, :], in_=ps[2][64:96, :]),
                    waits=[mm_last, R["bcw"]])
                R["bcw"] = E("sync", lambda sy: sy.dma_start(
                    out=BCSCR, in_=btctbf[0:32, :]),
                    dma="qbcw", waits=[R["bt"]] + R["il_last"])

            def ph2a1_ilB(cc):
                wB = [R["bcw"]] + ([R["tb_last"]] if cc > 0 else [])
                ilB = [None, None]
                for i in range(8):
                    for half in range(2):
                        r = 2 * i + half
                        qe, qs = (("sync", "qilb_s") if half == 0 else
                                  ("gpsimd", "qilb_g"))
                        ilB[half] = E(qe, lambda q, r=r: q.dma_start(
                            out=bcil[:, r * T:(r + 1) * T],
                            in_=BCSCR[r:r + 1, :].broadcast_to([128, T])),
                            dma=qs, waits=wB)
                R["ilB"] = ilB

            def ph2a1_ilC(cc):
                wC = [R["bcw"]] + ([R["tym_last_vec"], R["tym_last_g"]]
                                   if cc > 0 else [])
                ilC = [None, None]
                for i in range(8):
                    for half in range(2):
                        rc = 16 + 2 * i + half
                        qe, qs = (("scalar", "qilc_sc") if half == 0 else
                                  ("sync", "qilc_s2"))
                        ilC[half] = E(qe, lambda q, rc=rc: q.dma_start(
                            out=bcil[:, rc * T:(rc + 1) * T],
                            in_=BCSCR[rc:rc + 1, :].broadcast_to([128, T])),
                            dma=qs, waits=wC)
                R["ilC"] = ilC
                R["il_last"] = R["ilB"] + ilC

            def ph2a2(cc, tcc_pair):
                t_crow = [None] * NJ
                for j in range(NJ):
                    t_crow[j] = E("sync", lambda s, j=j: s.dma_start(
                        out=crow[0:1, j * 128:(j + 1) * 128],
                        in_=ccols[:, j:j + 1].bitcast(F32R),
                        allow_slow_non_contiguous=True),
                        dma="qcrow",
                        waits=[tcc_pair[0] if j < 2 else tcc_pair[1]])
                    t_crow_prev[j] = t_crow[j]
                t_crep = E("tensor", lambda t: t.matmul(
                    out=ps[2][:], lhsT=onesrow[0:1, :], rhs=crow[0:1, :],
                    start=True, stop=True),
                    waits=[t_crow[NJ - 1], R["bt"], R["dr"], t_orow])
                R["crepsb"] = E("vector", lambda v: v.tensor_copy(
                    out=crepsb[:], in_=ps[2][:]),
                    waits=[t_crep, t_preg_roll[0], t_preg_roll[1]])

            if c == 0:
                tcc_pair0 = ph1a(0)
                ph1b_xn(0, [])
                ph1b_xq(0, [])
                ph2a1_mm(0)
                ph2a1_ev(0)
                ph2a1_ilB(0)
                ph2a1_ilC(0)
                R["tcc"] = tcc_pair0
                ph2a2(0, R["tcc"])
            t_crepsb = R["crepsb"]
            t_dr = R["dr"]

            t_ta3 = [None] * NK
            t_deltaSTT = [None] * NK
            t_u = [None] * NK
            t_state_k = [None] * NK
            t_sfm_tok = [None] * NK
            t_tpg_m = [None] * NK
            t_ymult_g7 = [None] * NK
            scan_last_of_k = [None] * NK
            scan_toks = [None] * NK
            pend_sfm = [None]

            def ph2b_head(k):
                bank = ps[3 + k % 2]
                t_dmm = E("tensor", lambda t, k=k, bank=bank: t.matmul(
                    out=bank[:], lhsT=wdt[0:65, k * 128:(k + 1) * 128],
                    rhs=drones[0:65, :], start=True, stop=True),
                    waits=[R["dr"], t_wdt_ev[NK - 1], t_bdt,
                           ps34_reader[k % 2]])
                R["dmm_last"] = t_dmm
                sp0 = spbuf[:, 0:512]
                sp1 = spbuf[:, 512:1024]
                ta1 = E("scalar", lambda s, bank=bank: s.activation(
                    out=sp0, in_=bank[:], func=AF.Abs),
                    waits=[t_dmm, t_sp_roll[0]])
                ta2 = E("scalar", lambda s: s.activation(
                    out=sp1, in_=sp0, func=AF.Exp, scale=-1.0),
                    waits=[ta1, t_sp_roll[1]])
                ta3 = E("scalar", lambda s: s.activation(
                    out=sp0, in_=sp1, func=AF.Ln, bias=onecol[:, 0:1]),
                    waits=[ta2, t_one])
                t_sp_roll[1] = ta3
                t_ta3[k] = ta3

            def ph2b_tail(k):
                bank = ps[3 + k % 2]
                t_deltaSTT[k] = E("vector", lambda v, k=k, bank=bank:
                                  v.scalar_tensor_tensor(
                                      out=delta[:, k * T:(k + 1) * T],
                                      in0=bank[:], scalar=0.0, in1=sp0_ap(),
                                      op0=ALU.max, op1=ALU.add),
                                  waits=[t_ta3[k]])
                t_sp_roll[0] = t_deltaSTT[k]
                ps34_reader[k % 2] = t_deltaSTT[k]
                t_u[k] = E("gpsimd", lambda g, k=k: g.tensor_tensor(
                    out=ubuf[:, k * T:(k + 1) * T],
                    in0=delta[:, k * T:(k + 1) * T],
                    in1=xnfm[:, k * T:(k + 1) * T].bitcast(F32),
                    op=ALU.mult), waits=[t_deltaSTT[k]])

            def sp0_ap():
                return spbuf[:, 0:512]

            def ph2b(k):
                ph2b_head(k)
                ph2b_tail(k)

            def ph2b_dxn(k):
                t_dxn[k] = E("vector", lambda v, k=k: v.tensor_scalar(
                    out=dxnbuf[:, (k % 2) * T:(k % 2 + 1) * T],
                    in0=xnfm[:, k * T:(k + 1) * T].bitcast(F32),
                    scalar1=dcol[:, k:k + 1], scalar2=None, op0=ALU.mult),
                    waits=[t_xn_ev, t_dcol,
                           t_yred_of.get((k - 2) % NK)])

            def emit_exp(k, s_):
                i = k * S + s_
                ia = i % 3
                wa = [t_deltaSTT[k], t_acol]
                prev_a_user = t_scan_hist.get(("a", ia))
                if prev_a_user is not None:
                    wa.append(prev_a_user)
                return E("scalar", lambda s, k=k, s_=s_, ia=ia:
                         s.activation(
                             out=apool[:, ia * T:(ia + 1) * T],
                             in_=delta[:, k * T:(k + 1) * T],
                             func=AF.Exp,
                             scale=acol[:, k * S + s_:k * S + s_ + 1]),
                         waits=wa)

            def emit_scans(k, a_toks):
                nonlocal t_scan_all_prev
                scan_toks[k] = [None] * S
                tb = None
                for s_ in range(S):
                    a_toks.append(emit_exp(k, s_))
                    i = k * S + s_
                    ia = i % 3
                    if s_ % 2 == 0:
                        ibp = (i // 2) % 2
                        wb = [t_u[k]] + R["ilB"]
                        prev_b_user = t_scan_hist.get(("b", ibp))
                        if prev_b_user is not None:
                            wb.append(prev_b_user)
                        usrc = ubuf[:, k * T:(k + 1) * T].unsqueeze(1)\
                            .broadcast_to([128, 2, T])
                        tb = E("vector", lambda v, k=k, s_=s_, ibp=ibp,
                               usrc=usrc: v.tensor_tensor(
                                   out=bpool[:, ibp * 2 * T:(ibp + 1) * 2 * T]
                                   .rearrange("p (a b) -> p a b", a=2),
                                   in0=usrc,
                                   in1=bcil[:, s_ * T:(s_ + 2) * T]
                                   .rearrange("p (a b) -> p a b", a=2),
                                   op=ALU.mult), waits=wb)
                        R["tb_last"] = tb
                    ibp = (i // 2) % 2
                    boff = ibp * 2 * T + (s_ % 2) * T
                    ws = [a_toks[s_], tb, t_state]
                    if s_ == 0:
                        ws.append(t_state_k[k - 1] if k > 0 else t_state_gather)
                    off = cur0 + k * S + s_
                    tsc = E("vector", lambda v, s_=s_, ia=ia, boff=boff,
                            off=off: v.tensor_tensor_scan(
                                out=hall[:, s_ * T:(s_ + 1) * T],
                                data0=apool[:, ia * T:(ia + 1) * T],
                                data1=bpool[:, boff:boff + T],
                                initial=statebuf[:, off:off + 1],
                                op0=ALU.mult, op1=ALU.add), waits=ws)
                    t_scan_hist[("a", ia)] = tsc
                    if s_ % 2 == 1:
                        t_scan_hist[("b", ibp)] = tsc
                    scan_toks[k][s_] = tsc
                    if s_ == S - 1:
                        scan_last_of_k[k] = tsc
                        t_scan_all_prev = tsc
                    if s_ == 1 and k + 1 < NK:
                        ph2b_dxn(k + 1)
                    if s_ == 2 and pend_sfm[0] != 0:
                        emit_sfm()
                    if s_ == 8:
                        emit_sfm()
                    if s_ == 10 and R["pend_ev"] is not None:
                        cc2 = R["pend_ev"]
                        R["pend_ev"] = None
                        ph2a1_ev(cc2)

            def emit_ytail(k):
                nonlocal t_yp_reader_prev
                nxt_ok = (k + 1 < NK) and not (c > 0 and k + 1 <= 2)
                if nxt_ok:
                    ph2b_head(k + 1)
                t_state_k[k] = E("gpsimd", lambda g, k=k: g.tensor_copy(
                    out=statebuf[:, nxt0 + k * S:nxt0 + (k + 1) * S],
                    in_=hall[:].rearrange("p (s t2) -> p s t2", s=S)
                    [:, :, T - 1:T].squeeze(2)), waits=[scan_last_of_k[k]])
                t_yred = None
                for g in range(8):
                    ip = g % 2
                    eng_y = "gpsimd" if g == 7 else "vector"
                    wy = [scan_toks[k][2 * g + 1]] + R["ilC"]
                    prev_p_user = t_scan_hist.get(("p", ip))
                    if prev_p_user is not None:
                        wy.append(prev_p_user)
                    tym = E(eng_y, lambda e, g=g, ip=ip: e.tensor_tensor(
                        out=ppool[:, ip * 1024:(ip + 1) * 1024],
                        in0=hall[:, g * 1024:(g + 1) * 1024],
                        in1=bcil[:, (16 + 2 * g) * T:(16 + 2 * g + 2) * T],
                        op=ALU.mult), waits=wy)
                    if g == 7:
                        t_ymult_g7[k] = tym
                        R["tym_last_g"] = tym
                    else:
                        R["tym_last_vec"] = tym
                    tok = None
                    for half in range(2):
                        w = [tym, t_gbf]
                        if g == 0 and half == 0:
                            w.append(t_yp_reader_prev)
                        tok = E("tensor", lambda t, ip=ip, half=half, g=g:
                                t.matmul(
                                    out=ps[7][:],
                                    lhsT=identbf[:],
                                    rhs=ppool[:, ip * 1024 + half * 512:
                                              ip * 1024 + (half + 1) * 512],
                                    start=(g == 0 and half == 0),
                                    stop=False), waits=w)
                    t_scan_hist[("p", ip)] = tok
                    t_yred = tok
                t_yred = E("tensor", lambda t, k=k: t.matmul(
                    out=ps[7][:], lhsT=identbf[:],
                    rhs=dxnbuf[:, (k % 2) * T:(k % 2 + 1) * T],
                    start=False, stop=True), waits=[t_dxn[k], t_yred])
                t_yred_of[k] = t_yred
                pend_sfm[0] = k
                if nxt_ok:
                    ph2b_tail(k + 1)

            def emit_sfm():
                nonlocal t_yp_reader_prev
                k = pend_sfm[0]
                if k is None:
                    return
                pend_sfm[0] = None
                sl = k % 3
                t_sfm_tok[k] = E("vector", lambda v, k=k, sl=sl: v.tensor_copy(
                    out=sfm[:, sl * T:(sl + 1) * T], in_=ps[7][:]),
                    waits=[t_yred_of[k], t_sg_slot[sl]])
                t_yp_reader_prev = t_sfm_tok[k]

            def ph4a(m):
                rot = m % 2
                bank = ps[5 + rot]
                xre = X[c * T:(c + 1) * T, m * 128:(m + 1) * 128].rearrange(
                    "(j p) w -> p j w", p=128)
                t_xres_m[m] = E("sync", lambda s, rot=rot, xre=xre: s.dma_start(
                    out=xresm[:, rot * 512:(rot + 1) * 512].rearrange(
                        "p (j w) -> p j w", j=NJ), in_=xre),
                    dma="qxres", waits=[t_outblk_m[m - 2]])
                mm = None
                for k2 in range(NK):
                    w = [t_xq_ev, t_setup_w]
                    if k2 == 0:
                        w += [t_preg_roll[rot]]
                    mm = E("tensor", lambda t, m=m, k2=k2, bank=bank: t.matmul(
                        out=bank[:],
                        lhsT=wffn[:, k2 * 1024 + m * 128:k2 * 1024 + (m + 1) * 128],
                        rhs=xqfm[:, k2 * T:(k2 + 1) * T],
                        start=(k2 == 0), stop=(k2 == NK - 1)), waits=w)
                R["ffn_last"] = mm
                t_tpg_m[m] = E("vector", lambda v, m=m, rot=rot, bank=bank:
                               v.scalar_tensor_tensor(
                                   out=preg[:, rot * 512:(rot + 1) * 512],
                                   in0=bank[:], scalar=cscol[:, m:m + 1],
                                   in1=crepsb[:], op0=ALU.add, op1=ALU.mult),
                               waits=[mm, t_crepsb, t_cs[NK - 1],
                                      t_silu_roll[rot]])
                t_preg_roll[rot] = t_tpg_m[m]

            def ph4c(m):
                rot = m % 2
                tsl = E("scalar", lambda s, m=m, rot=rot: s.activation(
                    out=gbuf[:, rot * 512:(rot + 1) * 512],
                    in_=preg[:, rot * 512:(rot + 1) * 512], func=AF.Silu,
                    bias=biascol[:, m:m + 1]),
                    waits=[t_tpg_m[m], t_bias, t_gbuf_roll[rot]])
                t_silu_roll[rot] = tsl
                tsg = E("gpsimd", lambda g, m=m, rot=rot: g.tensor_tensor(
                    out=sgbuf[:, rot * 512:(rot + 1) * 512],
                    in0=sfm[:, (m % 3) * T:(m % 3 + 1) * T],
                    in1=gbuf[:, rot * 512:(rot + 1) * 512], op=ALU.mult),
                    waits=[tsl, t_sfm_tok[m], t_sgw_roll[rot]])
                t_gbuf_roll[rot] = tsg
                t_sg_slot[m % 3] = tsg
                last_blk = None
                for j in range(NJ):
                    tbank = ps[3 + (m * NJ + j) % 2]
                    ttr = E("tensor", lambda t, rot=rot, j=j, tbank=tbank:
                            t.transpose(
                                out=tbank[:, 0:128],
                                in_=sgbuf[:, rot * 512 + j * 128:
                                          rot * 512 + (j + 1) * 128],
                                identity=ident32[:]),
                            waits=[tsg, ps34_reader[(m * NJ + j) % 2]])
                    last_blk = E("vector", lambda v, rot=rot, j=j, tbank=tbank:
                                 v.tensor_tensor(
                                     out=outm[:, rot * 512 + j * 128:
                                              rot * 512 + (j + 1) * 128],
                                     in0=xresm[:, rot * 512 + j * 128:
                                               rot * 512 + (j + 1) * 128],
                                     in1=tbank[:, 0:128], op=ALU.add),
                                 waits=[ttr, t_xres_m[m],
                                        t_outdma_m[m - 2] if j == 0 else None])
                    ps34_reader[(m * NJ + j) % 2] = last_blk
                t_sgw_roll[rot] = last_blk
                t_outblk_m[m] = last_blk
                yap = Y[c * T:(c + 1) * T, m * 128:(m + 1) * 128].rearrange(
                    "(j p) w -> p j w", p=128)
                t_outdma_m[m] = E("sync", lambda s, rot=rot, yap=yap:
                                  s.dma_start(
                                      out=yap,
                                      in_=outm[:, rot * 512:(rot + 1) * 512]
                                      .rearrange("p (j w) -> p j w", j=NJ)),
                                  dma="qout", waits=[last_blk])

            if c == 0:
                ph2b(0)
            else:
                t_deltaSTT[0], t_u[0] = R["pre0"]
                t_deltaSTT[1], t_u[1] = R["pre1"]
                t_deltaSTT[2], t_u[2] = R["pre2"]
            ph2b_dxn(0)
            for k in range(NK):
                if k >= 3 and k % 2 == 1:
                    ph4c(k - 3)
                    ph4c(k - 2)
                a_toks = []
                emit_scans(k, a_toks)
                if k >= 1:
                    ph4a(k - 1)
                emit_ytail(k)
                if c + 1 < NCH:
                    if k == 4:
                        R["tcc"] = ph1a(c + 1)
                    elif k == 6:
                        ph1b_xn(c + 1, [t_u[NK - 1], t_dxn[NK - 1]])
                        ph2a1_mm(c + 1)
                        R["pend_ev"] = c + 1
                    elif k == 7:
                        ph2a1_ilB(c + 1)
                        ph2b(0)
                        R["pre0"] = (t_deltaSTT[0], t_u[0])
            emit_sfm()
            ph4a(NK - 1)
            ph4c(NK - 2)
            ph4c(NK - 1)
            if c + 1 < NCH:
                ph2a2(c + 1, R["tcc"])
                ph2a1_ilC(c + 1)
                ph2b(1)
                ph2b(2)
                R["pre1"] = (t_deltaSTT[1], t_u[1])
                R["pre2"] = (t_deltaSTT[2], t_u[2])
                ph1b_xq(c + 1, [R["ffn_last"]])
            t_state_gather = t_state_k[NK - 1]
            t_ymult_last_prev_chunk = t_ymult_g7[NK - 1]
            t_outblk_m[-2] = t_outblk_m[NK - 2]
            t_outblk_m[-1] = t_outblk_m[NK - 1]
            t_outdma_m[-2] = t_outdma_m[NK - 2]
            t_outdma_m[-1] = t_outdma_m[NK - 1]

        E("sync", lambda s: s.nop(),
          waits=[t_outdma_m[NK - 2], t_outdma_m[NK - 1]])

        B.resolve()
        for name, meth in (("sync", block.sync), ("vector", block.vector),
                           ("scalar", block.scalar), ("tensor", block.tensor),
                           ("gpsimd", block.gpsimd)):
            meth(lambda eng, name=name: B.run(eng, name, sems))

    return nc


_NC_CACHE = None


def kernel(L0_attn_x, L0_attn_W_deltaBC, L0_attn_W_dt, L0_attn_b_dt,
           L0_attn_A_log, L0_attn_D_param, L0_ffn_w_int8, L0_ffn_weight_scale,
           L0_ffn_bias):
    global _NC_CACHE
    if _NC_CACHE is None:
        _NC_CACHE = build_program()
    nc = _NC_CACHE

    x = np.ascontiguousarray(np.asarray(L0_attn_x, dtype=np.float32))
    base = {
        "w_dbc": np.ascontiguousarray(np.asarray(L0_attn_W_deltaBC, np.float32)),
        "w_dt": np.ascontiguousarray(np.asarray(L0_attn_W_dt, np.float32)),
        "b_dt": np.ascontiguousarray(np.asarray(L0_attn_b_dt, np.float32)),
        "a_log": np.ascontiguousarray(np.asarray(L0_attn_A_log, np.float32)),
        "d_par": np.ascontiguousarray(np.asarray(L0_attn_D_param, np.float32)),
        "w_int": np.ascontiguousarray(np.asarray(L0_ffn_w_int8, np.int32)),
        "w_sc": np.asarray(L0_ffn_weight_scale, np.float32).reshape(1),
        "f_bias": np.ascontiguousarray(np.asarray(L0_ffn_bias, np.float32)),
    }
    in_maps = []
    for b in range(NCORE):
        m = dict(base)
        m["x"] = np.ascontiguousarray(x[b])
        in_maps.append(m)
    res = run_bass_kernel_spmd(nc, in_maps, core_ids=list(range(NCORE)))
    out = np.stack([np.asarray(r["y"], np.float32) for r in res.results], axis=0)
    return out


# version 29 (final)
# speedup vs baseline: 1.0701x; 1.0041x over previous
import numpy as np
import concourse.bass as bass
import concourse.mybir as mybir
from concourse.bass_utils import run_bass_kernel_spmd

F32 = mybir.dt.float32
F32R = mybir.dt.float32r
F16 = mybir.dt.float16
BF16 = mybir.dt.bfloat16
I32 = mybir.dt.int32
AF = mybir.ActivationFunctionType
ALU = mybir.AluOpType
AX = mybir.AxisListType

L = 4096
D = 1024
S = 16
T = 512
NCH = L // T
NJ = T // 128
NK = D // 128
NCORE = 8

ENGINES = ("sync", "vector", "scalar", "tensor", "gpsimd")
STREAMS = ("qxin", "qxres", "qout", "qcrow", "qbcw", "qw",
           "qilb_s", "qilb_g", "qilc_sc", "qilc_s2")
UNORDERED = {"qilb_s", "qilb_g", "qilc_sc", "qilc_s2"}


class _Tok:
    __slots__ = ("eng", "idx", "marked", "count_after")

    def __init__(self, eng, idx):
        self.eng = eng
        self.idx = idx
        self.marked = False
        self.count_after = None


class _Builder:

    def __init__(self):
        self.ops = {e: [] for e in ENGINES}
        self.stream_n = {q: 0 for q in STREAMS}

    def emit(self, eng, fn, waits=(), dma=None):
        tok = _Tok(eng, len(self.ops[eng]))
        wl = {}
        for w in waits:
            if w is None:
                continue
            k = w.eng
            if k not in wl or w.idx > wl[k].idx:
                wl[k] = w
        for w in wl.values():
            w.marked = True
        prewait = None
        if dma is not None:
            tok.eng = dma
            tok.idx = self.stream_n[dma]
            tok.marked = True
            tok.count_after = 16 * (self.stream_n[dma] + 1)
            if self.stream_n[dma] > 0 and dma not in UNORDERED:
                prewait = 16 * self.stream_n[dma]
            self.stream_n[dma] += 1
        self.ops[eng].append({"fn": fn, "waits": list(wl.values()), "dma": dma,
                              "tok": tok, "prewait": prewait})
        return tok

    def resolve(self):
        for e in ENGINES:
            c = 0
            for op in self.ops[e]:
                if op["dma"] is not None:
                    continue
                if op["tok"].marked:
                    c += 1
                op["tok"].count_after = c

    def run(self, eng_obj, name, sems):
        wm = {}

        def w(key, val):
            if wm.get(key, -1) >= val:
                return
            eng_obj.wait_ge(sems[key], val)
            wm[key] = val

        for op in self.ops[name]:
            if op["prewait"] is not None:
                w(op["dma"], op["prewait"])
            for wt in op["waits"]:
                w(wt.eng, wt.count_after)
            inst = op["fn"](eng_obj)
            if op["dma"] is not None:
                inst.then_inc(sems[op["dma"]], 16)
            elif op["tok"].marked:
                inst.then_inc(sems[op["tok"].eng], 1)


def build_program():
    nc = bass.Bass("TRN2", target_bir_lowering=False)
    X = nc.dram_tensor("x", [L, D], F32, kind="ExternalInput").ap()
    WDBC = nc.dram_tensor("w_dbc", [96, D], F32, kind="ExternalInput").ap()
    WDT = nc.dram_tensor("w_dt", [D, 64], F32, kind="ExternalInput").ap()
    BDT = nc.dram_tensor("b_dt", [D], F32, kind="ExternalInput").ap()
    ALOG = nc.dram_tensor("a_log", [D, S], F32, kind="ExternalInput").ap()
    DPAR = nc.dram_tensor("d_par", [D], F32, kind="ExternalInput").ap()
    WINT = nc.dram_tensor("w_int", [D, D], I32, kind="ExternalInput").ap()
    WSC = nc.dram_tensor("w_sc", [1], F32, kind="ExternalInput").ap()
    FBIAS = nc.dram_tensor("f_bias", [D], F32, kind="ExternalInput").ap()
    Y = nc.dram_tensor("y", [L, D], F32, kind="ExternalOutput").ap()
    BCSCR = nc.dram_tensor("bcscr", [32, T], BF16, kind="Internal").ap()

    from contextlib import ExitStack
    with ExitStack() as ctx:
        def sb(name, shape, dt):
            return ctx.enter_context(nc.sbuf_tensor(name, shape, dt))

        xin = sb("xin", [128, 4096], F32)
        xntm = sb("xntm", [128, 2048], F32)
        scr = sb("scr", [128, 2048], F32)
        spbuf = sb("spbuf", [128, 1024], F32)
        preg = sb("preg", [128, 1024], F32)
        sgbuf = sb("sgbuf", [128, 1024], F32)
        gbuf = sb("gbuf", [128, 1024], F32)
        sfm = sb("sfm", [128, 1536], F32)
        xnfm = sb("xnfm", [128, 4096], F32R)
        xqfm = sb("xqfm", [128, 4096], F16)
        delta = sb("delta", [128, 4096], BF16)
        ubuf = sb("ubuf", [128, 4096], BF16)
        drones = sb("drones", [65, T], F32R)
        btctbf = sb("btctbf", [32, T], BF16)
        bcil = sb("bcil", [128, 32 * T], BF16)
        apool = sb("apool", [128, 3 * T], F32)
        bpool = sb("bpool", [128, 4 * T], BF16)
        dxnbuf = sb("dxnbuf", [128, 2 * T], BF16)
        hall = sb("hall", [128, 16 * T], BF16)
        ppool = sb("ppool", [128, 2 * 1024], BF16)
        wffn = sb("wffn", [128, 8192], F16)
        wdbc = sb("wdbc", [128, 768], F32R)
        wdt = sb("wdt", [65, 1024], F32R)
        acol = sb("acol", [128, 128], F32)
        dcol = sb("dcol", [128, 8], F32)
        biascol = sb("biascol", [128, 8], F32)
        cscol = sb("cscol", [128, 8], F32)
        ccols = sb("ccols", [128, 4], F32)
        wsrep = sb("wsrep", [128, 1], F32)
        bnbuf = sb("bnbuf", [128, 48], F32)
        mv = sb("mv", [128, 8], F32)
        cells = sb("cells", [128, 32], F32)
        eps5 = sb("eps5", [128, 1], F32)
        eps6 = sb("eps6", [128, 1], F32)
        onecol = sb("onecol", [128, 1], F32)
        ident32 = sb("ident32", [128, 128], F32)
        ident16 = sb("ident16", [128, 128], F16)
        identbf = sb("identbf", [128, 128], BF16)
        onesrow = sb("onesrow", [1, 128], F32R)
        onescol16 = sb("onescol16", [128, 1], F16)
        crow = sb("crow", [1, T], F32R)
        crepsb = sb("crepsb", [128, T], F32)
        statebuf = sb("statebuf", [128, 256], F32)
        xresm = sb("xresm", [128, 1024], F32)
        outm = sb("outm", [128, 1024], F32)

        ps = [ctx.enter_context(nc.psum_tensor(f"ps{i}", [128, 512], F32))
              for i in range(8)]

        sems = {}
        for e in ENGINES + STREAMS:
            sems[e] = ctx.enter_context(nc.semaphore(f"sem_{e}"))
        block = ctx.enter_context(nc.Block())

        B = _Builder()
        E = B.emit

        def cell(j, i):
            return cells[:, j * 8 + i:j * 8 + i + 1]

        reg_i32 = xin[:, 0:1024].bitcast(I32)
        reg_f32 = xin[:, 1024:2048]
        reg_dbc = xin[0:96, 2048:3072]
        reg_wdt = xin[:, 3072:3584]
        reg_alog = xin[:, 3584:3712]

        t_gset = E("gpsimd", lambda g: g.memset(ident32[:], 1.0))
        t_gset = E("gpsimd", lambda g: g.affine_select(
            out=ident32[:], in_=ident32[:], pattern=[[1, 128]],
            compare_op=ALU.is_equal, fill=0.0, base=0, channel_multiplier=-1),
            waits=[t_gset])
        t_g16 = E("gpsimd", lambda g: g.memset(ident16[:], 1.0))
        t_g16 = E("gpsimd", lambda g: g.affine_select(
            out=ident16[:], in_=ident16[:], pattern=[[1, 128]],
            compare_op=ALU.is_equal, fill=0.0, base=0, channel_multiplier=-1),
            waits=[t_g16])
        t_gbf = E("gpsimd", lambda g: g.memset(identbf[:], 1.0))
        t_gbf = E("gpsimd", lambda g: g.affine_select(
            out=identbf[:], in_=identbf[:], pattern=[[1, 128]],
            compare_op=ALU.is_equal, fill=0.0, base=0, channel_multiplier=-1),
            waits=[t_gbf])
        t_eps5 = E("gpsimd", lambda g: g.memset(eps5[:], 1e-5))
        t_eps6 = E("gpsimd", lambda g: g.memset(eps6[:], 1e-6))
        t_one = E("gpsimd", lambda g: g.memset(onecol[:], 1.0))
        t_orow0 = E("gpsimd", lambda g: g.memset(scr[0:1, 0:640], 1.0))
        t_orow = E("vector", lambda v: v.tensor_copy(
            out=onesrow[:], in_=scr[0:1, 0:128]), waits=[t_orow0])
        t_oc16 = E("gpsimd", lambda g: g.memset(onescol16[:], 1.0))
        t_drow = E("vector", lambda v: v.tensor_copy(
            out=drones[64:65, :], in_=scr[0:1, 128:640]), waits=[t_orow0])
        t_state = E("gpsimd", lambda g: g.memset(statebuf[:], 0.0))

        t_dcol = E("sync", lambda s: s.dma_start(
            out=dcol[:], in_=DPAR.rearrange("(k p) -> p k", p=128),
            allow_slow_non_contiguous=True), dma="qw")
        t_bias = E("sync", lambda s: s.dma_start(
            out=biascol[:], in_=FBIAS.rearrange("(k p) -> p k", p=128),
            allow_slow_non_contiguous=True), dma="qw")
        t_bdt = E("sync", lambda s: s.dma_start(
            out=wdt[64:65, :], in_=BDT.rearrange("(o d) -> o d", o=1).bitcast(F32R)), dma="qw")
        t_aload = E("sync", lambda s: s.dma_start(
            out=reg_alog.rearrange("p (k s) -> p k s", k=NK),
            in_=ALOG.rearrange("(k p) s -> p k s", p=128)), dma="qw")
        t_dbcl = E("sync", lambda s: s.dma_start(out=reg_dbc, in_=WDBC), dma="qw")
        t_wdtl = E("sync", lambda s: s.dma_start(
            out=reg_wdt.rearrange("p (k r) -> p k r", k=NK),
            in_=WDT.rearrange("(k p) r -> p k r", p=128)), dma="qw")

        t_wsrep_dma = E("sync", lambda s: s.dma_start(
            out=wsrep[:], in_=WSC.rearrange("(o d) -> o d", o=1)
            .broadcast_to([128, 1])), dma="qw")
        t_wsrep = t_wsrep_dma

        t_ae = E("scalar", lambda s: s.activation(
            out=scr[:, 0:128], in_=reg_alog, func=AF.Exp),
            waits=[t_aload, t_orow])
        t_acol = E("vector", lambda v: v.tensor_scalar(
            out=acol[:], in0=scr[:, 0:128], scalar1=-1.0, scalar2=None,
            op0=ALU.mult), waits=[t_ae])

        t_dbc_ev = [None] * NK
        for k in range(NK):
            bank = ps[k % 2]
            war = t_dbc_ev[k - 2] if k >= 2 else None
            t_tr = E("tensor", lambda t, k=k, bank=bank: t.transpose(
                out=bank[:, 0:96], in_=reg_dbc[:, k * 128:(k + 1) * 128],
                identity=ident32[0:96, 0:96]),
                waits=[t_dbcl, t_gset, war])
            t_dbc_ev[k] = E("vector", lambda v, k=k, bank=bank: v.tensor_copy(
                out=wdbc[:, k * 96:(k + 1) * 96], in_=bank[:, 0:96]), waits=[t_tr])

        t_wdt_ev = [None] * NK
        for k in range(NK):
            bank = ps[2 + k % 2]
            war = t_wdt_ev[k - 2] if k >= 2 else None
            t_tr = E("tensor", lambda t, k=k, bank=bank: t.transpose(
                out=bank[0:64, 0:128], in_=reg_wdt[:, k * 64:(k + 1) * 64],
                identity=ident32[:]),
                waits=[t_wdtl, t_gset, war])
            t_wdt_ev[k] = E("vector", lambda v, k=k, bank=bank: v.tensor_copy(
                out=wdt[0:64, k * 128:(k + 1) * 128], in_=bank[0:64, 0:128]),
                waits=[t_tr])

        t_wload = None
        t_wcopy = None
        t_wev_last = None
        t_wtr_last = [None, None]
        for m in range(NK):
            t_wload = E("sync", lambda s, m=m: s.dma_start(
                out=reg_i32, in_=WINT[m * 128:(m + 1) * 128, :]),
                dma="qw", waits=[t_wcopy])
            t_wcopy = E("vector", lambda v: v.tensor_copy(out=reg_f32, in_=reg_i32),
                        waits=[t_wload, t_wtr_last[1] or t_wtr_last[0]])
            for half in range(2):
                bank = ps[4 + half]
                trs = []
                for kk in range(4):
                    k = half * 4 + kk
                    trs.append(E("tensor", lambda t, k=k, kk=kk, bank=bank: t.transpose(
                        out=bank[:, kk * 128:(kk + 1) * 128],
                        in_=reg_f32[:, k * 128:(k + 1) * 128],
                        identity=ident32[:]),
                        waits=[t_wcopy, t_wev_last, t_gset]))
                t_wtr_last[half] = trs[-1]
                dst = wffn[:].rearrange("p (k o) -> p k o", k=NK)[
                    :, half * 4:(half + 1) * 4, m * 128:(m + 1) * 128]
                t_wev_last = E("vector", lambda v, bank=bank, dst=dst: v.tensor_copy(
                    out=dst, in_=bank[:].rearrange("p (a b) -> p a b", a=4)),
                    waits=[trs[-1]])

        t_cs = [None] * NK
        for m in range(NK):
            mm_last = None
            for k in range(NK):
                mm_last = E("tensor", lambda t, m=m, k=k: t.matmul(
                    out=ps[6][:, 0:1],
                    lhsT=wffn[:, k * 1024 + m * 128:k * 1024 + (m + 1) * 128],
                    rhs=onescol16[:], start=(k == 0), stop=(k == NK - 1)),
                    waits=[t_wev_last, t_oc16, t_cs[m - 1] if m else None])
            t_cs[m] = E("vector", lambda v, m=m: v.tensor_scalar(
                out=cscol[:, m:m + 1], in0=ps[6][:, 0:1], scalar1=-1536.0,
                scalar2=None, op0=ALU.mult), waits=[mm_last])

        t_setup_w = t_wev_last
        t_setup_x = t_cs[NK - 1]
        t_setup_pe = t_wtr_last[1]
        t_scr_set = t_acol
        t_evac_bank_init = [t_dbc_ev[NK - 2], t_dbc_ev[NK - 1]]

        t_lnts = [[None] * NJ for _ in range(NCH + 1)]
        t_xin_dma = [None] * NJ
        t_xn_ev = None
        t_xq_ev = None
        t_state_gather = None
        t_sfm_roll = [t_wsrep, t_wsrep, t_wsrep]
        t_sg_slot = [None, None, None]
        t_preg_roll = [t_wev_last, t_cs[NK - 1]]
        t_silu_roll = [None, None]
        t_sgw_roll = [None, None]
        t_gbuf_roll = [None, None]
        ps34_reader = [t_wev_last, t_wev_last]
        t_sp_roll = [None, None]
        t_xqslot = [None] * NJ
        t_scrslot = [None, None]
        t_evac_bank_roll = list(t_evac_bank_init)
        t_crow_prev = [None] * NJ
        t_quant_j = [None] * NJ
        t_ln_tok = [None] * NJ
        t_cc_next = None
        t_dxn = [None] * NK
        t_yred_of = {}
        R = {"dr": None, "bt": None, "bcw": None, "crepsb": None,
             "dmm_last": None, "ffn_last": None, "tcc": None,
             "il_last": [], "ilB": [], "ilC": [],
             "tb_last": None, "tym_last_vec": None, "tym_last_g": None,
             "bcmm": None, "pend_ev": None, "pre0": None}
        t_outblk_m = [None] * (NK + 2)
        t_outdma_m = [None] * (NK + 2)
        t_xres_m = [None] * (NK + 2)
        t_scan_hist = {}
        t_scan_all_prev = None
        t_ymult_last_prev_chunk = None
        t_dbc_done_prev = None
        t_crepsb = None
        t_yp_reader_prev = t_wsrep

        for c in range(NCH):
            par = c % 2
            cur0 = par * 128
            nxt0 = (1 - par) * 128

            def ph1a(cc):
                for j in range(NJ):
                    w = ([t_lnts[cc - 1][j], t_quant_j[j]] if cc > 0 else
                         [t_setup_pe, t_setup_x, t_scr_set, t_ae])
                    t_xin_dma[j] = E("sync", lambda s, cc=cc, j=j: s.dma_start(
                        out=xin[:, j * 1024:(j + 1) * 1024],
                        in_=X[cc * T + j * 128:cc * T + (j + 1) * 128, :]),
                        dma="qxin", waits=w)
                tag = None
                for j in range(NJ):
                    E("vector", lambda v, j=j: v.bn_stats(
                        out=bnbuf[:, j * 12:j * 12 + 6],
                        in_=xin[:, j * 1024:j * 1024 + 512]), waits=[t_xin_dma[j]])
                    tb2 = E("vector", lambda v, j=j: v.bn_stats(
                        out=bnbuf[:, j * 12 + 6:j * 12 + 12],
                        in_=xin[:, j * 1024 + 512:j * 1024 + 1024]),
                        waits=[t_xin_dma[j]])
                    tag = E("vector", lambda v, j=j: v.bn_aggr(
                        out=mv[:, j * 2:j * 2 + 2],
                        in_=bnbuf[:, j * 12:j * 12 + 12]), waits=[tb2])
                mv3 = mv[:].rearrange("p (j two) -> p j two", two=2)
                tcc_l = [None, None]
                for pair in range(2):
                    j0 = pair * 2
                    vars2 = mv3[:, j0:j0 + 2, 1:2].squeeze(2)
                    means2 = mv3[:, j0:j0 + 2, 0:1].squeeze(2)
                    cl = lambda a, b, pair=pair: cells[:, pair * 16 + a:
                                                       pair * 16 + b]
                    tl2 = E("scalar", lambda s, vars2=vars2, cl=cl: s.activation(
                        out=cl(0, 2), in_=vars2, func=AF.Ln, bias=eps5[:, 0:1]),
                        waits=[tag, t_eps5])
                    tr2_ = E("scalar", lambda s, cl=cl: s.activation(
                        out=cl(2, 4), in_=cl(0, 2), func=AF.Exp, scale=-0.5),
                        waits=[tl2])
                    tn2 = E("vector", lambda v, means2=means2, cl=cl:
                            v.scalar_tensor_tensor(
                                out=cl(4, 6), in0=means2, scalar=-1.0,
                                in1=cl(2, 4), op0=ALU.mult, op1=ALU.mult),
                            waits=[tr2_])
                    t_am_l = [None, None]
                    for jj in range(2):
                        j = j0 + jj
                        jr = j % 2
                        t_ln = E("vector", lambda v, j=j, jj=jj, cl=cl:
                                 v.tensor_scalar(
                                     out=xin[:, j * 1024:(j + 1) * 1024],
                                     in0=xin[:, j * 1024:(j + 1) * 1024],
                                     scalar1=cl(2 + jj, 3 + jj),
                                     scalar2=cl(4 + jj, 5 + jj),
                                     op0=ALU.mult, op1=ALU.add),
                                 waits=[tn2])
                        t_ln_tok[j] = t_ln
                        t_am_l[jj] = E("vector", lambda v, j=j, jj=jj, cl=cl:
                                       v.reduce_max(
                                           out=cl(6 + jj, 7 + jj),
                                           in_=xin[:, j * 1024:(j + 1) * 1024],
                                           axis=AX.X,
                                           apply_absolute_value=True),
                                       waits=[t_ln])
                    tr22 = E("vector", lambda v, cl=cl: v.tensor_mul(
                        out=cl(8, 10), in0=cl(2, 4), in1=cl(2, 4)), waits=[tn2])
                    tms = E("vector", lambda v, vars2=vars2, cl=cl:
                            v.tensor_tensor(
                                out=cl(0, 2), in0=vars2, in1=cl(8, 10),
                                op=ALU.mult), waits=[tr22, tl2])
                    tlm = E("scalar", lambda s, cl=cl: s.activation(
                        out=cl(0, 2), in_=cl(0, 2), func=AF.Ln,
                        bias=eps6[:, 0:1]), waits=[tms, t_eps6])
                    trq = E("scalar", lambda s, cl=cl: s.activation(
                        out=cl(10, 12), in_=cl(0, 2), func=AF.Exp, scale=-0.5),
                        waits=[tlm])
                    tamn = E("vector", lambda v, cl=cl: v.tensor_mul(
                        out=cl(12, 14), in0=cl(6, 8), in1=cl(10, 12)),
                        waits=[trq, t_am_l[1]])
                    tamc = E("vector", lambda v, cl=cl: v.tensor_scalar(
                        out=cl(12, 14), in0=cl(12, 14), scalar1=1e-5,
                        scalar2=None, op0=ALU.max), waits=[tamn])
                    trec = E("vector", lambda v, cl=cl: v.reciprocal(
                        out=cl(0, 2), in_=cl(12, 14)), waits=[tamc])
                    tqc2 = E("vector", lambda v, cl=cl: v.scalar_tensor_tensor(
                        out=cl(14, 16), in0=cl(0, 2), scalar=127.0,
                        in1=cl(10, 12), op0=ALU.mult, op1=ALU.mult),
                        waits=[trec])
                    tcc_l[pair] = E("vector", lambda v, j0=j0, cl=cl:
                                    v.scalar_tensor_tensor(
                                        out=ccols[:, j0:j0 + 2], in0=cl(12, 14),
                                        scalar=1.0 / 127.0,
                                        in1=wsrep[:, 0:1].broadcast_to([128, 2]),
                                        op0=ALU.mult, op1=ALU.mult),
                                    waits=[tqc2, t_wsrep, t_crow_prev[j0],
                                           t_crow_prev[j0 + 1]])
                    for jj in range(2):
                        j = j0 + jj
                        jr = j % 2
                        tq1 = E("vector", lambda v, j=j, jr=jr, jj=jj, cl=cl:
                                v.tensor_scalar(
                                    out=scr[:, jr * 1024:(jr + 1) * 1024],
                                    in0=xin[:, j * 1024:(j + 1) * 1024],
                                    scalar1=cl(14 + jj, 15 + jj),
                                    scalar2=1536.0,
                                    op0=ALU.mult, op1=ALU.add),
                                waits=[tqc2])
                        t_quant_j[j] = E("vector", lambda v, j=j, jr=jr:
                                         v.tensor_scalar(
                                             out=xntm[:].bitcast(F16)[
                                                 :, j * 1024:(j + 1) * 1024],
                                             in0=scr[:, jr * 1024:
                                                     (jr + 1) * 1024],
                                             scalar1=1408.0, scalar2=1663.0,
                                             op0=ALU.max, op1=ALU.min),
                                        waits=[tq1, t_xqslot[j]])
                return tcc_l

            def ph1b_xn(cc, extra_w):
                nonlocal t_xn_ev
                for j in range(NJ):
                    jr = j % 2
                    t_ln = t_ln_tok[j]
                    for half in range(2):
                        bank = ps[half]
                        trs_last = None
                        for kk in range(4):
                            k = half * 4 + kk
                            trs_last = E("tensor", lambda t, j=j, k=k, kk=kk,
                                         bank=bank: t.transpose(
                                             out=bank[:, kk * 128:
                                                      (kk + 1) * 128],
                                             in_=xin[:, j * 1024 + k * 128:
                                                     j * 1024 + (k + 1) * 128],
                                             identity=ident32[:]),
                                         waits=[t_ln, t_evac_bank_roll[half],
                                                t_gset])
                        dst = xnfm[:].rearrange("p (k t2) -> p k t2", k=NK)[
                            :, half * 4:(half + 1) * 4, j * 128:(j + 1) * 128]
                        t_evac_bank_roll[half] = E(
                            "scalar", lambda sc, bank=bank, dst=dst: sc.copy(
                                out=dst,
                                in_=bank[:].rearrange("p (a b) -> p a b", a=4)),
                            waits=[trs_last] + extra_w)
                        t_xn_ev = t_evac_bank_roll[half]
                        t_lnts[cc][j] = trs_last

            def ph1b_xq(cc, extra_w):
                nonlocal t_xq_ev
                for j in range(NJ):
                    tq16 = t_quant_j[j]
                    xq_trs = None
                    for half in range(2):
                        bank = ps[half]
                        trs_last = None
                        for kk in range(4):
                            k = half * 4 + kk
                            src = xntm[:].bitcast(F16)[
                                :, j * 1024 + k * 128:j * 1024 + (k + 1) * 128]
                            trs_last = E("tensor", lambda t, src=src, kk=kk,
                                         bank=bank: t.transpose(
                                             out=bank[:].bitcast(F16)[
                                                 :, kk * 128:(kk + 1) * 128],
                                             in_=src, identity=ident16[:]),
                                         waits=[tq16, t_evac_bank_roll[half],
                                                t_g16])
                        xq_trs = trs_last
                        dst = xqfm[:].rearrange("p (k t2) -> p k t2", k=NK)[
                            :, half * 4:(half + 1) * 4, j * 128:(j + 1) * 128]
                        t_evac_bank_roll[half] = E(
                            "scalar", lambda sc, bank=bank, dst=dst: sc.copy(
                                out=dst,
                                in_=bank[:].bitcast(F16)[:, 0:512].rearrange(
                                    "p (a b) -> p a b", a=4)),
                            waits=[trs_last] + extra_w)
                        t_xq_ev = t_evac_bank_roll[half]
                    t_xqslot[j] = xq_trs

            def ph2a1_mm(cc):
                mm_last = None
                for k in range(NK):
                    w = [t_xn_ev, t_dbc_ev[NK - 1]]
                    if k == 0:
                        w += [R["crepsb"] if cc > 0 else t_wdt_ev[NK - 2]]
                    mm_last = E("tensor", lambda t, k=k: t.matmul(
                        out=ps[2][0:96, :], lhsT=wdbc[:, k * 96:(k + 1) * 96],
                        rhs=xnfm[:, k * T:(k + 1) * T],
                        start=(k == 0), stop=(k == NK - 1)), waits=w)
                R["bcmm"] = mm_last

            def ph2a1_ev(cc):
                mm_last = R["bcmm"]
                R["dr"] = E("vector", lambda v: v.tensor_copy(
                    out=drones[0:64, :], in_=ps[2][0:64, :]),
                    waits=[mm_last, t_drow, R["dmm_last"]])
                R["bt"] = E("vector", lambda v: v.tensor_copy(
                    out=btctbf[0:32, :], in_=ps[2][64:96, :]),
                    waits=[mm_last, R["bcw"]])
                R["bcw"] = E("sync", lambda sy: sy.dma_start(
                    out=BCSCR, in_=btctbf[0:32, :]),
                    dma="qbcw", waits=[R["bt"]] + R["il_last"])

            def ph2a1_ilB(cc):
                wB = [R["bcw"]] + ([R["tb_last"]] if cc > 0 else [])
                ilB = [None, None]
                for i in range(8):
                    for half in range(2):
                        r = 2 * i + half
                        qe, qs = (("sync", "qilb_s") if half == 0 else
                                  ("gpsimd", "qilb_g"))
                        ilB[half] = E(qe, lambda q, r=r: q.dma_start(
                            out=bcil[:, r * T:(r + 1) * T],
                            in_=BCSCR[r:r + 1, :].broadcast_to([128, T])),
                            dma=qs, waits=wB)
                R["ilB"] = ilB

            def ph2a1_ilC(cc):
                wC = [R["bcw"]] + ([R["tym_last_vec"], R["tym_last_g"]]
                                   if cc > 0 else [])
                ilC = [None, None]
                for i in range(8):
                    for half in range(2):
                        rc = 16 + 2 * i + half
                        qe, qs = (("scalar", "qilc_sc") if half == 0 else
                                  ("sync", "qilc_s2"))
                        ilC[half] = E(qe, lambda q, rc=rc: q.dma_start(
                            out=bcil[:, rc * T:(rc + 1) * T],
                            in_=BCSCR[rc:rc + 1, :].broadcast_to([128, T])),
                            dma=qs, waits=wC)
                R["ilC"] = ilC
                R["il_last"] = R["ilB"] + ilC

            def ph2a2(cc, tcc_pair):
                t_crow = [None] * NJ
                for j in range(NJ):
                    t_crow[j] = E("sync", lambda s, j=j: s.dma_start(
                        out=crow[0:1, j * 128:(j + 1) * 128],
                        in_=ccols[:, j:j + 1].bitcast(F32R),
                        allow_slow_non_contiguous=True),
                        dma="qcrow",
                        waits=[tcc_pair[0] if j < 2 else tcc_pair[1]])
                    t_crow_prev[j] = t_crow[j]
                t_crep = E("tensor", lambda t: t.matmul(
                    out=ps[2][:], lhsT=onesrow[0:1, :], rhs=crow[0:1, :],
                    start=True, stop=True),
                    waits=[t_crow[NJ - 1], R["bt"], R["dr"], t_orow])
                R["crepsb"] = E("vector", lambda v: v.tensor_copy(
                    out=crepsb[:], in_=ps[2][:]),
                    waits=[t_crep, t_preg_roll[0], t_preg_roll[1]])

            if c == 0:
                tcc_pair0 = ph1a(0)
                ph1b_xn(0, [])
                ph1b_xq(0, [])
                ph2a1_mm(0)
                ph2a1_ev(0)
                ph2a1_ilB(0)
                ph2a1_ilC(0)
                R["tcc"] = tcc_pair0
                ph2a2(0, R["tcc"])
            t_crepsb = R["crepsb"]
            t_dr = R["dr"]

            t_dmm_k = [None] * NK
            t_ta3 = [None] * NK
            t_deltaSTT = [None] * NK
            t_u = [None] * NK
            t_state_k = [None] * NK
            t_sfm_tok = [None] * NK
            t_tpg_m = [None] * NK
            t_ymult_g7 = [None] * NK
            scan_last_of_k = [None] * NK
            scan_toks = [None] * NK
            pend_sfm = [None]

            def ph2b_dm(k):
                bank = ps[3 + k % 2]
                t_dmm = E("tensor", lambda t, k=k, bank=bank: t.matmul(
                    out=bank[:], lhsT=wdt[0:65, k * 128:(k + 1) * 128],
                    rhs=drones[0:65, :], start=True, stop=True),
                    waits=[R["dr"], t_wdt_ev[NK - 1], t_bdt,
                           ps34_reader[k % 2]])
                t_dmm_k[k] = t_dmm
                R["dmm_last"] = t_dmm

            def ph2b_act(k):
                bank = ps[3 + k % 2]
                t_dmm = t_dmm_k[k]
                sp0 = spbuf[:, 0:512]
                sp1 = spbuf[:, 512:1024]
                ta1 = E("scalar", lambda s, bank=bank: s.activation(
                    out=sp0, in_=bank[:], func=AF.Abs),
                    waits=[t_dmm, t_sp_roll[0]])
                ta2 = E("scalar", lambda s: s.activation(
                    out=sp1, in_=sp0, func=AF.Exp, scale=-1.0),
                    waits=[ta1, t_sp_roll[1]])
                ta3 = E("scalar", lambda s: s.activation(
                    out=sp0, in_=sp1, func=AF.Ln, bias=onecol[:, 0:1]),
                    waits=[ta2, t_one])
                t_sp_roll[1] = ta3
                t_ta3[k] = ta3

            def ph2b_tail(k):
                bank = ps[3 + k % 2]
                t_deltaSTT[k] = E("vector", lambda v, k=k, bank=bank:
                                  v.scalar_tensor_tensor(
                                      out=delta[:, k * T:(k + 1) * T],
                                      in0=bank[:], scalar=0.0, in1=sp0_ap(),
                                      op0=ALU.max, op1=ALU.add),
                                  waits=[t_ta3[k]])
                t_sp_roll[0] = t_deltaSTT[k]
                ps34_reader[k % 2] = t_deltaSTT[k]
                t_u[k] = E("gpsimd", lambda g, k=k: g.tensor_tensor(
                    out=ubuf[:, k * T:(k + 1) * T],
                    in0=delta[:, k * T:(k + 1) * T],
                    in1=xnfm[:, k * T:(k + 1) * T].bitcast(F32),
                    op=ALU.mult), waits=[t_deltaSTT[k]])

            def sp0_ap():
                return spbuf[:, 0:512]

            def ph2b_head(k):
                ph2b_dm(k)
                ph2b_act(k)

            def ph2b(k):
                ph2b_head(k)
                ph2b_tail(k)

            def ph2b_dxn(k):
                t_dxn[k] = E("vector", lambda v, k=k: v.tensor_scalar(
                    out=dxnbuf[:, (k % 2) * T:(k % 2 + 1) * T],
                    in0=xnfm[:, k * T:(k + 1) * T].bitcast(F32),
                    scalar1=dcol[:, k:k + 1], scalar2=None, op0=ALU.mult),
                    waits=[t_xn_ev, t_dcol,
                           t_yred_of.get((k - 2) % NK)])

            def emit_exp(k, s_):
                i = k * S + s_
                ia = i % 3
                wa = [t_deltaSTT[k], t_acol]
                prev_a_user = t_scan_hist.get(("a", ia))
                if prev_a_user is not None:
                    wa.append(prev_a_user)
                return E("scalar", lambda s, k=k, s_=s_, ia=ia:
                         s.activation(
                             out=apool[:, ia * T:(ia + 1) * T],
                             in_=delta[:, k * T:(k + 1) * T],
                             func=AF.Exp,
                             scale=acol[:, k * S + s_:k * S + s_ + 1]),
                         waits=wa)

            def emit_scans(k, a_toks):
                nonlocal t_scan_all_prev
                scan_toks[k] = [None] * S
                tb = None
                for s_ in range(S):
                    a_toks.append(emit_exp(k, s_))
                    i = k * S + s_
                    ia = i % 3
                    if s_ % 2 == 0:
                        ibp = (i // 2) % 2
                        wb = [t_u[k]] + R["ilB"]
                        prev_b_user = t_scan_hist.get(("b", ibp))
                        if prev_b_user is not None:
                            wb.append(prev_b_user)
                        usrc = ubuf[:, k * T:(k + 1) * T].unsqueeze(1)\
                            .broadcast_to([128, 2, T])
                        tb = E("vector", lambda v, k=k, s_=s_, ibp=ibp,
                               usrc=usrc: v.tensor_tensor(
                                   out=bpool[:, ibp * 2 * T:(ibp + 1) * 2 * T]
                                   .rearrange("p (a b) -> p a b", a=2),
                                   in0=usrc,
                                   in1=bcil[:, s_ * T:(s_ + 2) * T]
                                   .rearrange("p (a b) -> p a b", a=2),
                                   op=ALU.mult), waits=wb)
                        R["tb_last"] = tb
                    ibp = (i // 2) % 2
                    boff = ibp * 2 * T + (s_ % 2) * T
                    ws = [a_toks[s_], tb, t_state]
                    if s_ == 0:
                        ws.append(t_state_k[k - 1] if k > 0 else t_state_gather)
                    off = cur0 + k * S + s_
                    tsc = E("vector", lambda v, s_=s_, ia=ia, boff=boff,
                            off=off: v.tensor_tensor_scan(
                                out=hall[:, s_ * T:(s_ + 1) * T],
                                data0=apool[:, ia * T:(ia + 1) * T],
                                data1=bpool[:, boff:boff + T],
                                initial=statebuf[:, off:off + 1],
                                op0=ALU.mult, op1=ALU.add), waits=ws)
                    t_scan_hist[("a", ia)] = tsc
                    if s_ % 2 == 1:
                        t_scan_hist[("b", ibp)] = tsc
                    scan_toks[k][s_] = tsc
                    if s_ == S - 1:
                        scan_last_of_k[k] = tsc
                        t_scan_all_prev = tsc
                    if s_ == 1 and k + 1 < NK:
                        ph2b_dxn(k + 1)
                    if s_ == 2 and pend_sfm[0] != 0:
                        emit_sfm()
                    if s_ == 8:
                        emit_sfm()
                    if s_ == 10 and R["pend_ev"] is not None:
                        cc2 = R["pend_ev"]
                        R["pend_ev"] = None
                        ph2a1_ev(cc2)

            def emit_ytail(k):
                nonlocal t_yp_reader_prev
                nxt_ok = (k + 1 < NK) and not (c > 0 and k + 1 <= 2)
                if nxt_ok:
                    ph2b_head(k + 1)
                t_state_k[k] = E("gpsimd", lambda g, k=k: g.tensor_copy(
                    out=statebuf[:, nxt0 + k * S:nxt0 + (k + 1) * S],
                    in_=hall[:].rearrange("p (s t2) -> p s t2", s=S)
                    [:, :, T - 1:T].squeeze(2)), waits=[scan_last_of_k[k]])
                t_yred = None
                for g in range(8):
                    ip = g % 2
                    eng_y = "gpsimd" if g == 7 else "vector"
                    wy = [scan_toks[k][2 * g + 1]] + R["ilC"]
                    prev_p_user = t_scan_hist.get(("p", ip))
                    if prev_p_user is not None:
                        wy.append(prev_p_user)
                    tym = E(eng_y, lambda e, g=g, ip=ip: e.tensor_tensor(
                        out=ppool[:, ip * 1024:(ip + 1) * 1024],
                        in0=hall[:, g * 1024:(g + 1) * 1024],
                        in1=bcil[:, (16 + 2 * g) * T:(16 + 2 * g + 2) * T],
                        op=ALU.mult), waits=wy)
                    if g == 7:
                        t_ymult_g7[k] = tym
                        R["tym_last_g"] = tym
                    else:
                        R["tym_last_vec"] = tym
                    tok = None
                    for half in range(2):
                        w = [tym, t_gbf]
                        if g == 0 and half == 0:
                            w.append(t_yp_reader_prev)
                        tok = E("tensor", lambda t, ip=ip, half=half, g=g:
                                t.matmul(
                                    out=ps[7][:],
                                    lhsT=identbf[:],
                                    rhs=ppool[:, ip * 1024 + half * 512:
                                              ip * 1024 + (half + 1) * 512],
                                    start=(g == 0 and half == 0),
                                    stop=False), waits=w)
                    t_scan_hist[("p", ip)] = tok
                    t_yred = tok
                t_yred = E("tensor", lambda t, k=k: t.matmul(
                    out=ps[7][:], lhsT=identbf[:],
                    rhs=dxnbuf[:, (k % 2) * T:(k % 2 + 1) * T],
                    start=False, stop=True), waits=[t_dxn[k], t_yred])
                t_yred_of[k] = t_yred
                pend_sfm[0] = k
                if nxt_ok:
                    ph2b_tail(k + 1)

            def emit_sfm():
                nonlocal t_yp_reader_prev
                k = pend_sfm[0]
                if k is None:
                    return
                pend_sfm[0] = None
                sl = k % 3
                t_sfm_tok[k] = E("vector", lambda v, k=k, sl=sl: v.tensor_copy(
                    out=sfm[:, sl * T:(sl + 1) * T], in_=ps[7][:]),
                    waits=[t_yred_of[k], t_sg_slot[sl]])
                t_yp_reader_prev = t_sfm_tok[k]

            def ph4a(m):
                rot = m % 2
                bank = ps[5 + rot]
                xre = X[c * T:(c + 1) * T, m * 128:(m + 1) * 128].rearrange(
                    "(j p) w -> p j w", p=128)
                t_xres_m[m] = E("sync", lambda s, rot=rot, xre=xre: s.dma_start(
                    out=xresm[:, rot * 512:(rot + 1) * 512].rearrange(
                        "p (j w) -> p j w", j=NJ), in_=xre),
                    dma="qxres", waits=[t_outblk_m[m - 2]])
                mm = None
                for k2 in range(NK):
                    w = [t_xq_ev, t_setup_w]
                    if k2 == 0:
                        w += [t_preg_roll[rot]]
                    mm = E("tensor", lambda t, m=m, k2=k2, bank=bank: t.matmul(
                        out=bank[:],
                        lhsT=wffn[:, k2 * 1024 + m * 128:k2 * 1024 + (m + 1) * 128],
                        rhs=xqfm[:, k2 * T:(k2 + 1) * T],
                        start=(k2 == 0), stop=(k2 == NK - 1)), waits=w)
                R["ffn_last"] = mm
                t_tpg_m[m] = E("vector", lambda v, m=m, rot=rot, bank=bank:
                               v.scalar_tensor_tensor(
                                   out=preg[:, rot * 512:(rot + 1) * 512],
                                   in0=bank[:], scalar=cscol[:, m:m + 1],
                                   in1=crepsb[:], op0=ALU.add, op1=ALU.mult),
                               waits=[mm, t_crepsb, t_cs[NK - 1],
                                      t_silu_roll[rot]])
                t_preg_roll[rot] = t_tpg_m[m]

            def ph4c(m):
                rot = m % 2
                tsl = E("scalar", lambda s, m=m, rot=rot: s.activation(
                    out=gbuf[:, rot * 512:(rot + 1) * 512],
                    in_=preg[:, rot * 512:(rot + 1) * 512], func=AF.Silu,
                    bias=biascol[:, m:m + 1]),
                    waits=[t_tpg_m[m], t_bias, t_gbuf_roll[rot]])
                t_silu_roll[rot] = tsl
                tsg = E("gpsimd", lambda g, m=m, rot=rot: g.tensor_tensor(
                    out=sgbuf[:, rot * 512:(rot + 1) * 512],
                    in0=sfm[:, (m % 3) * T:(m % 3 + 1) * T],
                    in1=gbuf[:, rot * 512:(rot + 1) * 512], op=ALU.mult),
                    waits=[tsl, t_sfm_tok[m], t_sgw_roll[rot]])
                t_gbuf_roll[rot] = tsg
                t_sg_slot[m % 3] = tsg
                last_blk = None
                for j in range(NJ):
                    tbank = ps[3 + (m * NJ + j) % 2]
                    ttr = E("tensor", lambda t, rot=rot, j=j, tbank=tbank:
                            t.transpose(
                                out=tbank[:, 0:128],
                                in_=sgbuf[:, rot * 512 + j * 128:
                                          rot * 512 + (j + 1) * 128],
                                identity=ident32[:]),
                            waits=[tsg, ps34_reader[(m * NJ + j) % 2]])
                    last_blk = E("vector", lambda v, rot=rot, j=j, tbank=tbank:
                                 v.tensor_tensor(
                                     out=outm[:, rot * 512 + j * 128:
                                              rot * 512 + (j + 1) * 128],
                                     in0=xresm[:, rot * 512 + j * 128:
                                               rot * 512 + (j + 1) * 128],
                                     in1=tbank[:, 0:128], op=ALU.add),
                                 waits=[ttr, t_xres_m[m],
                                        t_outdma_m[m - 2] if j == 0 else None])
                    ps34_reader[(m * NJ + j) % 2] = last_blk
                t_sgw_roll[rot] = last_blk
                t_outblk_m[m] = last_blk
                yap = Y[c * T:(c + 1) * T, m * 128:(m + 1) * 128].rearrange(
                    "(j p) w -> p j w", p=128)
                t_outdma_m[m] = E("sync", lambda s, rot=rot, yap=yap:
                                  s.dma_start(
                                      out=yap,
                                      in_=outm[:, rot * 512:(rot + 1) * 512]
                                      .rearrange("p (j w) -> p j w", j=NJ)),
                                  dma="qout", waits=[last_blk])

            if c == 0:
                ph2b(0)
            else:
                t_deltaSTT[0], t_u[0] = R["pre0"]
                t_deltaSTT[1], t_u[1] = R["pre1"]
                t_deltaSTT[2], t_u[2] = R["pre2"]
            ph2b_dxn(0)
            for k in range(NK):
                if k >= 3 and k % 2 == 1:
                    ph4c(k - 3)
                    ph4c(k - 2)
                a_toks = []
                emit_scans(k, a_toks)
                if k >= 1:
                    ph4a(k - 1)
                if k == 7 and c + 1 < NCH:
                    ph2b_dm(0)
                    ph2b_dm(1)
                emit_ytail(k)
                if c + 1 < NCH:
                    if k == 4:
                        R["tcc"] = ph1a(c + 1)
                    elif k == 6:
                        ph1b_xn(c + 1, [t_u[NK - 1], t_dxn[NK - 1]])
                        ph2a1_mm(c + 1)
                        R["pend_ev"] = c + 1
                    elif k == 7:
                        ph2a1_ilB(c + 1)
                        ph2b_act(0)
                        ph2b_tail(0)
                        ph2b_dm(2)
                        R["pre0"] = (t_deltaSTT[0], t_u[0])
            emit_sfm()
            ph4a(NK - 1)
            if c + 1 < NCH:
                ph2b_act(1)
                ph2b_tail(1)
                ph2b_act(2)
                ph2b_tail(2)
                R["pre1"] = (t_deltaSTT[1], t_u[1])
                R["pre2"] = (t_deltaSTT[2], t_u[2])
            ph4c(NK - 2)
            ph4c(NK - 1)
            if c + 1 < NCH:
                ph2a2(c + 1, R["tcc"])
                ph2a1_ilC(c + 1)
                ph1b_xq(c + 1, [R["ffn_last"]])
            t_state_gather = t_state_k[NK - 1]
            t_ymult_last_prev_chunk = t_ymult_g7[NK - 1]
            t_outblk_m[-2] = t_outblk_m[NK - 2]
            t_outblk_m[-1] = t_outblk_m[NK - 1]
            t_outdma_m[-2] = t_outdma_m[NK - 2]
            t_outdma_m[-1] = t_outdma_m[NK - 1]

        E("sync", lambda s: s.nop(),
          waits=[t_outdma_m[NK - 2], t_outdma_m[NK - 1]])

        B.resolve()
        for name, meth in (("sync", block.sync), ("vector", block.vector),
                           ("scalar", block.scalar), ("tensor", block.tensor),
                           ("gpsimd", block.gpsimd)):
            meth(lambda eng, name=name: B.run(eng, name, sems))

    return nc


_NC_CACHE = None


def kernel(L0_attn_x, L0_attn_W_deltaBC, L0_attn_W_dt, L0_attn_b_dt,
           L0_attn_A_log, L0_attn_D_param, L0_ffn_w_int8, L0_ffn_weight_scale,
           L0_ffn_bias):
    global _NC_CACHE
    if _NC_CACHE is None:
        _NC_CACHE = build_program()
    nc = _NC_CACHE

    x = np.ascontiguousarray(np.asarray(L0_attn_x, dtype=np.float32))
    base = {
        "w_dbc": np.ascontiguousarray(np.asarray(L0_attn_W_deltaBC, np.float32)),
        "w_dt": np.ascontiguousarray(np.asarray(L0_attn_W_dt, np.float32)),
        "b_dt": np.ascontiguousarray(np.asarray(L0_attn_b_dt, np.float32)),
        "a_log": np.ascontiguousarray(np.asarray(L0_attn_A_log, np.float32)),
        "d_par": np.ascontiguousarray(np.asarray(L0_attn_D_param, np.float32)),
        "w_int": np.ascontiguousarray(np.asarray(L0_ffn_w_int8, np.int32)),
        "w_sc": np.asarray(L0_ffn_weight_scale, np.float32).reshape(1),
        "f_bias": np.ascontiguousarray(np.asarray(L0_ffn_bias, np.float32)),
    }
    in_maps = []
    for b in range(NCORE):
        m = dict(base)
        m["x"] = np.ascontiguousarray(x[b])
        in_maps.append(m)
    res = run_bass_kernel_spmd(nc, in_maps, core_ids=list(range(NCORE)))
    out = np.stack([np.asarray(r["y"], np.float32) for r in res.results], axis=0)
    return out
